# version 14
# baseline (speedup 1.0000x reference)
import numpy as np
from contextlib import ExitStack

import concourse.bass as bass
import concourse.mybir as mybir
from concourse.bass_utils import run_bass_kernel_spmd

F32 = mybir.dt.float32
I32 = mybir.dt.int32
BF16 = mybir.dt.bfloat16
AF = mybir.ActivationFunctionType
ALU = mybir.AluOpType
AX = mybir.AxisListType.X

D = 1024
T = 2048
NG = 4
GT = 512
NH = 16
NE = 16
EPS = 1e-5
NEG = -1e30


def build_nc():
    nc = bass.Bass("TRN2", target_bir_lowering=False)

    xf = nc.dram_tensor("xf", [D, T], F32, kind="ExternalInput").ap()
    xpf = nc.dram_tensor("xpf", [D, T], F32, kind="ExternalInput").ap()
    wq_h = nc.dram_tensor("wq", [D, D], F32, kind="ExternalInput").ap()
    wk_h = nc.dram_tensor("wk", [D, D], F32, kind="ExternalInput").ap()
    wv_h = nc.dram_tensor("wv", [D, D], F32, kind="ExternalInput").ap()
    wo_h = nc.dram_tensor("wo", [D, D], F32, kind="ExternalInput").ap()
    rw_h = nc.dram_tensor("rw", [D, 16], F32, kind="ExternalInput").ap()
    rb_h = nc.dram_tensor("rb", [16, 1], F32, kind="ExternalInput").ap()
    wgu_h = nc.dram_tensor("wgu", [NE, 8, D, 256], F32, kind="ExternalInput").ap()
    wd_h = nc.dram_tensor("wd", [NE, 8, D, 128], F32, kind="ExternalInput").ap()
    gub_h = nc.dram_tensor("gub", [128, 256], F32, kind="ExternalInput").ap()
    dnb_h = nc.dram_tensor("dnb", [16, D], F32, kind="ExternalInput").ap()
    esel_h = nc.dram_tensor("esel", [16, NE * 128], F32, kind="ExternalInput").ap()
    iden_h = nc.dram_tensor("iden", [128, 128], F32, kind="ExternalInput").ap()
    dclamp_h = nc.dram_tensor("dclamp", [128, 128], F32, kind="ExternalInput").ap()
    mask01_h = nc.dram_tensor("mask01", [128, 128], F32, kind="ExternalInput").ap()
    tposb_h = nc.dram_tensor("tposb", [128, 128], F32, kind="ExternalInput").ap()
    jrev_h = nc.dram_tensor("jrev", [128, 1], F32, kind="ExternalInput").ap()
    m64_h = nc.dram_tensor("m64", [128, 2], F32, kind="ExternalInput").ap()
    flag_h = nc.dram_tensor("flag", [128, 1], F32, kind="ExternalInput").ap()
    lg_h = nc.dram_tensor("lg", [1, 16], F32, kind="ExternalInput").ap()

    y_h = nc.dram_tensor("y", [T, D], F32, kind="ExternalOutput").ap()
    xtok_h = nc.dram_tensor("xtok", [T, D], F32, kind="ExternalInput").ap()
    rscr = nc.dram_tensor("rscr", [128, 8 * T], F32, kind="Internal").ap()

    es = ExitStack()
    sb = lambda name, shape, dt: es.enter_context(nc.sbuf_tensor("sb_" + name, shape, dt))
    sem = lambda name: es.enter_context(nc.semaphore(name))

    xn_bf = sb("xn_bf", [128, 8 * T], BF16)
    scoresT = sb("scoresT", [16, T], F32)
    S_T = sb("S_T", [128, 512], F32)
    iden = sb("iden", [128, 128], F32)
    ones_r = sb("ones_r", [1, 128], F32)
    ones_c = sb("ones_c", [128, 1], F32)
    onesw = sb("onesw", [128, 64], F32)
    eps_c = sb("eps_c", [128, 1], F32)
    jnk1 = sb("jnk1", [128, 512], F32)
    jnkA = sb("jnkA", [128, 128], F32)
    flag = sb("flag", [128, 1], F32)
    rb_sb = sb("rb_sb", [16, 1], F32)
    rw_sb = sb("rw_sb", [128, 8 * 16], F32)

    esA = ExitStack()
    sbA = lambda name, shape, dt: esA.enter_context(nc.sbuf_tensor("sa_" + name, shape, dt))
    wq_sb = sbA("wq_sb", [128, 8 * D], BF16)
    wk_sb = sbA("wk_sb", [128, 8 * D], BF16)
    wv_sb = sbA("wv_sb", [128, 8 * D], BF16)
    wo_sb = sbA("wo_sb", [128, 8 * D], BF16)
    xg = sbA("xg", [128, 8 * GT], F32)
    scr = sbA("scr", [128, 8 * GT], F32)
    QY = sbA("QY", [128, 8 * GT], BF16)
    Vb = sbA("Vb", [128, NG * D], BF16)
    Kfm = sbA("Kfm", [128, 8 * GT], BF16)
    qg = sbA("qg", [128, 8 * 128], BF16)
    kg = sbA("kg", [128, D], BF16)
    Klnb = sbA("Klnb", [128, NG * D], BF16)
    iden_bf = sbA("iden_bf", [128, 128], BF16)
    adb = sbA("adb", [128, 2048], BF16)
    S_bf = sbA("S_bf", [128, 512], BF16)
    Dw = sbA("Dw", [128, NH * 128], BF16)
    Gq = sbA("Gq", [128, 8 * 128], BF16)
    Gk = sbA("Gk", [128, D], BF16)
    gC = sbA("gC", [128, 512], F32)
    rowb = sbA("rowb", [1, 3 * GT], F32)
    ltok = sbA("ltok", [128, 64], F32)
    lz = sbA("lz", [128, 64], F32)
    m8b = sbA("m8b", [128, 32], F32)
    nmx = sbA("nmx", [128, 4], F32)
    e4b = sbA("e4b", [128, 16], F32)
    p16b = sbA("p16b", [128, 64], F32)
    selb = sbA("selb", [128, 64], F32)
    scob = sbA("scob", [128, 64], F32)
    dnm = sbA("dnm", [128, 8], F32)
    kst = sbA("kst", [128, 256], F32)
    lraw = sbA("lraw", [16, GT], F32)
    esP = ExitStack()
    sbP = lambda name, shape, dt: esP.enter_context(nc.sbuf_tensor("sp_" + name, shape, dt))
    ptmp = sbP("ptmp", [128, 128], F32)
    grow = sbP("grow", [1, 16], F32)
    lnrow = sbP("lnrow", [1, 16], F32)
    lnbc = sbP("lnbc", [128, 16], F32)
    lnpair = sbP("lnpair", [128, 8], F32)
    gcol = sbP("gcol", [128, 1], F32)
    gCcol = sbP("gCcol", [128, 16], F32)
    gCrow = sbP("gCrow", [1, 16], F32)
    m64 = sbP("m64", [128, 2], F32)
    jrev = sbP("jrev", [128, 1], F32)
    dclamp = sbP("dclamp", [128, 128], BF16)
    mask01 = sbP("mask01", [128, 128], BF16)
    tposb = sbP("tposb", [128, 128], BF16)
    lgt = sbP("lgt", [1, 16], F32)

    ps_bc = esA.enter_context(nc.psum_tensor("ps_bc", [128, 1024], F32))
    ps_x0 = esA.enter_context(nc.psum_tensor("ps_x0", [128, 512], F32))
    ps_x1 = esA.enter_context(nc.psum_tensor("ps_x1", [128, 512], F32))
    ps_r0 = esA.enter_context(nc.psum_tensor("ps_r0", [128, 512], F32))
    ps_r1 = esA.enter_context(nc.psum_tensor("ps_r1", [128, 512], F32))
    ps_sd = esA.enter_context(nc.psum_tensor("ps_sd", [128, 512], F32))
    ps_y1 = esA.enter_context(nc.psum_tensor("ps_y1", [128, 512], F32))
    PSX = [ps_x0, ps_x1]
    PSR = [ps_r0, ps_r1]

    s_ld = sem("s_ld")
    s_ldw = sem("s_ldw")
    pg = sem("pg")
    p0s = sem("p0s")
    vself = sem("vself")
    vself_n = [0]

    def vg(v, ins):
        vself_n[0] += 1
        ins.then_inc(vself, 1)
        v.wait_ge(vself, vself_n[0])

    with nc.Block("p0load") as b:
        @b.sync
        def _(sync):
            n = 0
            for dst, src in [(iden, iden_h), (jrev, jrev_h), (m64, m64_h),
                             (flag, flag_h), (lgt, lg_h), (rb_sb, rb_h),
                             (rw_sb, rw_h)]:
                if dst is rw_sb:
                    sync.dma_start(rw_sb[:].rearrange("p (c o) -> p c o", c=8),
                                   rw_h.rearrange("(c p) o -> p c o", p=128)).then_inc(s_ld, 16)
                else:
                    sync.dma_start(dst[:], src).then_inc(s_ld, 16)
                n += 1
            sync.wait_ge(s_ld, n * 16)

        @b.gpsimd
        def _(g):
            for dst, src in [(dclamp, dclamp_h), (mask01, mask01_h), (tposb, tposb_h)]:
                g.dma_start(dst[:].rearrange("p (a b) -> p a b", a=2),
                            src.rearrange("p (a b) -> p a b", a=2)).then_inc(s_ldw, 16)
            for dst, src in [(wk_sb, wk_h), (wv_sb, wv_h)]:
                g.dma_start(dst[:].rearrange("p (c o) -> p c o", c=8),
                            src.rearrange("(c p) o -> p c o", p=128)).then_inc(s_ldw, 16)
            g.wait_ge(s_ldw, 5 * 16)

    with nc.Block("p0gamma") as b:
        @b.vector
        def _(v):
            for t_, val in [(ones_r, 1.0), (ones_c, 1.0), (onesw, 1.0),
                            (eps_c, EPS), (jnk1, 0.0)]:
                v.memset(t_[:], val)
            v.tensor_copy(iden_bf[:], iden[:])
            v.memset(S_T[:], 0.0).then_inc(pg, 1)

            def vsp(semh, k=1):
                v.tensor_copy(jnk1[:], jnk1[:]).then_inc(semh, k)
            v.wait_ge(pg, 3)
            v.tensor_copy(lnbc[:], ps_bc[:, 0:16])
            vsp(p0s)
            v.wait_ge(p0s, 2)
            lnbc3 = lnbc[:].rearrange("p (a b) -> p a b", b=2)
            vg(v, v.tensor_scalar(lnpair[:], lnbc3[:, :, 0], m64[:, 0:1], None, op0=ALU.mult))
            vg(v, v.tensor_scalar(ptmp[:, 0:8], lnbc3[:, :, 1], m64[:, 1:2], None, op0=ALU.mult))
            vg(v, v.tensor_add(lnpair[:], lnpair[:], ptmp[:, 0:8]))
            vsp(p0s)
            v.wait_ge(p0s, 3)
            v.tensor_scalar(gCrow[:], lnrow[:], 128.0, None, op0=ALU.mult)
            vsp(pg)
            v.wait_ge(pg, 6)
            v.tensor_copy(gCcol[:], ps_bc[:, 16:32])
            vsp(p0s)
            v.wait_ge(p0s, 4)
            for h in range(NH):
                pb = 64 * (h % 2)
                v.tensor_scalar(gC[pb:pb + 64, (h // 2) * 64:(h // 2 + 1) * 64],
                                onesw[pb:pb + 64, :], gCcol[pb:pb + 64, h:h + 1], None,
                                op0=ALU.mult)
            v.nop().then_inc(pg, 1)
            TMP = xn_bf[:].bitcast(F32)
            for h in range(NH):
                v.tensor_scalar(TMP[:, h * 128:(h + 1) * 128], dclamp[:], lnbc[:, h:h + 1],
                                None, op0=ALU.mult)
            for p in range(8):
                v.tensor_scalar(TMP[:, 2048 + p * 128:2048 + (p + 1) * 128], tposb[:],
                                lnpair[:, p:p + 1], None, op0=ALU.mult)
            v.tensor_scalar(TMP[:, 3072:3088], lnbc[:], jrev[:], None, op0=ALU.mult)
            vsp(pg)
            v.wait_ge(pg, 9)
            for h in range(NH):
                v.tensor_mul(Dw[:, h * 128:(h + 1) * 128], TMP[:, h * 128:(h + 1) * 128],
                             mask01[:])
            v.tensor_copy(Gq[:], TMP[:, 2048:3072])
            for h in range(NH):
                v.tensor_scalar(Gk[:, h * 64:(h + 1) * 64], onesw[:], TMP[:, 3072 + h:3073 + h],
                                None, op0=ALU.mult)
            vg(v, v.tensor_mul(ptmp[:, 0:16], gCcol[:], gCcol[:]))
            vg(v, v.tensor_mul(ptmp[:, 16:32], ptmp[:, 0:16], gCcol[:]))
            Gk3 = Gk[:].rearrange("p (h d) -> p h d", d=64)
            for cc, fac in ((0, ptmp[:, 16:32]), (1, ptmp[:, 0:16]), (2, gCcol[:])):
                v.tensor_tensor(Kfm[:, cc * 1024:(cc + 1) * 1024].rearrange("p (h d) -> p h d", d=64),
                                Gk3, fac.unsqueeze(2).to_broadcast([128, NH, 64]), op=ALU.mult)

        @b.scalar
        def _(s):
            s.wait_ge(s_ld, 7 * 16)
            def asp(semh, k=1):
                s.copy(jnkA[:], iden[:]).then_inc(semh, k)

            s.activation(grow[:], lgt[:], AF.Sigmoid)
            asp(p0s)
            s.wait_ge(p0s, 1)
            s.activation(lnrow[:], grow[:], AF.Ln)
            asp(pg)
            s.wait_ge(pg, 4)
            s.activation(gCrow[:], gCrow[:], AF.Exp)
            asp(pg)
            TMPs = xn_bf[:].bitcast(F32)
            s.wait_ge(pg, 8)
            s.activation(TMPs[:, 0:2048], TMPs[:, 0:2048], AF.Exp)
            s.activation(TMPs[:, 2048:3088], TMPs[:, 2048:3088], AF.Exp)
            asp(pg)

        @b.tensor
        def _(t):
            t.wait_ge(pg, 2)
            t.matmul(ps_bc[:, 0:16], ones_r[:], lnrow[:], start=True, stop=True).then_inc(pg, 1)
            t.wait_ge(pg, 5)
            t.matmul(ps_bc[:, 16:32], ones_r[:], gCrow[:], start=True, stop=True).then_inc(pg, 1)

    esP.close()

    ATTN_SEM_NAMES = ("sx sst a_sq p_st d_rw a_rs p_bc d_xn p_rl d_lg p_lt r1 r2 r3 r4 r5 "
                      "a_e4 d_dn a_p16 d_sc p_sT d_sE p_q d_q p_k d_ke a_kq d_ms d_st2 a_kr "
                      "d_kl p_kt d_kf p_v d_ve d_qk d_sb p_A d_ad p_y d_ye p_sd d_su p_wo d_r a_l1 a_l2 d_rdy")
    ATTN_SEMS = {n: sem("at_" + n) for n in ATTN_SEM_NAMES.split()}
    PREFIX_TOTALS = dict(sx=16 * NG, a_sq=NG, p_st=NG, d_rw=NG, a_rs=NG, p_bc=NG, d_xn=NG,
                         p_k=8 * NG, d_ke=8 * NG, a_kq=8 * NG, d_ms=8 * NG, d_st2=NG,
                         a_kr=NG, d_kl=4 * NG, p_v=8 * NG, d_ve=8 * NG, d_qk=4 * NG,
                         p_sd=NG, d_su=NG, a_l1=NG, a_l2=NG, d_rdy=NG)

    def attn_phase(tag, xsrc, prefix, lvl=5):
        S = ATTN_SEMS
        BASE = {} if prefix else PREFIX_TOTALS
        L2 = (not prefix) and lvl >= 2
        L3 = (not prefix) and lvl >= 3
        L4 = (not prefix) and lvl >= 4
        L5 = (not prefix) and lvl >= 5
        xnT = QY if prefix else xn_bf
        PKB = [kg[:], qg[:], adb[:, 0:1024], adb[:, 1024:2048]]
        NMU, VAR, RSTD = 0, GT, 2 * GT

        def xn_col(fc, g):
            return fc * (GT if prefix else T) + (0 if prefix else g * GT)

        with nc.Block(tag) as b:
            @b.sync
            def _(sync):
                def w(s, v):
                    v += BASE.get(s, 0)
                    if v > 0:
                        sync.wait_ge(S[s], v)

                def xload(g):
                    if g > 0:
                        w("d_xn", g)
                        if L5:
                            w("p_st", g)
                    elif not prefix:
                        w("d_xn", 0)
                    sync.dma_start(
                        xg[:].rearrange("p (c t) -> p c t", c=8),
                        xsrc[:, g * GT:(g + 1) * GT].rearrange("(c p) t -> p c t", p=128),
                    ).then_inc(S["sx"], 16)

                def rstore(g):
                    rstage = Kfm[:].bitcast(F32)
                    for hh in range(2):
                        w("d_r", 8 * g + 4 * (hh + 1))
                        sync.dma_start(rscr[:, g * 4096 + hh * 2048:g * 4096 + (hh + 1) * 2048],
                                       rstage[:, 0:2048]).then_inc(S["sst"], 16)

                for g in range(NG):
                    xload(g)
                    if L5 and g >= 1:
                        rstore(g - 1)
                if L5:
                    rstore(NG - 1)
                    w("sst", 32 * NG)

            @b.tensor
            def _(t):
                def w(s, v):
                    v += BASE.get(s, 0)
                    if v > 0:
                        t.wait_ge(S[s], v)

                def st_xsum(g):
                    w("sx", 16 * (g + 1)); w("d_ms", 8 * g)
                    if g == 0:
                        w("d_xn", 0)
                        if L2:
                            w("d_lg", 0)
                    for fc in range(8):
                        t.matmul(ps_bc[0:1, 0:512], ones_c[:], xg[:, fc * GT:(fc + 1) * GT],
                                 start=(fc == 0), stop=(fc == 7))

                def st_stats(g):
                    w("a_sq", g + 1); w("d_ms", 8 * g); w("d_xn", g)
                    for fc in range(8):
                        ins = t.matmul(ps_bc[0:1, 512:1024], ones_c[:], scr[:, fc * GT:(fc + 1) * GT],
                                       start=(fc == 0), stop=(fc == 7))
                    ins.then_inc(S["p_st"], 1)

                def st_bcast(g):
                    w("a_rs", g + 1)
                    t.matmul(ps_bc[:, 0:512], ones_r[:], rowb[0:1, 0:GT], start=True, stop=True)
                    t.matmul(ps_bc[:, 512:1024], ones_r[:], rowb[0:1, 2 * GT:3 * GT],
                             start=True, stop=True).then_inc(S["p_bc"], 1)

                def st_router(g):
                    w("d_xn", g + 1); w("d_lg", g)
                    if L4:
                        w("d_ad", 8 * g - 3)
                    if L3:
                        w("d_ke", 8 * g - 1)
                    for fc in range(8):
                        ins = t.matmul(ps_x0[0:16, :], rw_sb[:, fc * 16:(fc + 1) * 16],
                                       scr[:, fc * GT:(fc + 1) * GT],
                                       start=(fc == 0), stop=(fc == 7))
                    ins.then_inc(S["p_rl"], 1)

                def st_ltrans(g):
                    w("d_lg", g + 1); w("r1", g)
                    if L4 or prefix:
                        w("d_su", 4 * g)
                    for j in range(4):
                        ins = t.transpose(ps_sd[:, j * 16:(j + 1) * 16],
                                          lraw[:, j * 128:(j + 1) * 128], iden[0:16, 0:16])
                    ins.then_inc(S["p_lt"], 1)

                def st_q(g):
                    for ob in range(8):
                        if ob == 0:
                            if g == 0:
                                t.wait_ge(s_ldw, 6 * 16)
                            w("d_lg", g + 1)
                            if L5:
                                w("d_r", 8 * g - 1)
                            if L4:
                                w("d_ad", 8 * g - 1)
                            w("d_ke", 8 * g - 1)
                        elif ob == 1:
                            if L5:
                                w("d_r", 8 * g)
                            if L4:
                                w("d_ad", 8 * g)
                            w("d_ke", 8 * g)
                        else:
                            w("d_q", 8 * g + ob - 1)
                        for fc in range(8):
                            ins = t.matmul(PSX[ob % 2][:], wq_sb[:, fc * D + ob * 128:fc * D + (ob + 1) * 128],
                                           xn_bf[:, fc * T + g * GT:fc * T + (g + 1) * GT],
                                           start=(fc == 0), stop=(fc == 7))
                        ins.then_inc(S["p_q"], 1)

                def st_k(g):
                    for idx in range(8):
                        j, nb = idx // 2, idx % 2
                        if prefix:
                            w("d_ke", 8 * g + idx - 1)
                            if idx < 2:
                                w("d_xn", g + 1)
                        else:
                            if idx == 0:
                                w("d_q", 8 * g + 7)
                            elif idx == 1:
                                w("d_q", 8 * g + 8)
                            else:
                                w("d_ke", 8 * g + idx - 1)
                        for fc in range(8):
                            ins = t.matmul(PSX[idx % 2][:],
                                           xnT[:, xn_col(fc, g) + j * 128:xn_col(fc, g) + (j + 1) * 128],
                                           wk_sb[:, fc * D + nb * 512:fc * D + (nb + 1) * 512],
                                           start=(fc == 0), stop=(fc == 7))
                        ins.then_inc(S["p_k"], 1)

                def st_v(g):
                    for idx in range(8):
                        j, nb = idx // 2, idx % 2
                        if idx < 2 and not prefix:
                            if L4:
                                w("d_ad", 8 * g - 1 + idx)
                            elif L3:
                                w("d_kf", 8 * g - 1 + idx)
                        w("d_ve", 8 * g + idx - 1)
                        for fc in range(8):
                            ins = t.matmul(PSR[idx % 2][:],
                                           xnT[:, xn_col(fc, g) + j * 128:xn_col(fc, g) + (j + 1) * 128],
                                           wv_sb[:, fc * D + nb * 512:fc * D + (nb + 1) * 512],
                                           start=(fc == 0), stop=(fc == 7))
                        ins.then_inc(S["p_v"], 1)

                def st_sct(g):
                    w("d_sc", g + 1); w("d_ke", 8 * g + 8)
                    for j in range(4):
                        ins = t.transpose(ps_x1[0:16, j * 128:(j + 1) * 128],
                                          scob[:, j * 16:(j + 1) * 16], iden[:])
                    ins.then_inc(S["p_sT"], 1)

                def st_kt(g):
                    for i in range(8):
                        sub, half = i // 2, i % 2
                        w("d_kl", 4 * g + sub + 1)
                        if i == 0:
                            w("d_ve", 8 * g + 7)
                        elif i == 1:
                            w("d_ve", 8 * g + 8)
                        else:
                            w("d_kf", 8 * g + i - 1)
                        dstb = PSR[i % 2][:].bitcast(BF16)
                        for tt in range(4):
                            p = half * 4 + tt
                            ins = t.transpose(dstb[:, tt * 128:(tt + 1) * 128],
                                              Klnb[:, sub * D + p * 128:sub * D + (p + 1) * 128],
                                              iden_bf[:])
                        ins.then_inc(S["p_kt"], 1)

                def st_ret(g, cc):
                    w("d_sb", 4 * g + cc + 1)
                    w("d_qk", 4 * g + cc + 1)
                    w("d_kf", 8 * g + 8)
                    w("d_q", 8 * g + 8)
                    w("d_rdy", g + 1)
                    if cc == 0:
                        w("d_ke", 8 * g + 8)
                        w("d_sE", g + 1)
                    else:
                        w("d_ad", 8 * g + 2 * cc)
                    for p in range(8):
                        for hh in range(2):
                            dst = (PSX if hh == 0 else PSR)[p // 4]
                            ins = t.matmul(dst[:, (p % 4) * 128:(p % 4) * 128 + 128],
                                           Kfm[64 * hh:64 * hh + 64,
                                               p * GT + cc * 128:p * GT + (cc + 1) * 128],
                                           QY[64 * hh:64 * hh + 64,
                                              p * GT + cc * 128:p * GT + (cc + 1) * 128],
                                           start=True, stop=True)
                    ins.then_inc(S["p_A"], 1)
                    for r in range(2):
                        w("d_ad", 8 * g + 2 * cc + r + 1)
                        w("d_ye", 8 * g + 2 * cc + r)
                        for p in range(4 * r, 4 * r + 4):
                            for hh in range(2):
                                h = 2 * p + hh
                                t.matmul(ps_y1[64 * hh:64 * hh + 64, (p % 4) * 128:(p % 4) * 128 + 128],
                                         Vb[:, cc * D + h * 64:cc * D + (h + 1) * 64],
                                         adb[:, h * 128:(h + 1) * 128],
                                         start=True, stop=False)
                                ins = t.matmul(ps_y1[64 * hh:64 * hh + 64, (p % 4) * 128:(p % 4) * 128 + 128],
                                               S_bf[64 * hh:64 * hh + 64, p * 64:(p + 1) * 64],
                                               qg[64 * hh:64 * hh + 64, p * 128:(p + 1) * 128],
                                               start=False, stop=True)
                        ins.then_inc(S["p_y"], 1)
                    w("d_su", 4 * g + cc)
                    for h in range(NH):
                        ins = t.matmul(ps_sd[64 * (h % 2):64 * (h % 2) + 64,
                                             (h // 2) * 64:(h // 2 + 1) * 64],
                                       kg[:, h * 64:(h + 1) * 64],
                                       Vb[:, cc * D + h * 64:cc * D + (h + 1) * 64],
                                       start=True, stop=True)
                    ins.then_inc(S["p_sd"], 1)

                def st_sdall(g):
                    w("d_qk", 4 * g + 4)
                    w("d_ve", 8 * g + 8)
                    w("d_rdy", g + 1)
                    w("d_su", g)
                    for h in range(NH):
                        for cc in range(4):
                            kb = PKB[cc]
                            ins = t.matmul(ps_sd[64 * (h % 2):64 * (h % 2) + 64,
                                                 (h // 2) * 64:(h // 2 + 1) * 64],
                                           kb[:, h * 64:(h + 1) * 64],
                                           Vb[:, cc * D + h * 64:cc * D + (h + 1) * 64],
                                           start=(cc == 0), stop=(cc == 3))
                    ins.then_inc(S["p_sd"], 1)

                def st_warm(g):
                    for q in range(8):
                        w("a_kq", 8 * g + q + 1)
                        t.matmul(ps_y1[0:16, 0:16], ones_r[0:1, 0:16], ones_r[0:1, 0:16],
                                 start=True, stop=True)
                    for q in range(8):
                        w("d_ms", 8 * g + q + 1)
                        t.matmul(ps_y1[0:16, 0:16], ones_r[0:1, 0:16], ones_r[0:1, 0:16],
                                 start=True, stop=True)
                    for j in range(4):
                        w("d_kl", 4 * g + j + 1)
                        t.matmul(ps_y1[0:16, 0:16], ones_r[0:1, 0:16], ones_r[0:1, 0:16],
                                 start=True, stop=True)

                def st_wo(g):
                    w("d_ye", 8 * (g + 1))
                    for oc in range(8):
                        if oc == 0:
                            if g == 0:
                                t.wait_ge(s_ldw, 7 * 16)
                            w("d_ad", 8 * g + 7)
                        elif oc == 1:
                            w("d_ad", 8 * g + 8)
                        else:
                            w("d_r", 8 * g + oc - 1)
                        for yc in range(8):
                            ins = t.matmul(PSX[oc % 2][:],
                                           wo_sb[:, yc * D + oc * 128:yc * D + (oc + 1) * 128],
                                           QY[:, yc * GT:(yc + 1) * GT],
                                           start=(yc == 0), stop=(yc == 7))
                        ins.then_inc(S["p_wo"], 1)

                for g in range(NG):
                    bk = g >= 1
                    if prefix:
                        if g == 0:
                            st_xsum(0)
                        st_stats(g)
                        st_bcast(g)
                        if bk:
                            st_sdall(g - 1)
                        st_k(g)
                        st_v(g)
                        st_warm(g)
                        if g + 1 < NG:
                            st_xsum(g + 1)
                    else:
                        if g == 0:
                            st_xsum(0)
                        if bk and L4:
                            st_ret(g - 1, 0)
                        st_stats(g)
                        if bk and L4:
                            st_ret(g - 1, 1)
                        st_bcast(g)
                        if bk and L4:
                            st_ret(g - 1, 2)
                        if L2:
                            st_router(g)
                        if bk and L4:
                            st_ret(g - 1, 3)
                        if L2:
                            st_ltrans(g)
                        if bk and L5:
                            st_wo(g - 1)
                        if L3:
                            st_q(g)
                            st_k(g)
                            st_v(g)
                            st_warm(g)
                            if g + 1 < NG:
                                st_xsum(g + 1)
                        if L2:
                            st_sct(g)
                        if L3:
                            st_kt(g)
                if prefix:
                    st_sdall(NG - 1)
                else:
                    if L4:
                        for cc in range(4):
                            st_ret(NG - 1, cc)
                    if L5:
                        st_wo(NG - 1)

            @b.vector
            def _(v):
                def w(s, val):
                    val += BASE.get(s, 0)
                    if val > 0:
                        v.wait_ge(S[s], val)

                def sp(sname, k=1):
                    v.tensor_copy(jnk1[:], jnk1[:]).then_inc(S[sname], k)

                def newton_rsqrt(dst, varap, scr_f32):
                    vg(v, v.tensor_scalar(varap, varap, EPS, None, op0=ALU.add))
                    iv = varap.bitcast(I32)
                    si = scr_f32.bitcast(I32)
                    vg(v, v.tensor_scalar(si, iv, 1, None, op0=ALU.arith_shift_right))
                    vg(v, v.tensor_scalar(si, si, -1, None, op0=ALU.mult))
                    vg(v, v.tensor_scalar(si, si, 0x5F3759DF, None, op0=ALU.add))
                    for _ in range(2):
                        vg(v, v.tensor_mul(dst, scr_f32, scr_f32))
                        vg(v, v.tensor_mul(dst, dst, varap))
                        vg(v, v.tensor_scalar(dst, dst, -0.5, 1.5, op0=ALU.mult, op1=ALU.add))
                        vg(v, v.tensor_mul(scr_f32, scr_f32, dst))
                    ins = v.tensor_copy(dst, scr_f32)
                    vg(v, ins)
                    return ins
                def dv_rw(g):
                    mus = jnk1[0:1, 0:GT]
                    w("p_st", g + 1); w("p_bc", g)
                    vg(v, v.tensor_scalar(mus, ps_bc[0:1, 0:512], 1.0 / D, None, op0=ALU.mult))
                    vg(v, v.tensor_scalar(rowb[0:1, NMU:NMU + GT], mus, -1.0, None, op0=ALU.mult))
                    vg(v, v.tensor_mul(mus, mus, mus))
                    vg(v, v.tensor_scalar(rowb[0:1, VAR:VAR + GT], ps_bc[0:1, 512:1024], 1.0 / D, None, op0=ALU.mult))
                    vg(v, v.tensor_sub(rowb[0:1, VAR:VAR + GT], rowb[0:1, VAR:VAR + GT], mus))
                    if prefix:
                        sp("d_rw")
                    else:
                        newton_rsqrt(rowb[0:1, RSTD:RSTD + GT], rowb[0:1, VAR:VAR + GT], mus)
                        sp("a_rs")

                def dv_xn(g):
                    w("p_bc", g + 1)
                    if L2:
                        w("p_rl", g)
                    if prefix:
                        w("p_v", 8 * g)
                        w("p_k", 8 * g)
                    for fc in range(8):
                        v.tensor_add(scr[:, fc * GT:(fc + 1) * GT], xg[:, fc * GT:(fc + 1) * GT],
                                     ps_bc[:, 0:512])
                        v.tensor_mul(xnT[:, xn_col(fc, g):xn_col(fc, g) + GT],
                                     scr[:, fc * GT:(fc + 1) * GT], ps_bc[:, 512:1024])
                    sp("d_xn")

                def dv_lga(g):
                    w("p_rl", g + 1); w("p_lt", g)
                    vg(v, v.tensor_copy(lraw[:], ps_x0[0:16, :]))
                    vg(v, v.tensor_mul(lraw[:], lraw[:], ps_bc[0:16, 512:1024]))
                    vg(v, v.tensor_scalar(lraw[:], lraw[:], rb_sb[:], None, op0=ALU.add))
                    sp("d_lg")

                def dv_lgb(g):
                    w("p_lt", g + 1); w("a_e4", g); w("a_p16", g)
                    for j in range(4):
                        ins = v.tensor_copy(ltok[:, j * 16:(j + 1) * 16], ps_sd[:, j * 16:(j + 1) * 16])
                    ins.then_inc(S["r1"], 1)
                    v.wait_ge(S["r1"], g + 1)
                    for j in range(4):
                        ins = v.max(out=m8b[:, j * 8:(j + 1) * 8], in_=ltok[:, j * 16:(j + 1) * 16])
                    ins.then_inc(S["r2"], 1)
                    v.wait_ge(S["r2"], g + 1)
                    for j in range(4):
                        ins = v.memset(m8b[:, j * 8 + 4:(j + 1) * 8], NEG)
                    ins.then_inc(S["r3"], 1)
                    v.wait_ge(S["r3"], g + 1)
                    for j in range(4):
                        ins = v.match_replace(out=lz[:, j * 16:(j + 1) * 16],
                                              in_to_replace=m8b[:, j * 8:(j + 1) * 8],
                                              in_values=ltok[:, j * 16:(j + 1) * 16],
                                              imm_value=NEG)
                    ins.then_inc(S["r4"], 1)
                    v.wait_ge(S["r4"], g + 1)
                    for j in range(4):
                        ins = v.tensor_scalar(nmx[:, j:j + 1], m8b[:, j * 8:j * 8 + 1], -1.0, None,
                                              op0=ALU.mult)
                    ins.then_inc(S["r5"], 1)
                    for j in range(4):
                        ins = v.tensor_tensor(selb[:, j * 16:(j + 1) * 16], lz[:, j * 16:(j + 1) * 16],
                                              ltok[:, j * 16:(j + 1) * 16], op=ALU.not_equal)
                    vg(v, ins)

                def dv_lgc(g):
                    w("a_e4", g + 1)
                    for j in range(4):
                        ins = v.reduce_sum(dnm[:, j:j + 1], e4b[:, j * 4:(j + 1) * 4], axis=AX)
                    ins.then_inc(S["d_dn"], 1)
                    v.wait_ge(S["d_dn"], 2 * g + 1)
                    for j in range(4):
                        ins = v.reciprocal(dnm[:, 4 + j:5 + j], dnm[:, j:j + 1])
                    ins.then_inc(S["d_dn"], 1)
                    w("a_p16", g + 1); w("p_sT", g)
                    v.wait_ge(S["d_dn"], 2 * g + 2)
                    for j in range(4):
                        v.scalar_tensor_tensor(scob[:, j * 16:(j + 1) * 16],
                                               p16b[:, j * 16:(j + 1) * 16],
                                               dnm[:, 4 + j:5 + j],
                                               selb[:, j * 16:(j + 1) * 16],
                                               op0=ALU.mult, op1=ALU.mult)
                    sp("d_sc")

                def dv_kln(g):
                    for j in range(4):
                        w("d_ke", 8 * g + 2 * j + 2)
                        v.reduce_sum(kst[:, j * 16:(j + 1) * 16],
                                     scr[:, j * D:(j + 1) * D].rearrange("p (h d) -> p h d", d=64),
                                     axis=AX)
                        for hf in range(2):
                            q = 2 * j + hf
                            w("a_kq", 8 * g + q + 1)
                            c0 = 64 + j * 16 + hf * 8
                            ins = v.reduce_sum(kst[:, c0:c0 + 8],
                                               ps_bc[:, (q % 2) * 512:(q % 2) * 512 + 512]
                                               .rearrange("p (h d) -> p h d", d=64), axis=AX)
                            ins.then_inc(S["d_ms"], 1)
                    v.wait_ge(S["d_ms"], 8 * g + 8 + BASE.get("d_ms", 0))
                    vg(v, v.tensor_scalar(kst[:, 128:192], kst[:, 0:64], 1.0 / 64, None, op0=ALU.mult))
                    vg(v, v.tensor_mul(kst[:, 0:64], kst[:, 128:192], kst[:, 128:192]))
                    vg(v, v.tensor_scalar(kst[:, 64:128], kst[:, 64:128], 1.0 / 64, None, op0=ALU.mult))
                    vg(v, v.tensor_sub(kst[:, 64:128], kst[:, 64:128], kst[:, 0:64]))
                    sp("a_kr")
                    v.wait_ge(S["d_st2"], g + 1 + BASE.get("d_st2", 0))
                    if not prefix:
                        w("p_kt", 8 * g)
                    for j in range(4):
                        K3 = scr[:, j * D:(j + 1) * D].rearrange("p (h d) -> p h d", d=64)
                        KB3 = Klnb[:, j * D:(j + 1) * D].rearrange("p (h d) -> p h d", d=64)
                        mu3 = kst[:, 128 + j * 16:128 + (j + 1) * 16].unsqueeze(2).to_broadcast([128, 16, 64])
                        rs3 = kst[:, 192 + j * 16:192 + (j + 1) * 16].unsqueeze(2).to_broadcast([128, 16, 64])
                        v.tensor_tensor(K3, K3, mu3, op=ALU.subtract)
                        v.tensor_tensor(KB3, K3, rs3, op=ALU.mult).then_inc(S["d_kl"], 1)

                def dv_ret(g, cc):
                    w("p_sd", 4 * g + cc)
                    w("p_y", 8 * g + 2 * cc)
                    w("d_q", 8 * g + 8)
                    w("d_rdy", g + 1)
                    v.tensor_mul(kg[:], Klnb[:, cc * D:(cc + 1) * D], Gk[:])
                    v.tensor_mul(qg[:].rearrange("p (pr t) -> p pr t", t=128),
                                 QY[:].rearrange("p (pr t) -> p pr t", t=GT)[:, :, cc * 128:(cc + 1) * 128],
                                 Gq[:].rearrange("p (pr t) -> p pr t", t=128))
                    sp("d_qk")
                    v.tensor_copy(S_bf[:], S_T[:])
                    sp("d_sb")
                    w("p_A", 4 * g + cc + 1)
                    adb3 = adb[:].rearrange("p (pr two t) -> p pr two t", two=2, t=128)
                    Dw3 = Dw[:].rearrange("p (pr two t) -> p pr two t", two=2, t=128)
                    for half in range(2):
                        v.tensor_tensor(adb3[:, 4 * half:4 * half + 4, 0, :],
                                        PSX[half][:].rearrange("p (s t) -> p s t", t=128),
                                        Dw3[:, 4 * half:4 * half + 4, 0, :], op=ALU.mult)
                        v.tensor_tensor(adb3[:, 4 * half:4 * half + 4, 1, :],
                                        PSR[half][:].rearrange("p (s t) -> p s t", t=128),
                                        Dw3[:, 4 * half:4 * half + 4, 1, :], op=ALU.mult)
                        sp("d_ad")
                    w("p_sd", 4 * g + cc + 1)
                    v.tensor_mul(S_T[:], S_T[:], gC[:])
                    v.tensor_add(S_T[:], S_T[:], ps_sd[:]).then_inc(S["d_su"], 1)

                def dv_sd(g, cc):
                    kb = PKB[cc]
                    w("p_sd", g)
                    w("d_rdy", g + 1)
                    gkp = Gk[:] if cc == 3 else Kfm[:, cc * 1024:(cc + 1) * 1024]
                    v.tensor_mul(kb[:], Klnb[:, cc * D:(cc + 1) * D], gkp)
                    sp("d_qk")
                    if cc == 3:
                        w("p_sd", g + 1)
                        for _ in range(4):
                            v.tensor_mul(S_T[:], S_T[:], gC[:])
                        v.tensor_add(S_T[:], S_T[:], ps_sd[:]).then_inc(S["d_su"], 1)

                for g in range(NG):
                    bk = g >= 1
                    if prefix:
                        if bk:
                            dv_sd(g - 1, 0)
                        dv_rw(g)
                        if bk:
                            dv_sd(g - 1, 1)
                        dv_xn(g)
                        if bk:
                            dv_sd(g - 1, 2)
                            dv_sd(g - 1, 3)
                        dv_kln(g)
                    else:
                        if bk and L4:
                            dv_ret(g - 1, 0)
                        dv_rw(g)
                        if bk and L4:
                            dv_ret(g - 1, 1)
                        dv_xn(g)
                        if bk and L4:
                            dv_ret(g - 1, 2)
                        if L2:
                            dv_lga(g)
                        if bk and L4:
                            dv_ret(g - 1, 3)
                        if L3:
                            dv_kln(g)
                        if L2:
                            dv_lgb(g)
                            dv_lgc(g)
                if prefix:
                    for cc in range(4):
                        dv_sd(NG - 1, cc)
                    v.tensor_scalar(S_T[:], S_T[:], flag[:], None, op0=ALU.mult)
                elif L4:
                    for cc in range(4):
                        dv_ret(NG - 1, cc)

            if prefix:
                @b.gpsimd
                def _(gp):
                    for dst, srch in [(wq_sb, wq_h), (wo_sb, wo_h)]:
                        gp.dma_start(dst[:].rearrange("p (c o) -> p c o", c=8),
                                     srch.rearrange("(c p) o -> p c o", p=128)).then_inc(s_ldw, 16)
                    gp.wait_ge(s_ldw, 7 * 16)

            @b.scalar
            def _(s):
                def w(sn, val):
                    val += BASE.get(sn, 0)
                    if val > 0:
                        s.wait_ge(S[sn], val)

                def asp(sname, k=1):
                    s.copy(jnkA[:], iden[:]).then_inc(S[sname], k)

                def ac_rw(g):
                    w("d_rw", g + 1)
                    s.activation(rowb[0:1, VAR:VAR + GT], rowb[0:1, VAR:VAR + GT], AF.Ln,
                                 bias=eps_c[0:1, :])
                    s.copy(jnkA[:], iden[:])
                    s.activation(rowb[0:1, RSTD:RSTD + GT], rowb[0:1, VAR:VAR + GT], AF.Exp,
                                 scale=-0.5)
                    asp("a_rs")

                def ac_krstd(g):
                    w("a_kr", g + 1)
                    s.activation(kst[:, 64:128], kst[:, 64:128], AF.Ln, bias=eps_c[:])
                    s.copy(jnkA[:], iden[:])
                    s.activation(kst[:, 192:256], kst[:, 64:128], AF.Exp, scale=-0.5)
                    asp("d_st2")

                def ac_sq(g):
                    w("sx", 16 * (g + 1))
                    if not (prefix or L3):
                        if L2:
                            w("p_rl", g)
                        w("p_st", g)
                        s.activation(scr[:], xg[:], AF.Square).then_inc(S["a_sq"], 1)
                        return
                    w("p_st", g)
                    for j in range(4):
                        w("d_kl", 4 * (g - 1) + j + 1)
                        ins = s.activation(scr[:, j * D:(j + 1) * D], xg[:, j * D:(j + 1) * D],
                                           AF.Square)
                    ins.then_inc(S["a_sq"], 1)

                def ac_yev(g, cc, r):
                    w("p_y", 8 * g + 2 * cc + r + 1)
                    ins = s.copy(
                        QY[:].rearrange("p (pr t) -> p pr t", t=GT)[:, 4 * r:4 * r + 4, cc * 128:(cc + 1) * 128],
                        ps_y1[:].rearrange("p (s t) -> p s t", t=128))
                    if cc == 3 and r == 1:
                        asp("d_ye")
                    else:
                        ins.then_inc(S["d_ye"], 1)

                def ac_rev(g):
                    rstage = Kfm[:].bitcast(F32)
                    for oc in range(8):
                        w("p_wo", 8 * g + oc + 1)
                        if oc == 4:
                            w("sst", 32 * g + 16)
                        ins = s.copy(rstage[:, (oc % 4) * 512:(oc % 4 + 1) * 512], PSX[oc % 2][:])
                        if (oc + 1) % 4 != 0:
                            ins.then_inc(S["d_r"], 1)
                        else:
                            asp("d_r")

                def ac_e4p16(g):
                    w("r5", g + 1); w("d_dn", 2 * g)
                    for j in range(4):
                        ins = s.activation(e4b[:, j * 4:(j + 1) * 4], m8b[:, j * 8:j * 8 + 4],
                                           AF.Exp, bias=nmx[:, j:j + 1])
                    ins.then_inc(S["a_e4"], 1)
                    w("d_sc", g)
                    for j in range(4):
                        ins = s.activation(p16b[:, j * 16:(j + 1) * 16], ltok[:, j * 16:(j + 1) * 16],
                                           AF.Exp, bias=nmx[:, j:j + 1])
                    ins.then_inc(S["a_p16"], 1)

                def ac_qev(g):
                    w("p_wo", 8 * g)
                    for ob in range(8):
                        w("p_q", 8 * g + ob + 1)
                        ins = s.copy(QY[:, ob * GT:(ob + 1) * GT], PSX[ob % 2][:])
                        if ob < 7:
                            ins.then_inc(S["d_q"], 1)
                        else:
                            asp("d_q")

                def ac_kev(g):
                    for idx in range(8):
                        j, nb = idx // 2, idx % 2
                        w("p_k", 8 * g + idx + 1)
                        ins = s.copy(scr[:, j * D + nb * 512:j * D + (nb + 1) * 512],
                                     PSX[idx % 2][:])
                        if idx < 7:
                            ins.then_inc(S["d_ke"], 1)
                        else:
                            asp("d_ke")

                def ac_ksq(g):
                    for q in range(8):
                        j, hf = q // 2, q % 2
                        w("d_ms", 8 * g + q - 1)
                        s.activation(ps_bc[:, (q % 2) * 512:(q % 2) * 512 + 512],
                                     scr[:, j * D + hf * 512:j * D + hf * 512 + 512],
                                     AF.Square).then_inc(S["a_kq"], 1)

                def ac_vev(g):
                    if prefix:
                        w("p_sd", g)
                    elif L4:
                        w("p_sd", 4 * g)
                    if L4:
                        w("p_y", 8 * g)
                    for idx in range(8):
                        j, nb = idx // 2, idx % 2
                        w("p_v", 8 * g + idx + 1)
                        ins = s.copy(Vb[:, j * D + nb * 512:j * D + (nb + 1) * 512],
                                     PSR[idx % 2][:])
                        if idx < 7:
                            ins.then_inc(S["d_ve"], 1)
                        else:
                            asp("d_ve")

                def ac_scev(g):
                    w("p_sT", g + 1)
                    s.copy(scoresT[:, g * GT:(g + 1) * GT], ps_x1[0:16, 0:512]).then_inc(S["d_sE"], 1)

                def ac_ktev(g):
                    if L4:
                        w("p_A", 4 * g)
                    if L5:
                        w("sst", 32 * g)
                    for i in range(8):
                        sub, half = i // 2, i % 2
                        w("p_kt", 8 * g + i + 1)
                        s.copy(
                            Kfm[:].rearrange("p (pr t) -> p pr t", pr=8)
                                  [:, half * 4:(half + 1) * 4, sub * 128:(sub + 1) * 128],
                            PSR[i % 2][:].bitcast(BF16)[:, 0:512].rearrange("p (tt t) -> p tt t", tt=4),
                        ).then_inc(S["d_kf"], 1)

                for g in range(NG):
                    ac_sq(g)
                    if prefix:
                        ac_rw(g)
                    if g >= 1:
                        if L4:
                            for cc in range(4):
                                for r in range(2):
                                    ac_yev(g - 1, cc, r)
                        if L5:
                            ac_rev(g - 1)
                    if L3:
                        ac_qev(g)
                    if prefix or L3:
                        ac_kev(g)
                        ac_vev(g)
                        ac_ksq(g)
                        ac_krstd(g)
                    if L2:
                        ac_e4p16(g)
                        ac_scev(g)
                    if L3:
                        ac_ktev(g)
                    if prefix or L3:
                        asp("d_rdy")
                if L4:
                    for cc in range(4):
                        for r in range(2):
                            ac_yev(NG - 1, cc, r)
                if L5:
                    ac_rev(NG - 1)

        return S

    attn_phase("P1", xpf, prefix=True)
    attn_phase("P2", xf, prefix=False)

    esA.close()

    esB = ExitStack()
    sbB = lambda name, shape, dt: esB.enter_context(nc.sbuf_tensor("sc_" + name, shape, dt))
    acc = sbB("acc", [128, 8 * T], F32)
    gub_sb = sbB("gub_sb", [128, 256], F32)
    dnb_sb = sbB("dnb_sb", [16, D], F32)
    esel_sb = sbB("esel_sb", [16, NE * 128], F32)
    act = sbB("act", [128, 8 * T], BF16)
    wgu_sb = [sbB(f"wgu{i}", [128, 8 * 256], BF16) for i in range(3)]
    wd_sb = [sbB(f"wd{i}", [128, 8 * 128], BF16) for i in range(2)]
    sco_bc = sbB("sco_bc", [128, T], BF16)
    g1b = [sbB(f"g1b{i}", [128, 1024], F32) for i in range(2)]
    sigb = [sbB(f"sigb{i}", [128, 1024], F32) for i in range(2)]
    u2b = [sbB(f"u2b{i}", [128, 1024], F32) for i in range(2)]
    glub = [sbB("glub0", [128, 1024], F32)] * 2

    esB2 = ExitStack()
    ps_a = esB2.enter_context(nc.psum_tensor("ps_a", [128, 1024], F32))
    ps_b = esB2.enter_context(nc.psum_tensor("ps_b", [128, 1024], F32))
    ps_c = esB2.enter_context(nc.psum_tensor("ps_c", [128, 1024], F32))
    ps_d = esB2.enter_context(nc.psum_tensor("ps_d", [128, 1024], F32))
    PGT = [ps_a, ps_c]
    PUP = [ps_b, ps_d]
    PDN = [ps_b, ps_d]
    PSC = [ps_a, ps_c]

    s_r3 = sem("s_r3")
    s_wg = sem("s_wg")
    s_wd = sem("s_wd")
    p_bias = sem("p_bias")
    d_bias = sem("d_bias")
    p_sc3 = sem("p_sc3")
    d_sb3 = sem("d_sb3")
    p_gu = sem("p_gu")
    d_gu = sem("d_gu")
    d_g1 = sem("d_g1")
    a_sg3 = sem("a_sg3")
    d_act = sem("d_act")
    p_dn3 = sem("p_dn3")
    d_dn3 = sem("d_dn3")

    with nc.Block("P3") as b:
        @b.sync
        def _(sync):
            sync.dma_start(dnb_sb[:], dnb_h).then_inc(s_r3, 16)
            sync.dma_start(esel_sb[:], esel_h).then_inc(s_r3, 16)
            sync.dma_start(gub_sb[:], gub_h).then_inc(s_r3, 16)
            for c_ in range(8):
                sync.dma_start(
                    acc[:, c_ * T:(c_ + 1) * T].rearrange("p (gg t) -> p gg t", gg=4),
                    rscr.rearrange("p (gg c t) -> p gg c t", gg=4, c=8)[:, :, c_, :],
                ).then_inc(s_r3, 16)

        @b.gpsimd
        def _(g):
            for e in range(NE):
                for j in range(8):
                    idx = e * 8 + j
                    if idx >= 3:
                        g.wait_ge(p_gu, 4 * idx - 8)
                    g.dma_start(wgu_sb[idx % 3][:].rearrange("p (c f) -> p c f", c=8),
                                wgu_h[e, j].rearrange("(c p) f -> p c f", p=128)).then_inc(s_wg, 16)
                for hb in range(8):
                    idx = e * 8 + hb
                    if idx >= 2:
                        g.wait_ge(p_dn3, 4 * (idx - 1))
                    g.dma_start(wd_sb[idx % 2][:].rearrange("p (c f) -> p c f", c=8),
                                wd_h[e, hb].rearrange("(c p) f -> p c f", p=128)).then_inc(s_wd, 16)

        @b.tensor
        def _(t):
            t.wait_ge(s_r3, 16)
            for ds in range(32):
                hb, st = ds // 4, ds % 4
                if ds >= 2:
                    t.wait_ge(d_bias, ds - 1)
                t.matmul(PDN[ds % 2][:, 0:512], dnb_sb[:, hb * 128:(hb + 1) * 128],
                         scoresT[:, st * 512:(st + 1) * 512], start=True, stop=True).then_inc(p_bias, 1)
            def score_bc(e):
                for st in range(4):
                    ins = t.matmul(PSC[st // 2][:, (st % 2) * 512:(st % 2 + 1) * 512],
                                   esel_sb[:, e * 128:(e + 1) * 128],
                                   scoresT[:, st * 512:(st + 1) * 512], start=True, stop=True)
                ins.then_inc(p_sc3, 1)

            for e in range(NE):
                if e == 0:
                    t.wait_ge(s_r3, 32)
                    score_bc(0)
                    t.wait_ge(d_bias, 32)
                for u in range(16):
                    j, hf = u // 2, u % 2
                    U = 16 * e + u
                    t.wait_ge(s_wg, 16 * (e * 8 + j + 1))
                    if U >= 2:
                        t.wait_ge(d_gu, U - 1)
                    if u <= 1:
                        t.wait_ge(d_sb3, e + 1)
                        if e == 0:
                            t.wait_ge(d_bias, 32)
                        else:
                            t.wait_ge(d_dn3, 32 * e)
                    for fc in range(8):
                        for s2 in range(2):
                            t.matmul(PGT[U % 2][:, s2 * 512:(s2 + 1) * 512],
                                     wgu_sb[(e * 8 + j) % 3][:, fc * 256:fc * 256 + 128],
                                     xn_bf[:, fc * T + hf * 1024 + s2 * 512:fc * T + hf * 1024 + (s2 + 1) * 512],
                                     start=(fc == 0), stop=(fc == 7))
                    for fc in range(8):
                        for s2 in range(2):
                            ins = t.matmul(PUP[U % 2][:, s2 * 512:(s2 + 1) * 512],
                                           wgu_sb[(e * 8 + j) % 3][:, fc * 256 + 128:fc * 256 + 256],
                                           xn_bf[:, fc * T + hf * 1024 + s2 * 512:fc * T + hf * 1024 + (s2 + 1) * 512],
                                           start=(fc == 0), stop=(fc == 7))
                    ins.then_inc(p_gu, 2)
                t.wait_ge(d_gu, 16 * (e + 1))
                if e + 1 < NE:
                    score_bc(e + 1)
                t.wait_ge(d_act, 16 * (e + 1))
                for ds in range(32):
                    hb, st = ds // 4, ds % 4
                    t.wait_ge(s_wd, 16 * (e * 8 + hb + 1))
                    gds = 32 * e + ds
                    if gds >= 2:
                        t.wait_ge(d_dn3, gds - 1)
                    for fb in range(8):
                        ins = t.matmul(PDN[ds % 2][:, 0:512],
                                       wd_sb[(e * 8 + hb) % 2][:, fb * 128:(fb + 1) * 128],
                                       act[:, fb * T + st * 512:fb * T + (st + 1) * 512],
                                       start=(fb == 0), stop=(fb == 7))
                    ins.then_inc(p_dn3, 1)

        @b.vector
        def _(v):
            for ds in range(32):
                hb, st = ds // 4, ds % 4
                v.wait_ge(s_r3, 48 + 16 * (hb + 1))
                v.wait_ge(p_bias, ds + 1)
                v.tensor_add(acc[:, hb * T + st * 512:hb * T + (st + 1) * 512],
                             acc[:, hb * T + st * 512:hb * T + (st + 1) * 512],
                             PDN[ds % 2][:, 0:512]).then_inc(d_bias, 1)
            for e in range(NE):
                def frontend(u):
                    U = 16 * e + u
                    j, hf = u // 2, u % 2
                    v.wait_ge(p_gu, 2 * U + 2)
                    if U == 0:
                        v.wait_ge(s_r3, 48)
                    if U >= 1:
                        v.wait_ge(a_sg3, U - 1)
                    v.tensor_scalar(g1b[u % 2][:], PGT[U % 2][:],
                                    gub_sb[:, e * 16 + 2 * j:e * 16 + 2 * j + 1], 7.0,
                                    op0=ALU.add, op1=ALU.min)
                    v.tensor_scalar(u2b[u % 2][:], PUP[U % 2][:],
                                    gub_sb[:, e * 16 + 2 * j + 1:e * 16 + 2 * j + 2], -7.0,
                                    op0=ALU.add, op1=ALU.max).then_inc(d_g1, 1)
                    ins = v.tensor_scalar(u2b[u % 2][:], u2b[u % 2][:], 7.0, None, op0=ALU.min)
                    ins.then_inc(d_gu, 1)

                def backend(u):
                    U = 16 * e + u
                    j, hf = u // 2, u % 2
                    v.wait_ge(a_sg3, U + 1)
                    if u == 0:
                        v.wait_ge(p_dn3, 32 * e)
                        v.wait_ge(d_sb3, e + 1)
                    v.tensor_mul(glub[u % 2][:], g1b[u % 2][:], sigb[u % 2][:])
                    v.scalar_tensor_tensor(act[:, j * T + hf * 1024:j * T + (hf + 1) * 1024],
                                           u2b[u % 2][:], 1.0, glub[u % 2][:],
                                           op0=ALU.add, op1=ALU.mult)
                    v.tensor_mul(act[:, j * T + hf * 1024:j * T + (hf + 1) * 1024],
                                 act[:, j * T + hf * 1024:j * T + (hf + 1) * 1024],
                                 sco_bc[:, hf * 1024:(hf + 1) * 1024]).then_inc(d_act, 1)

                frontend(0)
                for u in range(1, 16):
                    frontend(u)
                    backend(u - 1)
                backend(15)
                for ds in range(32):
                    hb, st = ds // 4, ds % 4
                    v.wait_ge(p_dn3, 32 * e + ds + 1)
                    v.tensor_add(acc[:, hb * T + st * 512:hb * T + (st + 1) * 512],
                                 acc[:, hb * T + st * 512:hb * T + (st + 1) * 512],
                                 PDN[ds % 2][:, 0:512]).then_inc(d_dn3, 1)

        @b.scalar
        def _(s):
            for e in range(NE):
                s.wait_ge(p_sc3, e + 1)
                if e >= 1:
                    s.wait_ge(d_act, 16 * e)
                s.copy(sco_bc[:, 0:1024], PSC[0][:])
                s.copy(sco_bc[:, 1024:2048], PSC[1][:]).then_inc(d_sb3, 1)
                for u in range(16):
                    U = 16 * e + u
                    s.wait_ge(d_g1, U + 1)
                    if U >= 1:
                        s.wait_ge(d_act, U - 1)
                    s.activation(sigb[u % 2][:], g1b[u % 2][:], AF.Sigmoid,
                                 scale=1.702).then_inc(a_sg3, 1)

    esB2.close()
    ps_t = [esB.enter_context(nc.psum_tensor(f"ps_t{i}", [128, 1024], F32)) for i in range(2)]
    outb = g1b
    p_mg = sem("p_mg")
    d_mg = sem("d_mg")
    s_out = sem("s_out")

    s_xp = sem("s_xp")
    xb = sigb

    with nc.Block("P4") as b:
        @b.tensor
        def _(t):
            for cc in range(16):
                if cc >= 2:
                    t.wait_ge(d_mg, cc - 1)
                for fc in range(8):
                    ins = t.transpose(ps_t[cc % 2][:, fc * 128:(fc + 1) * 128],
                                      acc[:, fc * T + cc * 128:fc * T + (cc + 1) * 128], iden[:])
                ins.then_inc(p_mg, 1)

        @b.vector
        def _(v):
            for cc in range(16):
                v.wait_ge(p_mg, cc + 1)
                v.wait_ge(s_xp, 16 * (cc + 1))
                if cc >= 2:
                    v.wait_ge(s_out, 16 * (cc - 1))
                v.tensor_add(outb[cc % 2][:], xb[cc % 2][:], ps_t[cc % 2][:]).then_inc(d_mg, 1)

        @b.sync
        def _(sync):
            for cc in range(18):
                if cc < 16:
                    if cc >= 2:
                        sync.wait_ge(d_mg, cc - 1)
                    sync.dma_start(xb[cc % 2][:], xtok_h[cc * 128:(cc + 1) * 128, :]).then_inc(s_xp, 16)
                if cc >= 2:
                    sync.wait_ge(d_mg, cc - 1)
                    sync.dma_start(y_h[(cc - 2) * 128:(cc - 1) * 128, :],
                                   outb[cc % 2][:]).then_inc(s_out, 16)
            sync.wait_ge(s_out, 16 * 16)

    esB.close()
    es.close()
    return nc



_NC = None


def _get_nc():
    global _NC
    if _NC is None:
        _NC = build_nc()
    return _NC


def _host_consts():
    i = np.arange(128)
    c = {}
    c["iden"] = np.eye(128, dtype=np.float32)
    c["dclamp"] = np.maximum(i[None, :] - i[:, None], 0).astype(np.float32)
    c["mask01"] = (i[None, :] >= i[:, None]).astype(np.float32)
    c["tposb"] = np.broadcast_to((i + 1)[None, :], (128, 128)).astype(np.float32).copy()
    c["jrev"] = (127 - i)[:, None].astype(np.float32)
    m64 = np.zeros((128, 2), np.float32)
    m64[:64, 0] = 1.0
    m64[64:, 1] = 1.0
    c["m64"] = m64
    esel = np.zeros((16, NE * 128), np.float32)
    for e in range(NE):
        esel[e, e * 128:(e + 1) * 128] = 1.0
    c["esel"] = esel
    return c


def kernel(L0_attn_x, L0_attn_W_qkv, L0_attn_W_o, L0_attn_log_gamma,
           L0_ffn_router_w, L0_ffn_router_b, L0_ffn_gate_up_proj, L0_ffn_gate_up_bias,
           L0_ffn_down_proj, L0_ffn_down_bias):
    x = np.asarray(L0_attn_x, np.float32)
    Wqkv = np.asarray(L0_attn_W_qkv, np.float32)
    Wo = np.asarray(L0_attn_W_o, np.float32)
    lg = np.asarray(L0_attn_log_gamma, np.float32)
    rw = np.asarray(L0_ffn_router_w, np.float32)
    rb = np.asarray(L0_ffn_router_b, np.float32)
    gup = np.asarray(L0_ffn_gate_up_proj, np.float32)
    gub = np.asarray(L0_ffn_gate_up_bias, np.float32)
    wdp = np.asarray(L0_ffn_down_proj, np.float32)
    dnb = np.asarray(L0_ffn_down_bias, np.float32)

    consts = _host_consts()
    wq = np.ascontiguousarray(Wqkv[0:1024].T)
    wk = np.ascontiguousarray(Wqkv[1024:2048].T)
    wv = np.ascontiguousarray(Wqkv[2048:3072].T)
    wo = np.ascontiguousarray(Wo.T)

    gate_w = gup[:, :, 0::2]
    up_w = gup[:, :, 1::2]
    wgu = np.empty((NE, 8, D, 256), np.float32)
    for j in range(8):
        wgu[:, j, :, 0:128] = gate_w[:, :, j * 128:(j + 1) * 128]
        wgu[:, j, :, 128:256] = up_w[:, :, j * 128:(j + 1) * 128]
    wd = np.ascontiguousarray(wdp.reshape(NE, D, 8, 128).transpose(0, 2, 1, 3))

    gate_b = gub[:, 0::2]
    up_b = gub[:, 1::2]
    gub_cols = np.empty((128, 256), np.float32)
    for e in range(NE):
        for j in range(8):
            gub_cols[:, e * 16 + 2 * j] = gate_b[e, j * 128:(j + 1) * 128]
            gub_cols[:, e * 16 + 2 * j + 1] = up_b[e, j * 128:(j + 1) * 128]

    nc = _get_nc()
    in_maps = []
    for c in range(8):
        bb, hh = c // 2, c % 2
        sl = x[bb, 2048 * hh:2048 * (hh + 1)]
        xfc = np.ascontiguousarray(sl.T)
        if hh == 1:
            xpfc = np.ascontiguousarray(x[bb, 0:2048].T)
            flagc = np.ones((128, 1), np.float32)
        else:
            xpfc = xfc
            flagc = np.zeros((128, 1), np.float32)
        m = dict(xf=xfc, xpf=xpfc, xtok=np.ascontiguousarray(sl), wq=wq, wk=wk, wv=wv, wo=wo,
                 rw=rw, rb=rb.reshape(16, 1), wgu=wgu, wd=wd,
                 gub=gub_cols, dnb=dnb, esel=consts["esel"],
                 iden=consts["iden"], dclamp=consts["dclamp"], mask01=consts["mask01"],
                 tposb=consts["tposb"], jrev=consts["jrev"], m64=consts["m64"],
                 flag=flagc, lg=lg.reshape(1, 16))
        in_maps.append(m)

    res = run_bass_kernel_spmd(nc, in_maps, core_ids=list(range(8)))
    out = np.empty((4, 4096, 1024), np.float32)
    for c in range(8):
        bb, hh = c // 2, c % 2
        out[bb, 2048 * hh:2048 * (hh + 1)] = res.results[c]["y"]
    return out


# version 15
# speedup vs baseline: 1.0226x; 1.0226x over previous
import numpy as np
from contextlib import ExitStack

import concourse.bass as bass
import concourse.mybir as mybir
from concourse.bass_utils import run_bass_kernel_spmd

F32 = mybir.dt.float32
I32 = mybir.dt.int32
BF16 = mybir.dt.bfloat16
AF = mybir.ActivationFunctionType
ALU = mybir.AluOpType
AX = mybir.AxisListType.X

D = 1024
T = 2048
NG = 4
GT = 512
NH = 16
NE = 16
EPS = 1e-5
NEG = -1e30


def build_nc():
    nc = bass.Bass("TRN2", target_bir_lowering=False)

    xf = nc.dram_tensor("xf", [D, T], F32, kind="ExternalInput").ap()
    xpf = nc.dram_tensor("xpf", [D, T], F32, kind="ExternalInput").ap()
    wq_h = nc.dram_tensor("wq", [D, D], F32, kind="ExternalInput").ap()
    wk_h = nc.dram_tensor("wk", [D, D], F32, kind="ExternalInput").ap()
    wv_h = nc.dram_tensor("wv", [D, D], F32, kind="ExternalInput").ap()
    wo_h = nc.dram_tensor("wo", [D, D], F32, kind="ExternalInput").ap()
    rw_h = nc.dram_tensor("rw", [D, 16], F32, kind="ExternalInput").ap()
    rb_h = nc.dram_tensor("rb", [16, 1], F32, kind="ExternalInput").ap()
    wgu_h = nc.dram_tensor("wgu", [NE, 8, D, 256], F32, kind="ExternalInput").ap()
    wd_h = nc.dram_tensor("wd", [NE, 8, D, 128], F32, kind="ExternalInput").ap()
    gub_h = nc.dram_tensor("gub", [128, 256], F32, kind="ExternalInput").ap()
    dnb_h = nc.dram_tensor("dnb", [16, D], F32, kind="ExternalInput").ap()
    esel_h = nc.dram_tensor("esel", [16, NE * 128], F32, kind="ExternalInput").ap()
    iden_h = nc.dram_tensor("iden", [128, 128], F32, kind="ExternalInput").ap()
    dclamp_h = nc.dram_tensor("dclamp", [128, 128], F32, kind="ExternalInput").ap()
    mask01_h = nc.dram_tensor("mask01", [128, 128], F32, kind="ExternalInput").ap()
    tposb_h = nc.dram_tensor("tposb", [128, 128], F32, kind="ExternalInput").ap()
    jrev_h = nc.dram_tensor("jrev", [128, 1], F32, kind="ExternalInput").ap()
    m64_h = nc.dram_tensor("m64", [128, 2], F32, kind="ExternalInput").ap()
    flag_h = nc.dram_tensor("flag", [128, 1], F32, kind="ExternalInput").ap()
    lg_h = nc.dram_tensor("lg", [1, 16], F32, kind="ExternalInput").ap()

    y_h = nc.dram_tensor("y", [T, D], F32, kind="ExternalOutput").ap()
    xtok_h = nc.dram_tensor("xtok", [T, D], F32, kind="ExternalInput").ap()
    rscr = nc.dram_tensor("rscr", [128, 8 * T], F32, kind="Internal").ap()

    es = ExitStack()
    sb = lambda name, shape, dt: es.enter_context(nc.sbuf_tensor("sb_" + name, shape, dt))
    sem = lambda name: es.enter_context(nc.semaphore(name))

    xn_bf = sb("xn_bf", [128, 8 * T], BF16)
    scoresT = sb("scoresT", [16, T], F32)
    S_T = sb("S_T", [128, 512], F32)
    iden = sb("iden", [128, 128], F32)
    ones_r = sb("ones_r", [1, 128], F32)
    ones_c = sb("ones_c", [128, 1], F32)
    onesw = sb("onesw", [128, 64], F32)
    eps_c = sb("eps_c", [128, 1], F32)
    jnk1 = sb("jnk1", [128, 512], F32)
    jnkA = sb("jnkA", [128, 128], F32)
    flag = sb("flag", [128, 1], F32)
    rb_sb = sb("rb_sb", [16, 1], F32)
    rw_sb = sb("rw_sb", [128, 8 * 16], F32)

    esA = ExitStack()
    sbA = lambda name, shape, dt: esA.enter_context(nc.sbuf_tensor("sa_" + name, shape, dt))
    wq_sb = sbA("wq_sb", [128, 8 * D], BF16)
    wk_sb = sbA("wk_sb", [128, 8 * D], BF16)
    wv_sb = sbA("wv_sb", [128, 8 * D], BF16)
    wo_sb = sbA("wo_sb", [128, 8 * D], BF16)
    xg = sbA("xg", [128, 8 * GT], F32)
    scr = sbA("scr", [128, 8 * GT], F32)
    QY = sbA("QY", [128, 8 * GT], BF16)
    Vb = sbA("Vb", [128, NG * D], BF16)
    Kfm = sbA("Kfm", [128, 8 * GT], BF16)
    qg = sbA("qg", [128, 8 * 128], BF16)
    kg = sbA("kg", [128, D], BF16)
    Klnb = sbA("Klnb", [128, NG * D], BF16)
    iden_bf = sbA("iden_bf", [128, 128], BF16)
    adb = sbA("adb", [128, 2048], BF16)
    S_bf = sbA("S_bf", [128, 512], BF16)
    Dw = sbA("Dw", [128, NH * 128], BF16)
    Gq = sbA("Gq", [128, 8 * 128], BF16)
    Gk = sbA("Gk", [128, D], BF16)
    gC = sbA("gC", [128, 512], F32)
    rowb = sbA("rowb", [1, 3 * GT], F32)
    ltok = sbA("ltok", [128, 64], F32)
    lz = sbA("lz", [128, 64], F32)
    m8b = sbA("m8b", [128, 32], F32)
    nmx = sbA("nmx", [128, 4], F32)
    e4b = sbA("e4b", [128, 16], F32)
    p16b = sbA("p16b", [128, 64], F32)
    selb = sbA("selb", [128, 64], F32)
    scob = sbA("scob", [128, 64], F32)
    dnm = sbA("dnm", [128, 8], F32)
    kst = sbA("kst", [128, 256], F32)
    lraw = sbA("lraw", [16, GT], F32)
    esP = ExitStack()
    sbP = lambda name, shape, dt: esP.enter_context(nc.sbuf_tensor("sp_" + name, shape, dt))
    ptmp = sbP("ptmp", [128, 128], F32)
    grow = sbP("grow", [1, 16], F32)
    lnrow = sbP("lnrow", [1, 16], F32)
    lnbc = sbP("lnbc", [128, 16], F32)
    lnpair = sbP("lnpair", [128, 8], F32)
    gcol = sbP("gcol", [128, 1], F32)
    gCcol = sbP("gCcol", [128, 16], F32)
    gCrow = sbP("gCrow", [1, 16], F32)
    m64 = sbP("m64", [128, 2], F32)
    jrev = sbP("jrev", [128, 1], F32)
    dclamp = sbP("dclamp", [128, 128], BF16)
    mask01 = sbP("mask01", [128, 128], BF16)
    tposb = sbP("tposb", [128, 128], BF16)
    lgt = sbP("lgt", [1, 16], F32)

    ps_bc = esA.enter_context(nc.psum_tensor("ps_bc", [128, 1024], F32))
    ps_x0 = esA.enter_context(nc.psum_tensor("ps_x0", [128, 512], F32))
    ps_x1 = esA.enter_context(nc.psum_tensor("ps_x1", [128, 512], F32))
    ps_r0 = esA.enter_context(nc.psum_tensor("ps_r0", [128, 512], F32))
    ps_r1 = esA.enter_context(nc.psum_tensor("ps_r1", [128, 512], F32))
    ps_sd = esA.enter_context(nc.psum_tensor("ps_sd", [128, 512], F32))
    ps_y1 = esA.enter_context(nc.psum_tensor("ps_y1", [128, 512], F32))
    PSX = [ps_x0, ps_x1]
    PSR = [ps_r0, ps_r1]

    s_ld = sem("s_ld")
    s_ldw = sem("s_ldw")
    pg = sem("pg")
    p0s = sem("p0s")
    vself = sem("vself")
    vself_n = [0]

    def vg(v, ins):
        vself_n[0] += 1
        ins.then_inc(vself, 1)
        v.wait_ge(vself, vself_n[0])

    with nc.Block("p0load") as b:
        @b.sync
        def _(sync):
            n = 0
            for dst, src in [(iden, iden_h), (jrev, jrev_h), (m64, m64_h),
                             (flag, flag_h), (lgt, lg_h), (rb_sb, rb_h),
                             (rw_sb, rw_h)]:
                if dst is rw_sb:
                    sync.dma_start(rw_sb[:].rearrange("p (c o) -> p c o", c=8),
                                   rw_h.rearrange("(c p) o -> p c o", p=128)).then_inc(s_ld, 16)
                else:
                    sync.dma_start(dst[:], src).then_inc(s_ld, 16)
                n += 1
            sync.wait_ge(s_ld, n * 16)

        @b.gpsimd
        def _(g):
            for dst, src in [(dclamp, dclamp_h), (mask01, mask01_h), (tposb, tposb_h)]:
                g.dma_start(dst[:].rearrange("p (a b) -> p a b", a=2),
                            src.rearrange("p (a b) -> p a b", a=2)).then_inc(s_ldw, 16)
            for dst, src in [(wk_sb, wk_h), (wv_sb, wv_h)]:
                g.dma_start(dst[:].rearrange("p (c o) -> p c o", c=8),
                            src.rearrange("(c p) o -> p c o", p=128)).then_inc(s_ldw, 16)
            g.wait_ge(s_ldw, 5 * 16)

    with nc.Block("p0gamma") as b:
        @b.vector
        def _(v):
            for t_, val in [(ones_r, 1.0), (ones_c, 1.0), (onesw, 1.0),
                            (eps_c, EPS), (jnk1, 0.0)]:
                v.memset(t_[:], val)
            v.tensor_copy(iden_bf[:], iden[:])
            v.memset(S_T[:], 0.0).then_inc(pg, 1)

            def vsp(semh, k=1):
                v.tensor_copy(jnk1[:], jnk1[:]).then_inc(semh, k)
            v.wait_ge(pg, 3)
            v.tensor_copy(lnbc[:], ps_bc[:, 0:16])
            vsp(p0s)
            v.wait_ge(p0s, 2)
            lnbc3 = lnbc[:].rearrange("p (a b) -> p a b", b=2)
            vg(v, v.tensor_scalar(lnpair[:], lnbc3[:, :, 0], m64[:, 0:1], None, op0=ALU.mult))
            vg(v, v.tensor_scalar(ptmp[:, 0:8], lnbc3[:, :, 1], m64[:, 1:2], None, op0=ALU.mult))
            vg(v, v.tensor_add(lnpair[:], lnpair[:], ptmp[:, 0:8]))
            vsp(p0s)
            v.wait_ge(p0s, 3)
            v.tensor_scalar(gCrow[:], lnrow[:], 128.0, None, op0=ALU.mult)
            vsp(pg)
            v.wait_ge(pg, 6)
            v.tensor_copy(gCcol[:], ps_bc[:, 16:32])
            vsp(p0s)
            v.wait_ge(p0s, 4)
            for h in range(NH):
                pb = 64 * (h % 2)
                v.tensor_scalar(gC[pb:pb + 64, (h // 2) * 64:(h // 2 + 1) * 64],
                                onesw[pb:pb + 64, :], gCcol[pb:pb + 64, h:h + 1], None,
                                op0=ALU.mult)
            v.nop().then_inc(pg, 1)
            TMP = xn_bf[:].bitcast(F32)
            for h in range(NH):
                v.tensor_scalar(TMP[:, h * 128:(h + 1) * 128], dclamp[:], lnbc[:, h:h + 1],
                                None, op0=ALU.mult)
            for p in range(8):
                v.tensor_scalar(TMP[:, 2048 + p * 128:2048 + (p + 1) * 128], tposb[:],
                                lnpair[:, p:p + 1], None, op0=ALU.mult)
            v.tensor_scalar(TMP[:, 3072:3088], lnbc[:], jrev[:], None, op0=ALU.mult)
            vsp(pg)
            v.wait_ge(pg, 9)
            for h in range(NH):
                v.tensor_mul(Dw[:, h * 128:(h + 1) * 128], TMP[:, h * 128:(h + 1) * 128],
                             mask01[:])
            v.tensor_copy(Gq[:], TMP[:, 2048:3072])
            for h in range(NH):
                v.tensor_scalar(Gk[:, h * 64:(h + 1) * 64], onesw[:], TMP[:, 3072 + h:3073 + h],
                                None, op0=ALU.mult)
            vg(v, v.tensor_mul(ptmp[:, 0:16], gCcol[:], gCcol[:]))
            vg(v, v.tensor_mul(ptmp[:, 16:32], ptmp[:, 0:16], gCcol[:]))
            Gk3 = Gk[:].rearrange("p (h d) -> p h d", d=64)
            for cc, fac in ((0, ptmp[:, 16:32]), (1, ptmp[:, 0:16]), (2, gCcol[:])):
                v.tensor_tensor(Kfm[:, cc * 1024:(cc + 1) * 1024].rearrange("p (h d) -> p h d", d=64),
                                Gk3, fac.unsqueeze(2).to_broadcast([128, NH, 64]), op=ALU.mult)

        @b.scalar
        def _(s):
            s.wait_ge(s_ld, 7 * 16)
            def asp(semh, k=1):
                s.copy(jnkA[:], iden[:]).then_inc(semh, k)

            s.activation(grow[:], lgt[:], AF.Sigmoid)
            asp(p0s)
            s.wait_ge(p0s, 1)
            s.activation(lnrow[:], grow[:], AF.Ln)
            asp(pg)
            s.wait_ge(pg, 4)
            s.activation(gCrow[:], gCrow[:], AF.Exp)
            asp(pg)
            TMPs = xn_bf[:].bitcast(F32)
            s.wait_ge(pg, 8)
            s.activation(TMPs[:, 0:2048], TMPs[:, 0:2048], AF.Exp)
            s.activation(TMPs[:, 2048:3088], TMPs[:, 2048:3088], AF.Exp)
            asp(pg)

        @b.tensor
        def _(t):
            t.wait_ge(pg, 2)
            t.matmul(ps_bc[:, 0:16], ones_r[:], lnrow[:], start=True, stop=True).then_inc(pg, 1)
            t.wait_ge(pg, 5)
            t.matmul(ps_bc[:, 16:32], ones_r[:], gCrow[:], start=True, stop=True).then_inc(pg, 1)

    esP.close()

    ATTN_SEM_NAMES = ("sx sst a_sq p_st d_rw a_rs p_bc d_xn p_rl d_lg p_lt r1 r2 r3 r4 r5 "
                      "a_e4 d_dn a_p16 d_sc p_sT d_sE p_q d_q p_k d_ke a_kq d_ms d_st2 a_kr "
                      "d_kl p_kt d_kf p_v d_ve d_qk d_sb p_A d_ad p_y d_ye p_sd d_su p_wo d_r a_l1 a_l2 d_rdy")
    ATTN_SEMS = {n: sem("at_" + n) for n in ATTN_SEM_NAMES.split()}
    PREFIX_TOTALS = dict(sx=16 * NG, a_sq=NG, p_st=NG, d_rw=NG, a_rs=NG, p_bc=NG, d_xn=NG,
                         p_k=8 * NG, d_ke=8 * NG, a_kq=8 * NG, d_ms=8 * NG, d_st2=NG,
                         a_kr=NG, d_kl=4 * NG, p_v=8 * NG, d_ve=8 * NG, d_qk=4 * NG,
                         p_sd=NG, d_su=NG, a_l1=NG, a_l2=NG, d_rdy=NG)

    def attn_phase(tag, xsrc, prefix, lvl=5):
        S = ATTN_SEMS
        BASE = {} if prefix else PREFIX_TOTALS
        L2 = (not prefix) and lvl >= 2
        L3 = (not prefix) and lvl >= 3
        L4 = (not prefix) and lvl >= 4
        L5 = (not prefix) and lvl >= 5
        xnT = QY if prefix else xn_bf
        PKB = [kg[:], qg[:], adb[:, 0:1024], adb[:, 1024:2048]]
        NMU, VAR, RSTD = 0, GT, 2 * GT

        def xn_col(fc, g):
            return fc * (GT if prefix else T) + (0 if prefix else g * GT)

        with nc.Block(tag) as b:
            @b.sync
            def _(sync):
                def w(s, v):
                    v += BASE.get(s, 0)
                    if v > 0:
                        sync.wait_ge(S[s], v)

                def xload(g):
                    if g > 0:
                        w("d_xn", g)
                        if L5:
                            w("p_st", g)
                    elif not prefix:
                        w("d_xn", 0)
                    sync.dma_start(
                        xg[:].rearrange("p (c t) -> p c t", c=8),
                        xsrc[:, g * GT:(g + 1) * GT].rearrange("(c p) t -> p c t", p=128),
                    ).then_inc(S["sx"], 16)

                def rstore(g):
                    rstage = Kfm[:].bitcast(F32)
                    for hh in range(2):
                        w("d_r", 8 * g + 4 * (hh + 1))
                        sync.dma_start(rscr[:, g * 4096 + hh * 2048:g * 4096 + (hh + 1) * 2048],
                                       rstage[:, 0:2048]).then_inc(S["sst"], 16)

                for g in range(NG):
                    xload(g)
                    if L5 and g >= 1:
                        rstore(g - 1)
                if L5:
                    rstore(NG - 1)
                    w("sst", 32 * NG)

            @b.tensor
            def _(t):
                def w(s, v):
                    v += BASE.get(s, 0)
                    if v > 0:
                        t.wait_ge(S[s], v)

                def st_xsum(g):
                    w("sx", 16 * (g + 1)); w("d_ms", 8 * g)
                    if g == 0:
                        w("d_xn", 0)
                        if L2:
                            w("d_lg", 0)
                    for fc in range(8):
                        t.matmul(ps_bc[0:1, 0:512], ones_c[:], xg[:, fc * GT:(fc + 1) * GT],
                                 start=(fc == 0), stop=(fc == 7))

                def st_stats(g):
                    w("a_sq", g + 1); w("d_ms", 8 * g); w("d_xn", g)
                    for fc in range(8):
                        ins = t.matmul(ps_bc[0:1, 512:1024], ones_c[:], scr[:, fc * GT:(fc + 1) * GT],
                                       start=(fc == 0), stop=(fc == 7))
                    ins.then_inc(S["p_st"], 1)

                def st_bcast(g):
                    w("a_rs", g + 1)
                    t.matmul(ps_bc[:, 0:512], ones_r[:], rowb[0:1, 0:GT], start=True, stop=True)
                    t.matmul(ps_bc[:, 512:1024], ones_r[:], rowb[0:1, 2 * GT:3 * GT],
                             start=True, stop=True).then_inc(S["p_bc"], 1)

                def st_router(g):
                    w("d_xn", g + 1); w("d_lg", g)
                    if L4:
                        w("d_ad", 8 * g - 3)
                    if L3:
                        w("d_ke", 8 * g - 1)
                    for fc in range(8):
                        ins = t.matmul(ps_x0[0:16, :], rw_sb[:, fc * 16:(fc + 1) * 16],
                                       scr[:, fc * GT:(fc + 1) * GT],
                                       start=(fc == 0), stop=(fc == 7))
                    ins.then_inc(S["p_rl"], 1)

                def st_ltrans(g):
                    w("d_lg", g + 1); w("r1", g)
                    if L4 or prefix:
                        w("d_su", 4 * g)
                    for j in range(4):
                        ins = t.transpose(ps_sd[:, j * 16:(j + 1) * 16],
                                          lraw[:, j * 128:(j + 1) * 128], iden[0:16, 0:16])
                    ins.then_inc(S["p_lt"], 1)

                def st_q(g):
                    for ob in range(8):
                        if ob == 0:
                            if g == 0:
                                t.wait_ge(s_ldw, 6 * 16)
                            w("d_lg", g + 1)
                            if L5:
                                w("d_r", 8 * g - 1)
                            if L4:
                                w("d_ad", 8 * g - 1)
                            w("d_ke", 8 * g - 1)
                        elif ob == 1:
                            if L5:
                                w("d_r", 8 * g)
                            if L4:
                                w("d_ad", 8 * g)
                            w("d_ke", 8 * g)
                        else:
                            w("d_q", 8 * g + ob - 1)
                        for fc in range(8):
                            ins = t.matmul(PSX[ob % 2][:], wq_sb[:, fc * D + ob * 128:fc * D + (ob + 1) * 128],
                                           xn_bf[:, fc * T + g * GT:fc * T + (g + 1) * GT],
                                           start=(fc == 0), stop=(fc == 7))
                        ins.then_inc(S["p_q"], 1)

                def st_k(g):
                    for idx in range(8):
                        j, nb = idx // 2, idx % 2
                        if prefix:
                            w("d_ke", 8 * g + idx - 1)
                            if idx < 2:
                                w("d_xn", g + 1)
                        else:
                            if idx == 0:
                                w("d_q", 8 * g + 7)
                            elif idx == 1:
                                w("d_q", 8 * g + 8)
                            else:
                                w("d_ke", 8 * g + idx - 1)
                        for fc in range(8):
                            ins = t.matmul(PSX[idx % 2][:],
                                           xnT[:, xn_col(fc, g) + j * 128:xn_col(fc, g) + (j + 1) * 128],
                                           wk_sb[:, fc * D + nb * 512:fc * D + (nb + 1) * 512],
                                           start=(fc == 0), stop=(fc == 7))
                        ins.then_inc(S["p_k"], 1)

                def st_v(g):
                    for idx in range(8):
                        j, nb = idx // 2, idx % 2
                        if idx < 2 and not prefix:
                            if L4:
                                w("d_ad", 8 * g - 1 + idx)
                            elif L3:
                                w("d_kf", 8 * g - 1 + idx)
                        w("d_ve", 8 * g + idx - 1)
                        for fc in range(8):
                            ins = t.matmul(PSR[idx % 2][:],
                                           xnT[:, xn_col(fc, g) + j * 128:xn_col(fc, g) + (j + 1) * 128],
                                           wv_sb[:, fc * D + nb * 512:fc * D + (nb + 1) * 512],
                                           start=(fc == 0), stop=(fc == 7))
                        ins.then_inc(S["p_v"], 1)

                def st_sct(g):
                    w("d_sc", g + 1); w("d_ke", 8 * g + 8)
                    for j in range(4):
                        ins = t.transpose(ps_x1[0:16, j * 128:(j + 1) * 128],
                                          scob[:, j * 16:(j + 1) * 16], iden[:])
                    ins.then_inc(S["p_sT"], 1)

                def st_kt(g):
                    for i in range(8):
                        sub, half = i // 2, i % 2
                        w("d_kl", 4 * g + sub + 1)
                        if i == 0:
                            w("d_ve", 8 * g + 7)
                        elif i == 1:
                            w("d_ve", 8 * g + 8)
                        else:
                            w("d_kf", 8 * g + i - 1)
                        dstb = PSR[i % 2][:].bitcast(BF16)
                        for tt in range(4):
                            p = half * 4 + tt
                            ins = t.transpose(dstb[:, tt * 128:(tt + 1) * 128],
                                              Klnb[:, sub * D + p * 128:sub * D + (p + 1) * 128],
                                              iden_bf[:])
                        ins.then_inc(S["p_kt"], 1)

                def st_ret(g, cc):
                    w("d_sb", 4 * g + cc + 1)
                    w("d_qk", 4 * g + cc + 1)
                    w("d_kf", 8 * g + 8)
                    w("d_q", 8 * g + 8)
                    w("d_rdy", g + 1)
                    if cc == 0:
                        w("d_ke", 8 * g + 8)
                        w("d_sE", g + 1)
                    else:
                        w("d_ad", 8 * g + 2 * cc)
                    for p in range(8):
                        for hh in range(2):
                            dst = (PSX if hh == 0 else PSR)[p // 4]
                            ins = t.matmul(dst[:, (p % 4) * 128:(p % 4) * 128 + 128],
                                           Kfm[64 * hh:64 * hh + 64,
                                               p * GT + cc * 128:p * GT + (cc + 1) * 128],
                                           QY[64 * hh:64 * hh + 64,
                                              p * GT + cc * 128:p * GT + (cc + 1) * 128],
                                           start=True, stop=True)
                    ins.then_inc(S["p_A"], 1)
                    for r in range(2):
                        w("d_ad", 8 * g + 2 * cc + r + 1)
                        w("d_ye", 8 * g + 2 * cc + r)
                        for p in range(4 * r, 4 * r + 4):
                            for hh in range(2):
                                h = 2 * p + hh
                                t.matmul(ps_y1[64 * hh:64 * hh + 64, (p % 4) * 128:(p % 4) * 128 + 128],
                                         Vb[:, cc * D + h * 64:cc * D + (h + 1) * 64],
                                         adb[:, h * 128:(h + 1) * 128],
                                         start=True, stop=False)
                                ins = t.matmul(ps_y1[64 * hh:64 * hh + 64, (p % 4) * 128:(p % 4) * 128 + 128],
                                               S_bf[64 * hh:64 * hh + 64, p * 64:(p + 1) * 64],
                                               qg[64 * hh:64 * hh + 64, p * 128:(p + 1) * 128],
                                               start=False, stop=True)
                        ins.then_inc(S["p_y"], 1)
                    w("d_su", 4 * g + cc)
                    for h in range(NH):
                        ins = t.matmul(ps_sd[64 * (h % 2):64 * (h % 2) + 64,
                                             (h // 2) * 64:(h // 2 + 1) * 64],
                                       kg[:, h * 64:(h + 1) * 64],
                                       Vb[:, cc * D + h * 64:cc * D + (h + 1) * 64],
                                       start=True, stop=True)
                    ins.then_inc(S["p_sd"], 1)

                def st_sdall(g):
                    w("d_qk", 4 * g + 4)
                    w("d_ve", 8 * g + 8)
                    w("d_rdy", g + 1)
                    w("d_su", g)
                    for h in range(NH):
                        for cc in range(4):
                            kb = PKB[cc]
                            ins = t.matmul(ps_sd[64 * (h % 2):64 * (h % 2) + 64,
                                                 (h // 2) * 64:(h // 2 + 1) * 64],
                                           kb[:, h * 64:(h + 1) * 64],
                                           Vb[:, cc * D + h * 64:cc * D + (h + 1) * 64],
                                           start=(cc == 0), stop=(cc == 3))
                    ins.then_inc(S["p_sd"], 1)

                def st_warm(g):
                    for q in range(8):
                        w("a_kq", 8 * g + q + 1)
                        t.matmul(ps_y1[0:16, 0:16], ones_r[0:1, 0:16], ones_r[0:1, 0:16],
                                 start=True, stop=True)
                    for q in range(8):
                        w("d_ms", 8 * g + q + 1)
                        t.matmul(ps_y1[0:16, 0:16], ones_r[0:1, 0:16], ones_r[0:1, 0:16],
                                 start=True, stop=True)
                    for j in range(4):
                        w("d_kl", 4 * g + j + 1)
                        t.matmul(ps_y1[0:16, 0:16], ones_r[0:1, 0:16], ones_r[0:1, 0:16],
                                 start=True, stop=True)

                def st_wo(g):
                    w("d_ye", 8 * (g + 1))
                    for oc in range(8):
                        if oc == 0:
                            if g == 0:
                                t.wait_ge(s_ldw, 7 * 16)
                            w("d_ad", 8 * g + 7)
                        elif oc == 1:
                            w("d_ad", 8 * g + 8)
                        else:
                            w("d_r", 8 * g + oc - 1)
                        for yc in range(8):
                            ins = t.matmul(PSX[oc % 2][:],
                                           wo_sb[:, yc * D + oc * 128:yc * D + (oc + 1) * 128],
                                           QY[:, yc * GT:(yc + 1) * GT],
                                           start=(yc == 0), stop=(yc == 7))
                        ins.then_inc(S["p_wo"], 1)

                for g in range(NG):
                    bk = g >= 1
                    if prefix:
                        if g == 0:
                            st_xsum(0)
                        st_stats(g)
                        if bk:
                            st_sdall(g - 1)
                        st_bcast(g)
                        st_k(g)
                        st_v(g)
                        st_warm(g)
                        if g + 1 < NG:
                            st_xsum(g + 1)
                    else:
                        if g == 0:
                            st_xsum(0)
                        if bk and L4:
                            st_ret(g - 1, 0)
                        st_stats(g)
                        if bk and L4:
                            st_ret(g - 1, 1)
                        st_bcast(g)
                        if bk and L4:
                            st_ret(g - 1, 2)
                        if L2:
                            st_router(g)
                        if bk and L4:
                            st_ret(g - 1, 3)
                        if L2:
                            st_ltrans(g)
                        if bk and L5:
                            st_wo(g - 1)
                        if L3:
                            st_q(g)
                            st_k(g)
                            st_v(g)
                            st_warm(g)
                            if g + 1 < NG:
                                st_xsum(g + 1)
                        if L2:
                            st_sct(g)
                        if L3:
                            st_kt(g)
                if prefix:
                    st_sdall(NG - 1)
                else:
                    if L4:
                        for cc in range(4):
                            st_ret(NG - 1, cc)
                    if L5:
                        st_wo(NG - 1)

            @b.vector
            def _(v):
                def w(s, val):
                    val += BASE.get(s, 0)
                    if val > 0:
                        v.wait_ge(S[s], val)

                def sp(sname, k=1):
                    v.tensor_copy(jnk1[:], jnk1[:]).then_inc(S[sname], k)

                def newton_rsqrt(dst, varap, scr_f32):
                    vg(v, v.tensor_scalar(varap, varap, EPS, None, op0=ALU.add))
                    iv = varap.bitcast(I32)
                    si = scr_f32.bitcast(I32)
                    vg(v, v.tensor_scalar(si, iv, 1, None, op0=ALU.arith_shift_right))
                    vg(v, v.tensor_scalar(si, si, -1, None, op0=ALU.mult))
                    vg(v, v.tensor_scalar(si, si, 0x5F3759DF, None, op0=ALU.add))
                    for _ in range(2):
                        vg(v, v.tensor_mul(dst, scr_f32, scr_f32))
                        vg(v, v.tensor_mul(dst, dst, varap))
                        vg(v, v.tensor_scalar(dst, dst, -0.5, 1.5, op0=ALU.mult, op1=ALU.add))
                        vg(v, v.tensor_mul(scr_f32, scr_f32, dst))
                    ins = v.tensor_copy(dst, scr_f32)
                    vg(v, ins)
                    return ins
                def dv_rw(g):
                    mus = jnk1[0:1, 0:GT]
                    w("p_st", g + 1); w("p_bc", g)
                    vg(v, v.tensor_scalar(mus, ps_bc[0:1, 0:512], 1.0 / D, None, op0=ALU.mult))
                    vg(v, v.tensor_scalar(rowb[0:1, NMU:NMU + GT], mus, -1.0, None, op0=ALU.mult))
                    vg(v, v.tensor_mul(mus, mus, mus))
                    vg(v, v.tensor_scalar(rowb[0:1, VAR:VAR + GT], ps_bc[0:1, 512:1024], 1.0 / D, None, op0=ALU.mult))
                    vg(v, v.tensor_sub(rowb[0:1, VAR:VAR + GT], rowb[0:1, VAR:VAR + GT], mus))
                    if prefix:
                        sp("d_rw")
                    else:
                        newton_rsqrt(rowb[0:1, RSTD:RSTD + GT], rowb[0:1, VAR:VAR + GT], mus)
                        sp("a_rs")

                def dv_xn(g):
                    w("p_bc", g + 1)
                    if L2:
                        w("p_rl", g)
                    if prefix:
                        w("p_v", 8 * g)
                        w("p_k", 8 * g)
                    for fc in range(8):
                        v.tensor_add(scr[:, fc * GT:(fc + 1) * GT], xg[:, fc * GT:(fc + 1) * GT],
                                     ps_bc[:, 0:512])
                        v.tensor_mul(xnT[:, xn_col(fc, g):xn_col(fc, g) + GT],
                                     scr[:, fc * GT:(fc + 1) * GT], ps_bc[:, 512:1024])
                    sp("d_xn")

                def dv_lga(g):
                    w("p_rl", g + 1); w("p_lt", g)
                    vg(v, v.tensor_copy(lraw[:], ps_x0[0:16, :]))
                    vg(v, v.tensor_mul(lraw[:], lraw[:], ps_bc[0:16, 512:1024]))
                    vg(v, v.tensor_scalar(lraw[:], lraw[:], rb_sb[:], None, op0=ALU.add))
                    sp("d_lg")

                def dv_lgb(g):
                    w("p_lt", g + 1); w("a_e4", g); w("a_p16", g)
                    for j in range(4):
                        ins = v.tensor_copy(ltok[:, j * 16:(j + 1) * 16], ps_sd[:, j * 16:(j + 1) * 16])
                    ins.then_inc(S["r1"], 1)
                    v.wait_ge(S["r1"], g + 1)
                    for j in range(4):
                        ins = v.max(out=m8b[:, j * 8:(j + 1) * 8], in_=ltok[:, j * 16:(j + 1) * 16])
                    ins.then_inc(S["r2"], 1)
                    v.wait_ge(S["r2"], g + 1)
                    for j in range(4):
                        ins = v.memset(m8b[:, j * 8 + 4:(j + 1) * 8], NEG)
                    ins.then_inc(S["r3"], 1)
                    v.wait_ge(S["r3"], g + 1)
                    for j in range(4):
                        ins = v.match_replace(out=lz[:, j * 16:(j + 1) * 16],
                                              in_to_replace=m8b[:, j * 8:(j + 1) * 8],
                                              in_values=ltok[:, j * 16:(j + 1) * 16],
                                              imm_value=NEG)
                    ins.then_inc(S["r4"], 1)
                    v.wait_ge(S["r4"], g + 1)
                    for j in range(4):
                        ins = v.tensor_scalar(nmx[:, j:j + 1], m8b[:, j * 8:j * 8 + 1], -1.0, None,
                                              op0=ALU.mult)
                    ins.then_inc(S["r5"], 1)
                    for j in range(4):
                        ins = v.tensor_tensor(selb[:, j * 16:(j + 1) * 16], lz[:, j * 16:(j + 1) * 16],
                                              ltok[:, j * 16:(j + 1) * 16], op=ALU.not_equal)
                    vg(v, ins)

                def dv_lgc(g):
                    w("a_e4", g + 1)
                    for j in range(4):
                        ins = v.reduce_sum(dnm[:, j:j + 1], e4b[:, j * 4:(j + 1) * 4], axis=AX)
                    ins.then_inc(S["d_dn"], 1)
                    v.wait_ge(S["d_dn"], 2 * g + 1)
                    for j in range(4):
                        ins = v.reciprocal(dnm[:, 4 + j:5 + j], dnm[:, j:j + 1])
                    ins.then_inc(S["d_dn"], 1)
                    w("a_p16", g + 1); w("p_sT", g)
                    v.wait_ge(S["d_dn"], 2 * g + 2)
                    for j in range(4):
                        v.scalar_tensor_tensor(scob[:, j * 16:(j + 1) * 16],
                                               p16b[:, j * 16:(j + 1) * 16],
                                               dnm[:, 4 + j:5 + j],
                                               selb[:, j * 16:(j + 1) * 16],
                                               op0=ALU.mult, op1=ALU.mult)
                    sp("d_sc")

                def dv_kln(g):
                    for j in range(4):
                        w("d_ke", 8 * g + 2 * j + 2)
                        v.reduce_sum(kst[:, j * 16:(j + 1) * 16],
                                     scr[:, j * D:(j + 1) * D].rearrange("p (h d) -> p h d", d=64),
                                     axis=AX)
                        for hf in range(2):
                            q = 2 * j + hf
                            w("a_kq", 8 * g + q + 1)
                            c0 = 64 + j * 16 + hf * 8
                            ins = v.reduce_sum(kst[:, c0:c0 + 8],
                                               ps_bc[:, (q % 2) * 512:(q % 2) * 512 + 512]
                                               .rearrange("p (h d) -> p h d", d=64), axis=AX)
                            ins.then_inc(S["d_ms"], 1)
                    v.wait_ge(S["d_ms"], 8 * g + 8 + BASE.get("d_ms", 0))
                    vg(v, v.tensor_scalar(kst[:, 128:192], kst[:, 0:64], 1.0 / 64, None, op0=ALU.mult))
                    vg(v, v.tensor_mul(kst[:, 0:64], kst[:, 128:192], kst[:, 128:192]))
                    vg(v, v.tensor_scalar(kst[:, 64:128], kst[:, 64:128], 1.0 / 64, None, op0=ALU.mult))
                    vg(v, v.tensor_sub(kst[:, 64:128], kst[:, 64:128], kst[:, 0:64]))
                    sp("a_kr")
                    v.wait_ge(S["d_st2"], g + 1 + BASE.get("d_st2", 0))
                    if not prefix:
                        w("p_kt", 8 * g)
                    for j in range(4):
                        K3 = scr[:, j * D:(j + 1) * D].rearrange("p (h d) -> p h d", d=64)
                        KB3 = Klnb[:, j * D:(j + 1) * D].rearrange("p (h d) -> p h d", d=64)
                        mu3 = kst[:, 128 + j * 16:128 + (j + 1) * 16].unsqueeze(2).to_broadcast([128, 16, 64])
                        rs3 = kst[:, 192 + j * 16:192 + (j + 1) * 16].unsqueeze(2).to_broadcast([128, 16, 64])
                        v.tensor_tensor(K3, K3, mu3, op=ALU.subtract)
                        v.tensor_tensor(KB3, K3, rs3, op=ALU.mult).then_inc(S["d_kl"], 1)

                def dv_ret(g, cc):
                    w("p_sd", 4 * g + cc)
                    w("p_y", 8 * g + 2 * cc)
                    w("d_q", 8 * g + 8)
                    w("d_rdy", g + 1)
                    v.tensor_mul(kg[:], Klnb[:, cc * D:(cc + 1) * D], Gk[:])
                    v.tensor_mul(qg[:].rearrange("p (pr t) -> p pr t", t=128),
                                 QY[:].rearrange("p (pr t) -> p pr t", t=GT)[:, :, cc * 128:(cc + 1) * 128],
                                 Gq[:].rearrange("p (pr t) -> p pr t", t=128))
                    sp("d_qk")
                    v.tensor_copy(S_bf[:], S_T[:])
                    sp("d_sb")
                    w("p_A", 4 * g + cc + 1)
                    adb3 = adb[:].rearrange("p (pr two t) -> p pr two t", two=2, t=128)
                    Dw3 = Dw[:].rearrange("p (pr two t) -> p pr two t", two=2, t=128)
                    for half in range(2):
                        v.tensor_tensor(adb3[:, 4 * half:4 * half + 4, 0, :],
                                        PSX[half][:].rearrange("p (s t) -> p s t", t=128),
                                        Dw3[:, 4 * half:4 * half + 4, 0, :], op=ALU.mult)
                        v.tensor_tensor(adb3[:, 4 * half:4 * half + 4, 1, :],
                                        PSR[half][:].rearrange("p (s t) -> p s t", t=128),
                                        Dw3[:, 4 * half:4 * half + 4, 1, :], op=ALU.mult)
                        sp("d_ad")
                    w("p_sd", 4 * g + cc + 1)
                    v.tensor_mul(S_T[:], S_T[:], gC[:])
                    v.tensor_add(S_T[:], S_T[:], ps_sd[:]).then_inc(S["d_su"], 1)

                def dv_sd(g, cc):
                    kb = PKB[cc]
                    w("p_sd", g)
                    w("d_rdy", g + 1)
                    gkp = Gk[:] if cc == 3 else Kfm[:, cc * 1024:(cc + 1) * 1024]
                    v.tensor_mul(kb[:], Klnb[:, cc * D:(cc + 1) * D], gkp)
                    sp("d_qk")

                def dv_sadd(g):
                    w("p_sd", g + 1)
                    for _ in range(4):
                        v.tensor_mul(S_T[:], S_T[:], gC[:])
                    v.tensor_add(S_T[:], S_T[:], ps_sd[:]).then_inc(S["d_su"], 1)

                for g in range(NG):
                    bk = g >= 1
                    if prefix:
                        if bk:
                            dv_sd(g - 1, 0)
                        dv_rw(g)
                        if bk:
                            dv_sd(g - 1, 1)
                            dv_sd(g - 1, 2)
                            dv_sd(g - 1, 3)
                        dv_xn(g)
                        if bk:
                            dv_sadd(g - 1)
                        dv_kln(g)
                    else:
                        if bk and L4:
                            dv_ret(g - 1, 0)
                            dv_ret(g - 1, 1)
                        dv_rw(g)
                        if bk and L4:
                            dv_ret(g - 1, 2)
                        dv_xn(g)
                        if L2:
                            dv_lga(g)
                        if bk and L4:
                            dv_ret(g - 1, 3)
                        if L3:
                            dv_kln(g)
                        if L2:
                            dv_lgb(g)
                            dv_lgc(g)
                if prefix:
                    for cc in range(4):
                        dv_sd(NG - 1, cc)
                    dv_sadd(NG - 1)
                    v.tensor_scalar(S_T[:], S_T[:], flag[:], None, op0=ALU.mult)
                elif L4:
                    for cc in range(4):
                        dv_ret(NG - 1, cc)

            if prefix:
                @b.gpsimd
                def _(gp):
                    for dst, srch in [(wq_sb, wq_h), (wo_sb, wo_h)]:
                        gp.dma_start(dst[:].rearrange("p (c o) -> p c o", c=8),
                                     srch.rearrange("(c p) o -> p c o", p=128)).then_inc(s_ldw, 16)
                    gp.wait_ge(s_ldw, 7 * 16)

            @b.scalar
            def _(s):
                def w(sn, val):
                    val += BASE.get(sn, 0)
                    if val > 0:
                        s.wait_ge(S[sn], val)

                def asp(sname, k=1):
                    s.copy(jnkA[:], iden[:]).then_inc(S[sname], k)

                def ac_rw(g):
                    w("d_rw", g + 1)
                    s.activation(rowb[0:1, VAR:VAR + GT], rowb[0:1, VAR:VAR + GT], AF.Ln,
                                 bias=eps_c[0:1, :])
                    s.copy(jnkA[:], iden[:])
                    s.activation(rowb[0:1, RSTD:RSTD + GT], rowb[0:1, VAR:VAR + GT], AF.Exp,
                                 scale=-0.5)
                    asp("a_rs")

                def ac_krstd(g):
                    w("a_kr", g + 1)
                    s.activation(kst[:, 64:128], kst[:, 64:128], AF.Ln, bias=eps_c[:])
                    s.copy(jnkA[:], iden[:])
                    s.activation(kst[:, 192:256], kst[:, 64:128], AF.Exp, scale=-0.5)
                    asp("d_st2")

                def ac_sq(g):
                    w("sx", 16 * (g + 1))
                    if not (prefix or L3):
                        if L2:
                            w("p_rl", g)
                        w("p_st", g)
                        s.activation(scr[:], xg[:], AF.Square).then_inc(S["a_sq"], 1)
                        return
                    w("p_st", g)
                    for j in range(4):
                        w("d_kl", 4 * (g - 1) + j + 1)
                        ins = s.activation(scr[:, j * D:(j + 1) * D], xg[:, j * D:(j + 1) * D],
                                           AF.Square)
                    ins.then_inc(S["a_sq"], 1)

                def ac_yev(g, cc, r):
                    w("p_y", 8 * g + 2 * cc + r + 1)
                    ins = s.copy(
                        QY[:].rearrange("p (pr t) -> p pr t", t=GT)[:, 4 * r:4 * r + 4, cc * 128:(cc + 1) * 128],
                        ps_y1[:].rearrange("p (s t) -> p s t", t=128))
                    if cc == 3 and r == 1:
                        asp("d_ye")
                    else:
                        ins.then_inc(S["d_ye"], 1)

                def ac_rev(g):
                    rstage = Kfm[:].bitcast(F32)
                    for oc in range(8):
                        w("p_wo", 8 * g + oc + 1)
                        if oc == 4:
                            w("sst", 32 * g + 16)
                        ins = s.copy(rstage[:, (oc % 4) * 512:(oc % 4 + 1) * 512], PSX[oc % 2][:])
                        if (oc + 1) % 4 != 0:
                            ins.then_inc(S["d_r"], 1)
                        else:
                            asp("d_r")

                def ac_e4p16(g):
                    w("r5", g + 1); w("d_dn", 2 * g)
                    for j in range(4):
                        ins = s.activation(e4b[:, j * 4:(j + 1) * 4], m8b[:, j * 8:j * 8 + 4],
                                           AF.Exp, bias=nmx[:, j:j + 1])
                    ins.then_inc(S["a_e4"], 1)
                    w("d_sc", g)
                    for j in range(4):
                        ins = s.activation(p16b[:, j * 16:(j + 1) * 16], ltok[:, j * 16:(j + 1) * 16],
                                           AF.Exp, bias=nmx[:, j:j + 1])
                    ins.then_inc(S["a_p16"], 1)

                def ac_qev(g):
                    w("p_wo", 8 * g)
                    for ob in range(8):
                        w("p_q", 8 * g + ob + 1)
                        ins = s.copy(QY[:, ob * GT:(ob + 1) * GT], PSX[ob % 2][:])
                        if ob < 7:
                            ins.then_inc(S["d_q"], 1)
                        else:
                            asp("d_q")

                def ac_kev(g):
                    for idx in range(8):
                        j, nb = idx // 2, idx % 2
                        w("p_k", 8 * g + idx + 1)
                        ins = s.copy(scr[:, j * D + nb * 512:j * D + (nb + 1) * 512],
                                     PSX[idx % 2][:])
                        if idx < 7:
                            ins.then_inc(S["d_ke"], 1)
                        else:
                            asp("d_ke")

                def ac_ksq(g):
                    for q in range(8):
                        j, hf = q // 2, q % 2
                        w("d_ms", 8 * g + q - 1)
                        s.activation(ps_bc[:, (q % 2) * 512:(q % 2) * 512 + 512],
                                     scr[:, j * D + hf * 512:j * D + hf * 512 + 512],
                                     AF.Square).then_inc(S["a_kq"], 1)

                def ac_vev(g):
                    if prefix:
                        w("p_sd", g)
                    elif L4:
                        w("p_sd", 4 * g)
                    if L4:
                        w("p_y", 8 * g)
                    for idx in range(8):
                        j, nb = idx // 2, idx % 2
                        w("p_v", 8 * g + idx + 1)
                        ins = s.copy(Vb[:, j * D + nb * 512:j * D + (nb + 1) * 512],
                                     PSR[idx % 2][:])
                        if idx < 7:
                            ins.then_inc(S["d_ve"], 1)
                        else:
                            asp("d_ve")

                def ac_scev(g):
                    w("p_sT", g + 1)
                    s.copy(scoresT[:, g * GT:(g + 1) * GT], ps_x1[0:16, 0:512]).then_inc(S["d_sE"], 1)

                def ac_ktev(g):
                    if L4:
                        w("p_A", 4 * g)
                    if L5:
                        w("sst", 32 * g)
                    for i in range(8):
                        sub, half = i // 2, i % 2
                        w("p_kt", 8 * g + i + 1)
                        s.copy(
                            Kfm[:].rearrange("p (pr t) -> p pr t", pr=8)
                                  [:, half * 4:(half + 1) * 4, sub * 128:(sub + 1) * 128],
                            PSR[i % 2][:].bitcast(BF16)[:, 0:512].rearrange("p (tt t) -> p tt t", tt=4),
                        ).then_inc(S["d_kf"], 1)

                for g in range(NG):
                    ac_sq(g)
                    if prefix:
                        ac_rw(g)
                    if g >= 1:
                        if L4:
                            for cc in range(4):
                                for r in range(2):
                                    ac_yev(g - 1, cc, r)
                        if L5:
                            ac_rev(g - 1)
                    if L3:
                        ac_qev(g)
                    if prefix or L3:
                        ac_kev(g)
                        ac_vev(g)
                        ac_ksq(g)
                        ac_krstd(g)
                    if L2:
                        ac_e4p16(g)
                        ac_scev(g)
                    if L3:
                        ac_ktev(g)
                    if prefix or L3:
                        asp("d_rdy")
                if L4:
                    for cc in range(4):
                        for r in range(2):
                            ac_yev(NG - 1, cc, r)
                if L5:
                    ac_rev(NG - 1)

        return S

    attn_phase("P1", xpf, prefix=True)
    attn_phase("P2", xf, prefix=False)

    esA.close()

    esB = ExitStack()
    sbB = lambda name, shape, dt: esB.enter_context(nc.sbuf_tensor("sc_" + name, shape, dt))
    acc = sbB("acc", [128, 8 * T], F32)
    gub_sb = sbB("gub_sb", [128, 256], F32)
    dnb_sb = sbB("dnb_sb", [16, D], F32)
    esel_sb = sbB("esel_sb", [16, NE * 128], F32)
    act = sbB("act", [128, 8 * T], BF16)
    wgu_sb = [sbB(f"wgu{i}", [128, 8 * 256], BF16) for i in range(3)]
    wd_sb = [sbB(f"wd{i}", [128, 8 * 128], BF16) for i in range(2)]
    sco_bc = sbB("sco_bc", [128, T], BF16)
    g1b = [sbB(f"g1b{i}", [128, 1024], F32) for i in range(2)]
    sigb = [sbB(f"sigb{i}", [128, 1024], F32) for i in range(2)]
    u2b = [sbB(f"u2b{i}", [128, 1024], F32) for i in range(2)]
    glub = [sbB("glub0", [128, 1024], F32)] * 2

    esB2 = ExitStack()
    ps_a = esB2.enter_context(nc.psum_tensor("ps_a", [128, 1024], F32))
    ps_b = esB2.enter_context(nc.psum_tensor("ps_b", [128, 1024], F32))
    ps_c = esB2.enter_context(nc.psum_tensor("ps_c", [128, 1024], F32))
    ps_d = esB2.enter_context(nc.psum_tensor("ps_d", [128, 1024], F32))
    PGT = [ps_a, ps_c]
    PUP = [ps_b, ps_d]
    PDN = [ps_b, ps_d]
    PSC = [ps_a, ps_c]

    s_r3 = sem("s_r3")
    s_wg = sem("s_wg")
    s_wd = sem("s_wd")
    p_bias = sem("p_bias")
    d_bias = sem("d_bias")
    p_sc3 = sem("p_sc3")
    d_sb3 = sem("d_sb3")
    p_gu = sem("p_gu")
    d_gu = sem("d_gu")
    d_g1 = sem("d_g1")
    a_sg3 = sem("a_sg3")
    d_act = sem("d_act")
    p_dn3 = sem("p_dn3")
    d_dn3 = sem("d_dn3")

    with nc.Block("P3") as b:
        @b.sync
        def _(sync):
            sync.dma_start(dnb_sb[:], dnb_h).then_inc(s_r3, 16)
            sync.dma_start(esel_sb[:], esel_h).then_inc(s_r3, 16)
            sync.dma_start(gub_sb[:], gub_h).then_inc(s_r3, 16)
            for c_ in range(8):
                sync.dma_start(
                    acc[:, c_ * T:(c_ + 1) * T].rearrange("p (gg t) -> p gg t", gg=4),
                    rscr.rearrange("p (gg c t) -> p gg c t", gg=4, c=8)[:, :, c_, :],
                ).then_inc(s_r3, 16)

        @b.gpsimd
        def _(g):
            for e in range(NE):
                for j in range(8):
                    idx = e * 8 + j
                    if idx >= 3:
                        g.wait_ge(p_gu, 4 * idx - 8)
                    g.dma_start(wgu_sb[idx % 3][:].rearrange("p (c f) -> p c f", c=8),
                                wgu_h[e, j].rearrange("(c p) f -> p c f", p=128)).then_inc(s_wg, 16)
                for hb in range(8):
                    idx = e * 8 + hb
                    if idx >= 2:
                        g.wait_ge(p_dn3, 4 * (idx - 1))
                    g.dma_start(wd_sb[idx % 2][:].rearrange("p (c f) -> p c f", c=8),
                                wd_h[e, hb].rearrange("(c p) f -> p c f", p=128)).then_inc(s_wd, 16)

        @b.tensor
        def _(t):
            t.wait_ge(s_r3, 16)
            for ds in range(32):
                hb, st = ds // 4, ds % 4
                if ds >= 2:
                    t.wait_ge(d_bias, ds - 1)
                t.matmul(PDN[ds % 2][:, 0:512], dnb_sb[:, hb * 128:(hb + 1) * 128],
                         scoresT[:, st * 512:(st + 1) * 512], start=True, stop=True).then_inc(p_bias, 1)
            def score_bc(e):
                for st in range(4):
                    ins = t.matmul(PSC[st // 2][:, (st % 2) * 512:(st % 2 + 1) * 512],
                                   esel_sb[:, e * 128:(e + 1) * 128],
                                   scoresT[:, st * 512:(st + 1) * 512], start=True, stop=True)
                ins.then_inc(p_sc3, 1)

            for e in range(NE):
                if e == 0:
                    t.wait_ge(s_r3, 32)
                    score_bc(0)
                    t.wait_ge(d_bias, 32)
                for u in range(16):
                    j, hf = u // 2, u % 2
                    U = 16 * e + u
                    t.wait_ge(s_wg, 16 * (e * 8 + j + 1))
                    if U >= 2:
                        t.wait_ge(d_gu, U - 1)
                    if u <= 1:
                        t.wait_ge(d_sb3, e + 1)
                        if e == 0:
                            t.wait_ge(d_bias, 32)
                        else:
                            t.wait_ge(d_dn3, 32 * e)
                    for fc in range(8):
                        for s2 in range(2):
                            t.matmul(PGT[U % 2][:, s2 * 512:(s2 + 1) * 512],
                                     wgu_sb[(e * 8 + j) % 3][:, fc * 256:fc * 256 + 128],
                                     xn_bf[:, fc * T + hf * 1024 + s2 * 512:fc * T + hf * 1024 + (s2 + 1) * 512],
                                     start=(fc == 0), stop=(fc == 7))
                    for fc in range(8):
                        for s2 in range(2):
                            ins = t.matmul(PUP[U % 2][:, s2 * 512:(s2 + 1) * 512],
                                           wgu_sb[(e * 8 + j) % 3][:, fc * 256 + 128:fc * 256 + 256],
                                           xn_bf[:, fc * T + hf * 1024 + s2 * 512:fc * T + hf * 1024 + (s2 + 1) * 512],
                                           start=(fc == 0), stop=(fc == 7))
                    ins.then_inc(p_gu, 2)
                t.wait_ge(d_gu, 16 * (e + 1))
                if e + 1 < NE:
                    score_bc(e + 1)
                t.wait_ge(d_act, 16 * (e + 1))
                for ds in range(32):
                    hb, st = ds // 4, ds % 4
                    t.wait_ge(s_wd, 16 * (e * 8 + hb + 1))
                    gds = 32 * e + ds
                    if gds >= 2:
                        t.wait_ge(d_dn3, gds - 1)
                    for fb in range(8):
                        ins = t.matmul(PDN[ds % 2][:, 0:512],
                                       wd_sb[(e * 8 + hb) % 2][:, fb * 128:(fb + 1) * 128],
                                       act[:, fb * T + st * 512:fb * T + (st + 1) * 512],
                                       start=(fb == 0), stop=(fb == 7))
                    ins.then_inc(p_dn3, 1)

        @b.vector
        def _(v):
            for ds in range(32):
                hb, st = ds // 4, ds % 4
                v.wait_ge(s_r3, 48 + 16 * (hb + 1))
                v.wait_ge(p_bias, ds + 1)
                v.tensor_add(acc[:, hb * T + st * 512:hb * T + (st + 1) * 512],
                             acc[:, hb * T + st * 512:hb * T + (st + 1) * 512],
                             PDN[ds % 2][:, 0:512]).then_inc(d_bias, 1)
            for e in range(NE):
                def frontend(u):
                    U = 16 * e + u
                    j, hf = u // 2, u % 2
                    v.wait_ge(p_gu, 2 * U + 2)
                    if U == 0:
                        v.wait_ge(s_r3, 48)
                    if U >= 1:
                        v.wait_ge(a_sg3, U - 1)
                    v.tensor_scalar(g1b[u % 2][:], PGT[U % 2][:],
                                    gub_sb[:, e * 16 + 2 * j:e * 16 + 2 * j + 1], 7.0,
                                    op0=ALU.add, op1=ALU.min)
                    v.tensor_scalar(u2b[u % 2][:], PUP[U % 2][:],
                                    gub_sb[:, e * 16 + 2 * j + 1:e * 16 + 2 * j + 2], -7.0,
                                    op0=ALU.add, op1=ALU.max).then_inc(d_g1, 1)
                    ins = v.tensor_scalar(u2b[u % 2][:], u2b[u % 2][:], 7.0, None, op0=ALU.min)
                    ins.then_inc(d_gu, 1)

                def backend(u):
                    U = 16 * e + u
                    j, hf = u // 2, u % 2
                    v.wait_ge(a_sg3, U + 1)
                    if u == 0:
                        v.wait_ge(p_dn3, 32 * e)
                        v.wait_ge(d_sb3, e + 1)
                    v.tensor_mul(glub[u % 2][:], g1b[u % 2][:], sigb[u % 2][:])
                    v.scalar_tensor_tensor(act[:, j * T + hf * 1024:j * T + (hf + 1) * 1024],
                                           u2b[u % 2][:], 1.0, glub[u % 2][:],
                                           op0=ALU.add, op1=ALU.mult)
                    v.tensor_mul(act[:, j * T + hf * 1024:j * T + (hf + 1) * 1024],
                                 act[:, j * T + hf * 1024:j * T + (hf + 1) * 1024],
                                 sco_bc[:, hf * 1024:(hf + 1) * 1024]).then_inc(d_act, 1)

                frontend(0)
                for u in range(1, 16):
                    frontend(u)
                    backend(u - 1)
                backend(15)
                for ds in range(32):
                    hb, st = ds // 4, ds % 4
                    v.wait_ge(p_dn3, 32 * e + ds + 1)
                    v.tensor_add(acc[:, hb * T + st * 512:hb * T + (st + 1) * 512],
                                 acc[:, hb * T + st * 512:hb * T + (st + 1) * 512],
                                 PDN[ds % 2][:, 0:512]).then_inc(d_dn3, 1)

        @b.scalar
        def _(s):
            for e in range(NE):
                s.wait_ge(p_sc3, e + 1)
                if e >= 1:
                    s.wait_ge(d_act, 16 * e)
                s.copy(sco_bc[:, 0:1024], PSC[0][:])
                s.copy(sco_bc[:, 1024:2048], PSC[1][:]).then_inc(d_sb3, 1)
                for u in range(16):
                    U = 16 * e + u
                    s.wait_ge(d_g1, U + 1)
                    if U >= 1:
                        s.wait_ge(d_act, U - 1)
                    s.activation(sigb[u % 2][:], g1b[u % 2][:], AF.Sigmoid,
                                 scale=1.702).then_inc(a_sg3, 1)

    esB2.close()
    ps_t = [esB.enter_context(nc.psum_tensor(f"ps_t{i}", [128, 1024], F32)) for i in range(2)]
    outb = g1b
    p_mg = sem("p_mg")
    d_mg = sem("d_mg")
    s_out = sem("s_out")

    s_xp = sem("s_xp")
    xb = sigb

    with nc.Block("P4") as b:
        @b.tensor
        def _(t):
            for cc in range(16):
                if cc >= 2:
                    t.wait_ge(d_mg, cc - 1)
                for fc in range(8):
                    ins = t.transpose(ps_t[cc % 2][:, fc * 128:(fc + 1) * 128],
                                      acc[:, fc * T + cc * 128:fc * T + (cc + 1) * 128], iden[:])
                ins.then_inc(p_mg, 1)

        @b.vector
        def _(v):
            for cc in range(16):
                v.wait_ge(p_mg, cc + 1)
                v.wait_ge(s_xp, 16 * (cc + 1))
                if cc >= 2:
                    v.wait_ge(s_out, 16 * (cc - 1))
                v.tensor_add(outb[cc % 2][:], xb[cc % 2][:], ps_t[cc % 2][:]).then_inc(d_mg, 1)

        @b.sync
        def _(sync):
            for cc in range(18):
                if cc < 16:
                    if cc >= 2:
                        sync.wait_ge(d_mg, cc - 1)
                    sync.dma_start(xb[cc % 2][:], xtok_h[cc * 128:(cc + 1) * 128, :]).then_inc(s_xp, 16)
                if cc >= 2:
                    sync.wait_ge(d_mg, cc - 1)
                    sync.dma_start(y_h[(cc - 2) * 128:(cc - 1) * 128, :],
                                   outb[cc % 2][:]).then_inc(s_out, 16)
            sync.wait_ge(s_out, 16 * 16)

    esB.close()
    es.close()
    return nc



_NC = None


def _get_nc():
    global _NC
    if _NC is None:
        _NC = build_nc()
    return _NC


def _host_consts():
    i = np.arange(128)
    c = {}
    c["iden"] = np.eye(128, dtype=np.float32)
    c["dclamp"] = np.maximum(i[None, :] - i[:, None], 0).astype(np.float32)
    c["mask01"] = (i[None, :] >= i[:, None]).astype(np.float32)
    c["tposb"] = np.broadcast_to((i + 1)[None, :], (128, 128)).astype(np.float32).copy()
    c["jrev"] = (127 - i)[:, None].astype(np.float32)
    m64 = np.zeros((128, 2), np.float32)
    m64[:64, 0] = 1.0
    m64[64:, 1] = 1.0
    c["m64"] = m64
    esel = np.zeros((16, NE * 128), np.float32)
    for e in range(NE):
        esel[e, e * 128:(e + 1) * 128] = 1.0
    c["esel"] = esel
    return c


def kernel(L0_attn_x, L0_attn_W_qkv, L0_attn_W_o, L0_attn_log_gamma,
           L0_ffn_router_w, L0_ffn_router_b, L0_ffn_gate_up_proj, L0_ffn_gate_up_bias,
           L0_ffn_down_proj, L0_ffn_down_bias):
    x = np.asarray(L0_attn_x, np.float32)
    Wqkv = np.asarray(L0_attn_W_qkv, np.float32)
    Wo = np.asarray(L0_attn_W_o, np.float32)
    lg = np.asarray(L0_attn_log_gamma, np.float32)
    rw = np.asarray(L0_ffn_router_w, np.float32)
    rb = np.asarray(L0_ffn_router_b, np.float32)
    gup = np.asarray(L0_ffn_gate_up_proj, np.float32)
    gub = np.asarray(L0_ffn_gate_up_bias, np.float32)
    wdp = np.asarray(L0_ffn_down_proj, np.float32)
    dnb = np.asarray(L0_ffn_down_bias, np.float32)

    consts = _host_consts()
    wq = np.ascontiguousarray(Wqkv[0:1024].T)
    wk = np.ascontiguousarray(Wqkv[1024:2048].T)
    wv = np.ascontiguousarray(Wqkv[2048:3072].T)
    wo = np.ascontiguousarray(Wo.T)

    gate_w = gup[:, :, 0::2]
    up_w = gup[:, :, 1::2]
    wgu = np.empty((NE, 8, D, 256), np.float32)
    for j in range(8):
        wgu[:, j, :, 0:128] = gate_w[:, :, j * 128:(j + 1) * 128]
        wgu[:, j, :, 128:256] = up_w[:, :, j * 128:(j + 1) * 128]
    wd = np.ascontiguousarray(wdp.reshape(NE, D, 8, 128).transpose(0, 2, 1, 3))

    gate_b = gub[:, 0::2]
    up_b = gub[:, 1::2]
    gub_cols = np.empty((128, 256), np.float32)
    for e in range(NE):
        for j in range(8):
            gub_cols[:, e * 16 + 2 * j] = gate_b[e, j * 128:(j + 1) * 128]
            gub_cols[:, e * 16 + 2 * j + 1] = up_b[e, j * 128:(j + 1) * 128]

    nc = _get_nc()
    in_maps = []
    for c in range(8):
        bb, hh = c // 2, c % 2
        sl = x[bb, 2048 * hh:2048 * (hh + 1)]
        xfc = np.ascontiguousarray(sl.T)
        if hh == 1:
            xpfc = np.ascontiguousarray(x[bb, 0:2048].T)
            flagc = np.ones((128, 1), np.float32)
        else:
            xpfc = xfc
            flagc = np.zeros((128, 1), np.float32)
        m = dict(xf=xfc, xpf=xpfc, xtok=np.ascontiguousarray(sl), wq=wq, wk=wk, wv=wv, wo=wo,
                 rw=rw, rb=rb.reshape(16, 1), wgu=wgu, wd=wd,
                 gub=gub_cols, dnb=dnb, esel=consts["esel"],
                 iden=consts["iden"], dclamp=consts["dclamp"], mask01=consts["mask01"],
                 tposb=consts["tposb"], jrev=consts["jrev"], m64=consts["m64"],
                 flag=flagc, lg=lg.reshape(1, 16))
        in_maps.append(m)

    res = run_bass_kernel_spmd(nc, in_maps, core_ids=list(range(8)))
    out = np.empty((4, 4096, 1024), np.float32)
    for c in range(8):
        bb, hh = c // 2, c % 2
        out[bb, 2048 * hh:2048 * (hh + 1)] = res.results[c]["y"]
    return out


# version 16
# speedup vs baseline: 1.0377x; 1.0148x over previous
import numpy as np
from contextlib import ExitStack

import concourse.bass as bass
import concourse.mybir as mybir
from concourse.bass_utils import run_bass_kernel_spmd

F32 = mybir.dt.float32
I32 = mybir.dt.int32
BF16 = mybir.dt.bfloat16
AF = mybir.ActivationFunctionType
ALU = mybir.AluOpType
AX = mybir.AxisListType.X

D = 1024
T = 2048
NG = 4
GT = 512
NH = 16
NE = 16
EPS = 1e-5
NEG = -1e30


def build_nc():
    nc = bass.Bass("TRN2", target_bir_lowering=False)

    xf = nc.dram_tensor("xf", [D, T], F32, kind="ExternalInput").ap()
    xpf = nc.dram_tensor("xpf", [D, T], F32, kind="ExternalInput").ap()
    wq_h = nc.dram_tensor("wq", [D, D], F32, kind="ExternalInput").ap()
    wk_h = nc.dram_tensor("wk", [D, D], F32, kind="ExternalInput").ap()
    wv_h = nc.dram_tensor("wv", [D, D], F32, kind="ExternalInput").ap()
    wo_h = nc.dram_tensor("wo", [D, D], F32, kind="ExternalInput").ap()
    rw_h = nc.dram_tensor("rw", [D, 16], F32, kind="ExternalInput").ap()
    rb_h = nc.dram_tensor("rb", [16, 1], F32, kind="ExternalInput").ap()
    wgu_h = nc.dram_tensor("wgu", [NE, 8, D, 256], F32, kind="ExternalInput").ap()
    wd_h = nc.dram_tensor("wd", [NE, 8, D, 128], F32, kind="ExternalInput").ap()
    gub_h = nc.dram_tensor("gub", [128, 256], F32, kind="ExternalInput").ap()
    dnb_h = nc.dram_tensor("dnb", [16, D], F32, kind="ExternalInput").ap()
    esel_h = nc.dram_tensor("esel", [16, NE * 128], F32, kind="ExternalInput").ap()
    iden_h = nc.dram_tensor("iden", [128, 128], F32, kind="ExternalInput").ap()
    dclamp_h = nc.dram_tensor("dclamp", [128, 128], F32, kind="ExternalInput").ap()
    mask01_h = nc.dram_tensor("mask01", [128, 128], F32, kind="ExternalInput").ap()
    tposb_h = nc.dram_tensor("tposb", [128, 128], F32, kind="ExternalInput").ap()
    jrev_h = nc.dram_tensor("jrev", [128, 1], F32, kind="ExternalInput").ap()
    m64_h = nc.dram_tensor("m64", [128, 2], F32, kind="ExternalInput").ap()
    flag_h = nc.dram_tensor("flag", [128, 1], F32, kind="ExternalInput").ap()
    lg_h = nc.dram_tensor("lg", [1, 16], F32, kind="ExternalInput").ap()

    y_h = nc.dram_tensor("y", [T, D], F32, kind="ExternalOutput").ap()
    xtok_h = nc.dram_tensor("xtok", [T, D], F32, kind="ExternalInput").ap()
    rscr = nc.dram_tensor("rscr", [128, 8 * T], F32, kind="Internal").ap()

    es = ExitStack()
    sb = lambda name, shape, dt: es.enter_context(nc.sbuf_tensor("sb_" + name, shape, dt))
    sem = lambda name: es.enter_context(nc.semaphore(name))

    xn_bf = sb("xn_bf", [128, 8 * T], BF16)
    scoresT = sb("scoresT", [16, T], F32)
    S_T = sb("S_T", [128, 512], F32)
    iden = sb("iden", [128, 128], F32)
    ones_r = sb("ones_r", [1, 128], F32)
    ones_c = sb("ones_c", [128, 1], F32)
    onesw = sb("onesw", [128, 64], F32)
    eps_c = sb("eps_c", [128, 1], F32)
    jnk1 = sb("jnk1", [128, 512], F32)
    jnkA = sb("jnkA", [128, 128], F32)
    flag = sb("flag", [128, 1], F32)
    rb_sb = sb("rb_sb", [16, 1], F32)
    rw_sb = sb("rw_sb", [128, 8 * 16], F32)

    esA = ExitStack()
    sbA = lambda name, shape, dt: esA.enter_context(nc.sbuf_tensor("sa_" + name, shape, dt))
    wq_sb = sbA("wq_sb", [128, 8 * D], BF16)
    wk_sb = sbA("wk_sb", [128, 8 * D], BF16)
    wv_sb = sbA("wv_sb", [128, 8 * D], BF16)
    wo_sb = sbA("wo_sb", [128, 8 * D], BF16)
    xg = sbA("xg", [128, 8 * GT], F32)
    scr = sbA("scr", [128, 8 * GT], F32)
    QY = sbA("QY", [128, 8 * GT], BF16)
    Vb = sbA("Vb", [128, NG * D], BF16)
    Kfm = sbA("Kfm", [128, 8 * GT], BF16)
    qg = sbA("qg", [128, 8 * 128], BF16)
    kg = sbA("kg", [128, D], BF16)
    Klnb = sbA("Klnb", [128, NG * D], BF16)
    iden_bf = sbA("iden_bf", [128, 128], BF16)
    adb = sbA("adb", [128, 2048], BF16)
    S_bf = sbA("S_bf", [128, 512], BF16)
    Dw = sbA("Dw", [128, NH * 128], BF16)
    Gq = sbA("Gq", [128, 8 * 128], BF16)
    Gk = sbA("Gk", [128, D], BF16)
    gC = sbA("gC", [128, 512], F32)
    rowb = sbA("rowb", [1, 3 * GT], F32)
    ltok = sbA("ltok", [128, 64], F32)
    lz = sbA("lz", [128, 64], F32)
    m8b = sbA("m8b", [128, 32], F32)
    nmx = sbA("nmx", [128, 4], F32)
    e4b = sbA("e4b", [128, 16], F32)
    p16b = sbA("p16b", [128, 64], F32)
    selb = sbA("selb", [128, 64], F32)
    scob = sbA("scob", [128, 64], F32)
    dnm = sbA("dnm", [128, 8], F32)
    kst = sbA("kst", [128, 256], F32)
    lraw = sbA("lraw", [16, GT], F32)
    esP = ExitStack()
    sbP = lambda name, shape, dt: esP.enter_context(nc.sbuf_tensor("sp_" + name, shape, dt))
    ptmp = sbP("ptmp", [128, 128], F32)
    grow = sbP("grow", [1, 16], F32)
    lnrow = sbP("lnrow", [1, 16], F32)
    lnbc = sbP("lnbc", [128, 16], F32)
    lnpair = sbP("lnpair", [128, 8], F32)
    gcol = sbP("gcol", [128, 1], F32)
    gCcol = sbP("gCcol", [128, 16], F32)
    gCrow = sbP("gCrow", [1, 16], F32)
    m64 = sbP("m64", [128, 2], F32)
    jrev = sbP("jrev", [128, 1], F32)
    dclamp = sbP("dclamp", [128, 128], BF16)
    mask01 = sbP("mask01", [128, 128], BF16)
    tposb = sbP("tposb", [128, 128], BF16)
    lgt = sbP("lgt", [1, 16], F32)

    ps_bc = esA.enter_context(nc.psum_tensor("ps_bc", [128, 1024], F32))
    ps_x0 = esA.enter_context(nc.psum_tensor("ps_x0", [128, 512], F32))
    ps_x1 = esA.enter_context(nc.psum_tensor("ps_x1", [128, 512], F32))
    ps_r0 = esA.enter_context(nc.psum_tensor("ps_r0", [128, 512], F32))
    ps_r1 = esA.enter_context(nc.psum_tensor("ps_r1", [128, 512], F32))
    ps_sd = esA.enter_context(nc.psum_tensor("ps_sd", [128, 512], F32))
    ps_y1 = esA.enter_context(nc.psum_tensor("ps_y1", [128, 512], F32))
    PSX = [ps_x0, ps_x1]
    PSR = [ps_r0, ps_r1]

    s_ld = sem("s_ld")
    s_ldw = sem("s_ldw")
    pg = sem("pg")
    p0s = sem("p0s")
    vself = sem("vself")
    vself_n = [0]

    def vg(v, ins):
        vself_n[0] += 1
        ins.then_inc(vself, 1)
        v.wait_ge(vself, vself_n[0])

    def emit_preamble(b):
        @b.sync
        def _(sync):
            n = 0
            for dst, src in [(iden, iden_h), (jrev, jrev_h), (m64, m64_h),
                             (flag, flag_h), (lgt, lg_h), (rb_sb, rb_h),
                             (rw_sb, rw_h)]:
                if dst is rw_sb:
                    sync.dma_start(rw_sb[:].rearrange("p (c o) -> p c o", c=8),
                                   rw_h.rearrange("(c p) o -> p c o", p=128)).then_inc(s_ld, 16)
                else:
                    sync.dma_start(dst[:], src).then_inc(s_ld, 16)
                n += 1

        @b.gpsimd
        def _(g):
            for dst, src in [(dclamp, dclamp_h), (mask01, mask01_h), (tposb, tposb_h)]:
                g.dma_start(dst[:].rearrange("p (a b) -> p a b", a=2),
                            src.rearrange("p (a b) -> p a b", a=2)).then_inc(s_ldw, 16)
            for dst, src in [(wk_sb, wk_h), (wv_sb, wv_h)]:
                g.dma_start(dst[:].rearrange("p (c o) -> p c o", c=8),
                            src.rearrange("(c p) o -> p c o", p=128)).then_inc(s_ldw, 16)

        @b.vector
        def _(v):
            v.wait_ge(s_ld, 7 * 16)
            v.wait_ge(s_ldw, 3 * 16)
            for t_, val in [(ones_r, 1.0), (ones_c, 1.0), (onesw, 1.0),
                            (eps_c, EPS), (jnk1, 0.0)]:
                v.memset(t_[:], val)
            v.tensor_copy(iden_bf[:], iden[:])
            v.memset(S_T[:], 0.0).then_inc(pg, 1)

            def vsp(semh, k=1):
                v.tensor_copy(jnk1[:], jnk1[:]).then_inc(semh, k)
            v.wait_ge(pg, 3)
            v.tensor_copy(lnbc[:], ps_bc[:, 0:16])
            vsp(p0s)
            v.wait_ge(p0s, 2)
            lnbc3 = lnbc[:].rearrange("p (a b) -> p a b", b=2)
            vg(v, v.tensor_scalar(lnpair[:], lnbc3[:, :, 0], m64[:, 0:1], None, op0=ALU.mult))
            vg(v, v.tensor_scalar(ptmp[:, 0:8], lnbc3[:, :, 1], m64[:, 1:2], None, op0=ALU.mult))
            vg(v, v.tensor_add(lnpair[:], lnpair[:], ptmp[:, 0:8]))
            vsp(p0s)
            v.wait_ge(p0s, 3)
            v.tensor_scalar(gCrow[:], lnrow[:], 128.0, None, op0=ALU.mult)
            vsp(pg)
            v.wait_ge(pg, 6)
            v.tensor_copy(gCcol[:], ps_bc[:, 16:32])
            vsp(p0s)
            v.wait_ge(p0s, 4)
            for h in range(NH):
                pb = 64 * (h % 2)
                v.tensor_scalar(gC[pb:pb + 64, (h // 2) * 64:(h // 2 + 1) * 64],
                                onesw[pb:pb + 64, :], gCcol[pb:pb + 64, h:h + 1], None,
                                op0=ALU.mult)
            v.nop().then_inc(pg, 1)
            TMP = xn_bf[:].bitcast(F32)
            for h in range(NH):
                v.tensor_scalar(TMP[:, h * 128:(h + 1) * 128], dclamp[:], lnbc[:, h:h + 1],
                                None, op0=ALU.mult)
            for p in range(8):
                v.tensor_scalar(TMP[:, 2048 + p * 128:2048 + (p + 1) * 128], tposb[:],
                                lnpair[:, p:p + 1], None, op0=ALU.mult)
            v.tensor_scalar(TMP[:, 3072:3088], lnbc[:], jrev[:], None, op0=ALU.mult)
            vsp(pg)
            v.wait_ge(pg, 9)
            for h in range(NH):
                v.tensor_mul(Dw[:, h * 128:(h + 1) * 128], TMP[:, h * 128:(h + 1) * 128],
                             mask01[:])
            v.tensor_copy(Gq[:], TMP[:, 2048:3072])
            for h in range(NH):
                v.tensor_scalar(Gk[:, h * 64:(h + 1) * 64], onesw[:], TMP[:, 3072 + h:3073 + h],
                                None, op0=ALU.mult)
            vg(v, v.tensor_mul(ptmp[:, 0:16], gCcol[:], gCcol[:]))
            vg(v, v.tensor_mul(ptmp[:, 16:32], ptmp[:, 0:16], gCcol[:]))
            Gk3 = Gk[:].rearrange("p (h d) -> p h d", d=64)
            for cc, fac in ((0, ptmp[:, 16:32]), (1, ptmp[:, 0:16]), (2, gCcol[:])):
                v.tensor_tensor(Kfm[:, cc * 1024:(cc + 1) * 1024].rearrange("p (h d) -> p h d", d=64),
                                Gk3, fac.unsqueeze(2).to_broadcast([128, NH, 64]), op=ALU.mult)

        @b.scalar
        def _(s):
            s.wait_ge(s_ld, 7 * 16)
            def asp(semh, k=1):
                s.copy(jnkA[:], iden[:]).then_inc(semh, k)

            s.activation(grow[:], lgt[:], AF.Sigmoid)
            asp(p0s)
            s.wait_ge(p0s, 1)
            s.activation(lnrow[:], grow[:], AF.Ln)
            asp(pg)
            s.wait_ge(pg, 4)
            s.activation(gCrow[:], gCrow[:], AF.Exp)
            asp(pg)
            TMPs = xn_bf[:].bitcast(F32)
            s.wait_ge(pg, 8)
            s.activation(TMPs[:, 0:2048], TMPs[:, 0:2048], AF.Exp)
            s.activation(TMPs[:, 2048:3088], TMPs[:, 2048:3088], AF.Exp)
            asp(pg)

        @b.tensor
        def _(t):
            t.wait_ge(pg, 2)
            t.matmul(ps_bc[:, 0:16], ones_r[:], lnrow[:], start=True, stop=True).then_inc(pg, 1)
            t.wait_ge(pg, 5)
            t.matmul(ps_bc[:, 16:32], ones_r[:], gCrow[:], start=True, stop=True).then_inc(pg, 1)

    ATTN_SEM_NAMES = ("sx sst a_sq p_st d_rw a_rs p_bc d_xn p_rl d_lg p_lt r1 r2 r3 r4 r5 "
                      "a_e4 d_dn a_p16 d_sc p_sT d_sE p_q d_q p_k d_ke a_kq d_ms d_st2 a_kr "
                      "d_kl p_kt d_kf p_v d_ve d_qk d_sb p_A d_ad p_y d_ye p_sd d_su p_wo d_r a_l1 a_l2 d_rdy")
    ATTN_SEMS = {n: sem("at_" + n) for n in ATTN_SEM_NAMES.split()}
    PREFIX_TOTALS = dict(sx=16 * NG, a_sq=NG, p_st=NG, d_rw=NG, a_rs=NG, p_bc=NG, d_xn=NG,
                         p_k=8 * NG, d_ke=8 * NG, a_kq=8 * NG, d_ms=8 * NG, d_st2=NG,
                         a_kr=NG, d_kl=4 * NG, p_v=8 * NG, d_ve=8 * NG, d_qk=4 * NG,
                         p_sd=NG, d_su=NG, a_l1=NG, a_l2=NG, d_rdy=NG)

    def attn_phase(b, xsrc, prefix, lvl=5):
        S = ATTN_SEMS
        BASE = {} if prefix else PREFIX_TOTALS
        L2 = (not prefix) and lvl >= 2
        L3 = (not prefix) and lvl >= 3
        L4 = (not prefix) and lvl >= 4
        L5 = (not prefix) and lvl >= 5
        xnT = QY if prefix else xn_bf
        PKB = [kg[:], qg[:], adb[:, 0:1024], adb[:, 1024:2048]]
        NMU, VAR, RSTD = 0, GT, 2 * GT

        def xn_col(fc, g):
            return fc * (GT if prefix else T) + (0 if prefix else g * GT)

        if True:
            @b.sync
            def _(sync):
                def w(s, v):
                    v += BASE.get(s, 0)
                    if v > 0:
                        sync.wait_ge(S[s], v)

                def xload(g):
                    if g > 0:
                        w("d_xn", g)
                        if L5:
                            w("p_st", g)
                    elif not prefix:
                        w("d_xn", 0)
                    sync.dma_start(
                        xg[:].rearrange("p (c t) -> p c t", c=8),
                        xsrc[:, g * GT:(g + 1) * GT].rearrange("(c p) t -> p c t", p=128),
                    ).then_inc(S["sx"], 16)

                def rstore(g):
                    rstage = Kfm[:].bitcast(F32)
                    for hh in range(2):
                        w("d_r", 8 * g + 4 * (hh + 1))
                        sync.dma_start(rscr[:, g * 4096 + hh * 2048:g * 4096 + (hh + 1) * 2048],
                                       rstage[:, 0:2048]).then_inc(S["sst"], 16)

                for g in range(NG):
                    xload(g)
                    if L5 and g >= 1:
                        rstore(g - 1)
                if L5:
                    rstore(NG - 1)
                    w("sst", 32 * NG)

            @b.tensor
            def _(t):
                def w(s, v):
                    v += BASE.get(s, 0)
                    if v > 0:
                        t.wait_ge(S[s], v)

                def st_xsum(g):
                    if prefix and g == 0:
                        t.wait_ge(p0s, 4)
                    w("sx", 16 * (g + 1)); w("d_ms", 8 * g)
                    if g == 0:
                        w("d_xn", 0)
                        if L2:
                            w("d_lg", 0)
                    for fc in range(8):
                        t.matmul(ps_bc[0:1, 0:512], ones_c[:], xg[:, fc * GT:(fc + 1) * GT],
                                 start=(fc == 0), stop=(fc == 7))

                def st_stats(g):
                    w("a_sq", g + 1); w("d_ms", 8 * g); w("d_xn", g)
                    for fc in range(8):
                        ins = t.matmul(ps_bc[0:1, 512:1024], ones_c[:], scr[:, fc * GT:(fc + 1) * GT],
                                       start=(fc == 0), stop=(fc == 7))
                    ins.then_inc(S["p_st"], 1)

                def st_bcast(g):
                    w("a_rs", g + 1)
                    t.matmul(ps_bc[:, 0:512], ones_r[:], rowb[0:1, 0:GT], start=True, stop=True)
                    t.matmul(ps_bc[:, 512:1024], ones_r[:], rowb[0:1, 2 * GT:3 * GT],
                             start=True, stop=True).then_inc(S["p_bc"], 1)

                def st_router(g):
                    w("d_xn", g + 1); w("d_lg", g)
                    if L4:
                        w("d_ad", 8 * g - 3)
                    if L3:
                        w("d_ke", 8 * g - 1)
                    for fc in range(8):
                        ins = t.matmul(ps_x0[0:16, :], rw_sb[:, fc * 16:(fc + 1) * 16],
                                       scr[:, fc * GT:(fc + 1) * GT],
                                       start=(fc == 0), stop=(fc == 7))
                    ins.then_inc(S["p_rl"], 1)

                def st_ltrans(g):
                    w("d_lg", g + 1); w("r1", g)
                    if L4 or prefix:
                        w("d_su", 4 * g)
                    for j in range(4):
                        ins = t.transpose(ps_sd[:, j * 16:(j + 1) * 16],
                                          lraw[:, j * 128:(j + 1) * 128], iden[0:16, 0:16])
                    ins.then_inc(S["p_lt"], 1)

                def st_q(g):
                    for ob in range(8):
                        if ob == 0:
                            if g == 0:
                                t.wait_ge(s_ldw, 6 * 16)
                            w("d_lg", g + 1)
                            if L5:
                                w("d_r", 8 * g - 1)
                            if L4:
                                w("d_ad", 8 * g - 1)
                            w("d_ke", 8 * g - 1)
                        elif ob == 1:
                            if L5:
                                w("d_r", 8 * g)
                            if L4:
                                w("d_ad", 8 * g)
                            w("d_ke", 8 * g)
                        else:
                            w("d_q", 8 * g + ob - 1)
                        for fc in range(8):
                            ins = t.matmul(PSX[ob % 2][:], wq_sb[:, fc * D + ob * 128:fc * D + (ob + 1) * 128],
                                           xn_bf[:, fc * T + g * GT:fc * T + (g + 1) * GT],
                                           start=(fc == 0), stop=(fc == 7))
                        ins.then_inc(S["p_q"], 1)

                def st_k(g):
                    for idx in range(8):
                        j, nb = idx // 2, idx % 2
                        if prefix:
                            if g == 0 and idx == 0:
                                t.wait_ge(s_ldw, 4 * 16)
                            w("d_ke", 8 * g + idx - 1)
                            if idx < 2:
                                w("d_xn", g + 1)
                        else:
                            if idx == 0:
                                w("d_q", 8 * g + 7)
                            elif idx == 1:
                                w("d_q", 8 * g + 8)
                            else:
                                w("d_ke", 8 * g + idx - 1)
                        for fc in range(8):
                            ins = t.matmul(PSX[idx % 2][:],
                                           xnT[:, xn_col(fc, g) + j * 128:xn_col(fc, g) + (j + 1) * 128],
                                           wk_sb[:, fc * D + nb * 512:fc * D + (nb + 1) * 512],
                                           start=(fc == 0), stop=(fc == 7))
                        ins.then_inc(S["p_k"], 1)

                def st_v(g):
                    for idx in range(8):
                        j, nb = idx // 2, idx % 2
                        if prefix and g == 0 and idx == 0:
                            t.wait_ge(s_ldw, 5 * 16)
                        if idx < 2 and not prefix:
                            if L4:
                                w("d_ad", 8 * g - 1 + idx)
                            elif L3:
                                w("d_kf", 8 * g - 1 + idx)
                        w("d_ve", 8 * g + idx - 1)
                        for fc in range(8):
                            ins = t.matmul(PSR[idx % 2][:],
                                           xnT[:, xn_col(fc, g) + j * 128:xn_col(fc, g) + (j + 1) * 128],
                                           wv_sb[:, fc * D + nb * 512:fc * D + (nb + 1) * 512],
                                           start=(fc == 0), stop=(fc == 7))
                        ins.then_inc(S["p_v"], 1)

                def st_sct(g):
                    w("d_sc", g + 1); w("d_ke", 8 * g + 8)
                    for j in range(4):
                        ins = t.transpose(ps_x1[0:16, j * 128:(j + 1) * 128],
                                          scob[:, j * 16:(j + 1) * 16], iden[:])
                    ins.then_inc(S["p_sT"], 1)

                def st_kt(g):
                    for i in range(8):
                        sub, half = i // 2, i % 2
                        w("d_kl", 4 * g + sub + 1)
                        if i == 0:
                            w("d_ve", 8 * g + 7)
                        elif i == 1:
                            w("d_ve", 8 * g + 8)
                        else:
                            w("d_kf", 8 * g + i - 1)
                        dstb = PSR[i % 2][:].bitcast(BF16)
                        for tt in range(4):
                            p = half * 4 + tt
                            ins = t.transpose(dstb[:, tt * 128:(tt + 1) * 128],
                                              Klnb[:, sub * D + p * 128:sub * D + (p + 1) * 128],
                                              iden_bf[:])
                        ins.then_inc(S["p_kt"], 1)

                def st_ret(g, cc):
                    w("d_sb", 4 * g + cc + 1)
                    w("d_qk", 4 * g + cc + 1)
                    w("d_kf", 8 * g + 8)
                    w("d_q", 8 * g + 8)
                    w("d_rdy", g + 1)
                    if cc == 0:
                        w("d_ke", 8 * g + 8)
                        w("d_sE", g + 1)
                    else:
                        w("d_ad", 8 * g + 2 * cc)
                    for p in range(8):
                        for hh in range(2):
                            dst = (PSX if hh == 0 else PSR)[p // 4]
                            ins = t.matmul(dst[:, (p % 4) * 128:(p % 4) * 128 + 128],
                                           Kfm[64 * hh:64 * hh + 64,
                                               p * GT + cc * 128:p * GT + (cc + 1) * 128],
                                           QY[64 * hh:64 * hh + 64,
                                              p * GT + cc * 128:p * GT + (cc + 1) * 128],
                                           start=True, stop=True)
                    ins.then_inc(S["p_A"], 1)
                    for r in range(2):
                        w("d_ad", 8 * g + 2 * cc + r + 1)
                        w("d_ye", 8 * g + 2 * cc + r)
                        for p in range(4 * r, 4 * r + 4):
                            for hh in range(2):
                                h = 2 * p + hh
                                t.matmul(ps_y1[64 * hh:64 * hh + 64, (p % 4) * 128:(p % 4) * 128 + 128],
                                         Vb[:, cc * D + h * 64:cc * D + (h + 1) * 64],
                                         adb[:, h * 128:(h + 1) * 128],
                                         start=True, stop=False)
                                ins = t.matmul(ps_y1[64 * hh:64 * hh + 64, (p % 4) * 128:(p % 4) * 128 + 128],
                                               S_bf[64 * hh:64 * hh + 64, p * 64:(p + 1) * 64],
                                               qg[64 * hh:64 * hh + 64, p * 128:(p + 1) * 128],
                                               start=False, stop=True)
                        ins.then_inc(S["p_y"], 1)
                    w("d_su", 4 * g + cc)
                    for h in range(NH):
                        ins = t.matmul(ps_sd[64 * (h % 2):64 * (h % 2) + 64,
                                             (h // 2) * 64:(h // 2 + 1) * 64],
                                       kg[:, h * 64:(h + 1) * 64],
                                       Vb[:, cc * D + h * 64:cc * D + (h + 1) * 64],
                                       start=True, stop=True)
                    ins.then_inc(S["p_sd"], 1)

                def st_sdall(g):
                    w("d_qk", 4 * g + 4)
                    w("d_ve", 8 * g + 8)
                    w("d_rdy", g + 1)
                    w("d_su", g)
                    for h in range(NH):
                        for cc in range(4):
                            kb = PKB[cc]
                            ins = t.matmul(ps_sd[64 * (h % 2):64 * (h % 2) + 64,
                                                 (h // 2) * 64:(h // 2 + 1) * 64],
                                           kb[:, h * 64:(h + 1) * 64],
                                           Vb[:, cc * D + h * 64:cc * D + (h + 1) * 64],
                                           start=(cc == 0), stop=(cc == 3))
                    ins.then_inc(S["p_sd"], 1)

                def st_warm(g):
                    for q in range(8):
                        w("a_kq", 8 * g + q + 1)
                        t.matmul(ps_y1[0:16, 0:16], ones_r[0:1, 0:16], ones_r[0:1, 0:16],
                                 start=True, stop=True)
                    for q in range(8):
                        w("d_ms", 8 * g + q + 1)
                        t.matmul(ps_y1[0:16, 0:16], ones_r[0:1, 0:16], ones_r[0:1, 0:16],
                                 start=True, stop=True)
                    for j in range(4):
                        w("d_kl", 4 * g + j + 1)
                        t.matmul(ps_y1[0:16, 0:16], ones_r[0:1, 0:16], ones_r[0:1, 0:16],
                                 start=True, stop=True)

                def st_wo(g):
                    w("d_ye", 8 * (g + 1))
                    for oc in range(8):
                        if oc == 0:
                            if g == 0:
                                t.wait_ge(s_ldw, 7 * 16)
                            w("d_ad", 8 * g + 7)
                        elif oc == 1:
                            w("d_ad", 8 * g + 8)
                        else:
                            w("d_r", 8 * g + oc - 1)
                        for yc in range(8):
                            ins = t.matmul(PSX[oc % 2][:],
                                           wo_sb[:, yc * D + oc * 128:yc * D + (oc + 1) * 128],
                                           QY[:, yc * GT:(yc + 1) * GT],
                                           start=(yc == 0), stop=(yc == 7))
                        ins.then_inc(S["p_wo"], 1)

                for g in range(NG):
                    bk = g >= 1
                    if prefix:
                        if g == 0:
                            st_xsum(0)
                        st_stats(g)
                        if bk:
                            st_sdall(g - 1)
                        st_bcast(g)
                        st_k(g)
                        st_v(g)
                        st_warm(g)
                        if g + 1 < NG:
                            st_xsum(g + 1)
                    else:
                        if g == 0:
                            st_xsum(0)
                        if bk and L4:
                            st_ret(g - 1, 0)
                        st_stats(g)
                        if bk and L4:
                            st_ret(g - 1, 1)
                        st_bcast(g)
                        if bk and L4:
                            st_ret(g - 1, 2)
                        if L2:
                            st_router(g)
                        if bk and L4:
                            st_ret(g - 1, 3)
                        if L2:
                            st_ltrans(g)
                        if bk and L5:
                            st_wo(g - 1)
                        if L3:
                            st_q(g)
                            st_k(g)
                            st_v(g)
                            st_warm(g)
                            if g + 1 < NG:
                                st_xsum(g + 1)
                        if L2:
                            st_sct(g)
                        if L3:
                            st_kt(g)
                if prefix:
                    st_sdall(NG - 1)
                else:
                    if L4:
                        for cc in range(4):
                            st_ret(NG - 1, cc)
                    if L5:
                        st_wo(NG - 1)

            @b.vector
            def _(v):
                def w(s, val):
                    val += BASE.get(s, 0)
                    if val > 0:
                        v.wait_ge(S[s], val)

                def sp(sname, k=1):
                    v.tensor_copy(jnk1[:], jnk1[:]).then_inc(S[sname], k)

                def newton_rsqrt(dst, varap, scr_f32):
                    vg(v, v.tensor_scalar(varap, varap, EPS, None, op0=ALU.add))
                    iv = varap.bitcast(I32)
                    si = scr_f32.bitcast(I32)
                    vg(v, v.tensor_scalar(si, iv, 1, None, op0=ALU.arith_shift_right))
                    vg(v, v.tensor_scalar(si, si, -1, None, op0=ALU.mult))
                    vg(v, v.tensor_scalar(si, si, 0x5F3759DF, None, op0=ALU.add))
                    for _ in range(2):
                        vg(v, v.tensor_mul(dst, scr_f32, scr_f32))
                        vg(v, v.tensor_mul(dst, dst, varap))
                        vg(v, v.tensor_scalar(dst, dst, -0.5, 1.5, op0=ALU.mult, op1=ALU.add))
                        vg(v, v.tensor_mul(scr_f32, scr_f32, dst))
                    ins = v.tensor_copy(dst, scr_f32)
                    vg(v, ins)
                    return ins
                def dv_rw(g):
                    mus = jnk1[0:1, 0:GT]
                    w("p_st", g + 1); w("p_bc", g)
                    vg(v, v.tensor_scalar(mus, ps_bc[0:1, 0:512], 1.0 / D, None, op0=ALU.mult))
                    vg(v, v.tensor_scalar(rowb[0:1, NMU:NMU + GT], mus, -1.0, None, op0=ALU.mult))
                    vg(v, v.tensor_mul(mus, mus, mus))
                    vg(v, v.tensor_scalar(rowb[0:1, VAR:VAR + GT], ps_bc[0:1, 512:1024], 1.0 / D, None, op0=ALU.mult))
                    vg(v, v.tensor_sub(rowb[0:1, VAR:VAR + GT], rowb[0:1, VAR:VAR + GT], mus))
                    if prefix:
                        sp("d_rw")
                    else:
                        newton_rsqrt(rowb[0:1, RSTD:RSTD + GT], rowb[0:1, VAR:VAR + GT], mus)
                        sp("a_rs")

                def dv_xn(g):
                    w("p_bc", g + 1)
                    if L2:
                        w("p_rl", g)
                    if prefix:
                        w("p_v", 8 * g)
                        w("p_k", 8 * g)
                    for fc in range(8):
                        v.tensor_add(scr[:, fc * GT:(fc + 1) * GT], xg[:, fc * GT:(fc + 1) * GT],
                                     ps_bc[:, 0:512])
                        v.tensor_mul(xnT[:, xn_col(fc, g):xn_col(fc, g) + GT],
                                     scr[:, fc * GT:(fc + 1) * GT], ps_bc[:, 512:1024])
                    sp("d_xn")

                def dv_lga(g):
                    w("p_rl", g + 1); w("p_lt", g)
                    vg(v, v.tensor_copy(lraw[:], ps_x0[0:16, :]))
                    vg(v, v.tensor_mul(lraw[:], lraw[:], ps_bc[0:16, 512:1024]))
                    vg(v, v.tensor_scalar(lraw[:], lraw[:], rb_sb[:], None, op0=ALU.add))
                    sp("d_lg")

                def dv_lgb(g):
                    w("p_lt", g + 1); w("a_e4", g); w("a_p16", g)
                    for j in range(4):
                        ins = v.tensor_copy(ltok[:, j * 16:(j + 1) * 16], ps_sd[:, j * 16:(j + 1) * 16])
                    ins.then_inc(S["r1"], 1)
                    v.wait_ge(S["r1"], g + 1)
                    for j in range(4):
                        ins = v.max(out=m8b[:, j * 8:(j + 1) * 8], in_=ltok[:, j * 16:(j + 1) * 16])
                    ins.then_inc(S["r2"], 1)
                    v.wait_ge(S["r2"], g + 1)
                    for j in range(4):
                        ins = v.memset(m8b[:, j * 8 + 4:(j + 1) * 8], NEG)
                    ins.then_inc(S["r3"], 1)
                    v.wait_ge(S["r3"], g + 1)
                    for j in range(4):
                        ins = v.match_replace(out=lz[:, j * 16:(j + 1) * 16],
                                              in_to_replace=m8b[:, j * 8:(j + 1) * 8],
                                              in_values=ltok[:, j * 16:(j + 1) * 16],
                                              imm_value=NEG)
                    ins.then_inc(S["r4"], 1)
                    v.wait_ge(S["r4"], g + 1)
                    for j in range(4):
                        ins = v.tensor_scalar(nmx[:, j:j + 1], m8b[:, j * 8:j * 8 + 1], -1.0, None,
                                              op0=ALU.mult)
                    ins.then_inc(S["r5"], 1)
                    for j in range(4):
                        ins = v.tensor_tensor(selb[:, j * 16:(j + 1) * 16], lz[:, j * 16:(j + 1) * 16],
                                              ltok[:, j * 16:(j + 1) * 16], op=ALU.not_equal)
                    vg(v, ins)

                def dv_lgc(g):
                    w("a_e4", g + 1)
                    for j in range(4):
                        ins = v.reduce_sum(dnm[:, j:j + 1], e4b[:, j * 4:(j + 1) * 4], axis=AX)
                    ins.then_inc(S["d_dn"], 1)
                    v.wait_ge(S["d_dn"], 2 * g + 1)
                    for j in range(4):
                        ins = v.reciprocal(dnm[:, 4 + j:5 + j], dnm[:, j:j + 1])
                    ins.then_inc(S["d_dn"], 1)
                    w("a_p16", g + 1); w("p_sT", g)
                    v.wait_ge(S["d_dn"], 2 * g + 2)
                    for j in range(4):
                        v.scalar_tensor_tensor(scob[:, j * 16:(j + 1) * 16],
                                               p16b[:, j * 16:(j + 1) * 16],
                                               dnm[:, 4 + j:5 + j],
                                               selb[:, j * 16:(j + 1) * 16],
                                               op0=ALU.mult, op1=ALU.mult)
                    sp("d_sc")

                def dv_kln(g):
                    for j in range(4):
                        w("d_ke", 8 * g + 2 * j + 2)
                        v.reduce_sum(kst[:, j * 16:(j + 1) * 16],
                                     scr[:, j * D:(j + 1) * D].rearrange("p (h d) -> p h d", d=64),
                                     axis=AX)
                        for hf in range(2):
                            q = 2 * j + hf
                            w("a_kq", 8 * g + q + 1)
                            c0 = 64 + j * 16 + hf * 8
                            ins = v.reduce_sum(kst[:, c0:c0 + 8],
                                               ps_bc[:, (q % 2) * 512:(q % 2) * 512 + 512]
                                               .rearrange("p (h d) -> p h d", d=64), axis=AX)
                            ins.then_inc(S["d_ms"], 1)
                    v.wait_ge(S["d_ms"], 8 * g + 8 + BASE.get("d_ms", 0))
                    vg(v, v.tensor_scalar(kst[:, 128:192], kst[:, 0:64], 1.0 / 64, None, op0=ALU.mult))
                    vg(v, v.tensor_mul(kst[:, 0:64], kst[:, 128:192], kst[:, 128:192]))
                    vg(v, v.tensor_scalar(kst[:, 64:128], kst[:, 64:128], 1.0 / 64, None, op0=ALU.mult))
                    vg(v, v.tensor_sub(kst[:, 64:128], kst[:, 64:128], kst[:, 0:64]))
                    sp("a_kr")
                    v.wait_ge(S["d_st2"], g + 1 + BASE.get("d_st2", 0))
                    if not prefix:
                        w("p_kt", 8 * g)
                    for j in range(4):
                        K3 = scr[:, j * D:(j + 1) * D].rearrange("p (h d) -> p h d", d=64)
                        KB3 = Klnb[:, j * D:(j + 1) * D].rearrange("p (h d) -> p h d", d=64)
                        mu3 = kst[:, 128 + j * 16:128 + (j + 1) * 16].unsqueeze(2).to_broadcast([128, 16, 64])
                        rs3 = kst[:, 192 + j * 16:192 + (j + 1) * 16].unsqueeze(2).to_broadcast([128, 16, 64])
                        v.tensor_tensor(K3, K3, mu3, op=ALU.subtract)
                        v.tensor_tensor(KB3, K3, rs3, op=ALU.mult).then_inc(S["d_kl"], 1)

                def dv_ret(g, cc):
                    w("p_sd", 4 * g + cc)
                    w("p_y", 8 * g + 2 * cc)
                    w("d_q", 8 * g + 8)
                    w("d_rdy", g + 1)
                    v.tensor_mul(kg[:], Klnb[:, cc * D:(cc + 1) * D], Gk[:])
                    v.tensor_mul(qg[:].rearrange("p (pr t) -> p pr t", t=128),
                                 QY[:].rearrange("p (pr t) -> p pr t", t=GT)[:, :, cc * 128:(cc + 1) * 128],
                                 Gq[:].rearrange("p (pr t) -> p pr t", t=128))
                    sp("d_qk")
                    v.tensor_copy(S_bf[:], S_T[:])
                    sp("d_sb")
                    w("p_A", 4 * g + cc + 1)
                    adb3 = adb[:].rearrange("p (pr two t) -> p pr two t", two=2, t=128)
                    Dw3 = Dw[:].rearrange("p (pr two t) -> p pr two t", two=2, t=128)
                    for half in range(2):
                        v.tensor_tensor(adb3[:, 4 * half:4 * half + 4, 0, :],
                                        PSX[half][:].rearrange("p (s t) -> p s t", t=128),
                                        Dw3[:, 4 * half:4 * half + 4, 0, :], op=ALU.mult)
                        v.tensor_tensor(adb3[:, 4 * half:4 * half + 4, 1, :],
                                        PSR[half][:].rearrange("p (s t) -> p s t", t=128),
                                        Dw3[:, 4 * half:4 * half + 4, 1, :], op=ALU.mult)
                        sp("d_ad")
                    w("p_sd", 4 * g + cc + 1)
                    v.tensor_mul(S_T[:], S_T[:], gC[:])
                    v.tensor_add(S_T[:], S_T[:], ps_sd[:]).then_inc(S["d_su"], 1)

                def dv_sd(g, cc):
                    kb = PKB[cc]
                    w("p_sd", g)
                    w("d_rdy", g + 1)
                    gkp = Gk[:] if cc == 3 else Kfm[:, cc * 1024:(cc + 1) * 1024]
                    v.tensor_mul(kb[:], Klnb[:, cc * D:(cc + 1) * D], gkp)
                    sp("d_qk")

                def dv_sadd(g):
                    w("p_sd", g + 1)
                    for _ in range(4):
                        v.tensor_mul(S_T[:], S_T[:], gC[:])
                    v.tensor_add(S_T[:], S_T[:], ps_sd[:]).then_inc(S["d_su"], 1)

                for g in range(NG):
                    bk = g >= 1
                    if prefix:
                        if bk:
                            dv_sd(g - 1, 0)
                        dv_rw(g)
                        if bk:
                            dv_sd(g - 1, 1)
                            dv_sd(g - 1, 2)
                            dv_sd(g - 1, 3)
                        dv_xn(g)
                        if bk:
                            dv_sadd(g - 1)
                        dv_kln(g)
                    else:
                        if bk and L4:
                            dv_ret(g - 1, 0)
                            dv_ret(g - 1, 1)
                        dv_rw(g)
                        if bk and L4:
                            dv_ret(g - 1, 2)
                        dv_xn(g)
                        if L2:
                            dv_lga(g)
                        if bk and L4:
                            dv_ret(g - 1, 3)
                        if L3:
                            dv_kln(g)
                        if L2:
                            dv_lgb(g)
                            dv_lgc(g)
                if prefix:
                    for cc in range(4):
                        dv_sd(NG - 1, cc)
                    dv_sadd(NG - 1)
                    v.tensor_scalar(S_T[:], S_T[:], flag[:], None, op0=ALU.mult)
                elif L4:
                    for cc in range(4):
                        dv_ret(NG - 1, cc)

            if prefix:
                @b.gpsimd
                def _(gp):
                    for dst, srch in [(wq_sb, wq_h), (wo_sb, wo_h)]:
                        gp.dma_start(dst[:].rearrange("p (c o) -> p c o", c=8),
                                     srch.rearrange("(c p) o -> p c o", p=128)).then_inc(s_ldw, 16)
                    gp.wait_ge(s_ldw, 7 * 16)

            @b.scalar
            def _(s):
                def w(sn, val):
                    val += BASE.get(sn, 0)
                    if val > 0:
                        s.wait_ge(S[sn], val)

                def asp(sname, k=1):
                    s.copy(jnkA[:], iden[:]).then_inc(S[sname], k)

                def ac_rw(g):
                    w("d_rw", g + 1)
                    s.activation(rowb[0:1, VAR:VAR + GT], rowb[0:1, VAR:VAR + GT], AF.Ln,
                                 bias=eps_c[0:1, :])
                    s.copy(jnkA[:], iden[:])
                    s.activation(rowb[0:1, RSTD:RSTD + GT], rowb[0:1, VAR:VAR + GT], AF.Exp,
                                 scale=-0.5)
                    asp("a_rs")

                def ac_krstd(g):
                    w("a_kr", g + 1)
                    s.activation(kst[:, 64:128], kst[:, 64:128], AF.Ln, bias=eps_c[:])
                    s.copy(jnkA[:], iden[:])
                    s.activation(kst[:, 192:256], kst[:, 64:128], AF.Exp, scale=-0.5)
                    asp("d_st2")

                def ac_sq(g):
                    w("sx", 16 * (g + 1))
                    if not (prefix or L3):
                        if L2:
                            w("p_rl", g)
                        w("p_st", g)
                        s.activation(scr[:], xg[:], AF.Square).then_inc(S["a_sq"], 1)
                        return
                    w("p_st", g)
                    for j in range(4):
                        w("d_kl", 4 * (g - 1) + j + 1)
                        ins = s.activation(scr[:, j * D:(j + 1) * D], xg[:, j * D:(j + 1) * D],
                                           AF.Square)
                    ins.then_inc(S["a_sq"], 1)

                def ac_yev(g, cc, r):
                    w("p_y", 8 * g + 2 * cc + r + 1)
                    ins = s.copy(
                        QY[:].rearrange("p (pr t) -> p pr t", t=GT)[:, 4 * r:4 * r + 4, cc * 128:(cc + 1) * 128],
                        ps_y1[:].rearrange("p (s t) -> p s t", t=128))
                    if cc == 3 and r == 1:
                        asp("d_ye")
                    else:
                        ins.then_inc(S["d_ye"], 1)

                def ac_rev(g):
                    rstage = Kfm[:].bitcast(F32)
                    for oc in range(8):
                        w("p_wo", 8 * g + oc + 1)
                        if oc == 4:
                            w("sst", 32 * g + 16)
                        ins = s.copy(rstage[:, (oc % 4) * 512:(oc % 4 + 1) * 512], PSX[oc % 2][:])
                        if (oc + 1) % 4 != 0:
                            ins.then_inc(S["d_r"], 1)
                        else:
                            asp("d_r")

                def ac_e4p16(g):
                    w("r5", g + 1); w("d_dn", 2 * g)
                    for j in range(4):
                        ins = s.activation(e4b[:, j * 4:(j + 1) * 4], m8b[:, j * 8:j * 8 + 4],
                                           AF.Exp, bias=nmx[:, j:j + 1])
                    ins.then_inc(S["a_e4"], 1)
                    w("d_sc", g)
                    for j in range(4):
                        ins = s.activation(p16b[:, j * 16:(j + 1) * 16], ltok[:, j * 16:(j + 1) * 16],
                                           AF.Exp, bias=nmx[:, j:j + 1])
                    ins.then_inc(S["a_p16"], 1)

                def ac_qev(g):
                    w("p_wo", 8 * g)
                    for ob in range(8):
                        w("p_q", 8 * g + ob + 1)
                        ins = s.copy(QY[:, ob * GT:(ob + 1) * GT], PSX[ob % 2][:])
                        if ob < 7:
                            ins.then_inc(S["d_q"], 1)
                        else:
                            asp("d_q")

                def ac_kev(g):
                    for idx in range(8):
                        j, nb = idx // 2, idx % 2
                        w("p_k", 8 * g + idx + 1)
                        ins = s.copy(scr[:, j * D + nb * 512:j * D + (nb + 1) * 512],
                                     PSX[idx % 2][:])
                        if idx < 7:
                            ins.then_inc(S["d_ke"], 1)
                        else:
                            asp("d_ke")

                def ac_ksq(g):
                    for q in range(8):
                        j, hf = q // 2, q % 2
                        w("d_ms", 8 * g + q - 1)
                        s.activation(ps_bc[:, (q % 2) * 512:(q % 2) * 512 + 512],
                                     scr[:, j * D + hf * 512:j * D + hf * 512 + 512],
                                     AF.Square).then_inc(S["a_kq"], 1)

                def ac_vev(g):
                    if prefix:
                        w("p_sd", g)
                    elif L4:
                        w("p_sd", 4 * g)
                    if L4:
                        w("p_y", 8 * g)
                    for idx in range(8):
                        j, nb = idx // 2, idx % 2
                        w("p_v", 8 * g + idx + 1)
                        ins = s.copy(Vb[:, j * D + nb * 512:j * D + (nb + 1) * 512],
                                     PSR[idx % 2][:])
                        if idx < 7:
                            ins.then_inc(S["d_ve"], 1)
                        else:
                            asp("d_ve")

                def ac_scev(g):
                    w("p_sT", g + 1)
                    s.copy(scoresT[:, g * GT:(g + 1) * GT], ps_x1[0:16, 0:512]).then_inc(S["d_sE"], 1)

                def ac_ktev(g):
                    if L4:
                        w("p_A", 4 * g)
                    if L5:
                        w("sst", 32 * g)
                    for i in range(8):
                        sub, half = i // 2, i % 2
                        w("p_kt", 8 * g + i + 1)
                        s.copy(
                            Kfm[:].rearrange("p (pr t) -> p pr t", pr=8)
                                  [:, half * 4:(half + 1) * 4, sub * 128:(sub + 1) * 128],
                            PSR[i % 2][:].bitcast(BF16)[:, 0:512].rearrange("p (tt t) -> p tt t", tt=4),
                        ).then_inc(S["d_kf"], 1)

                for g in range(NG):
                    ac_sq(g)
                    if prefix:
                        ac_rw(g)
                    if g >= 1:
                        if L4:
                            for cc in range(4):
                                for r in range(2):
                                    ac_yev(g - 1, cc, r)
                        if L5:
                            ac_rev(g - 1)
                    if L3:
                        ac_qev(g)
                    if prefix or L3:
                        ac_kev(g)
                        ac_vev(g)
                        ac_ksq(g)
                        ac_krstd(g)
                    if L2:
                        ac_e4p16(g)
                        ac_scev(g)
                    if L3:
                        ac_ktev(g)
                    if prefix or L3:
                        asp("d_rdy")
                if L4:
                    for cc in range(4):
                        for r in range(2):
                            ac_yev(NG - 1, cc, r)
                if L5:
                    ac_rev(NG - 1)

        return S

    with nc.Block("PA") as blk:
        emit_preamble(blk)
        attn_phase(blk, xpf, prefix=True)
        attn_phase(blk, xf, prefix=False)
    esP.close()

    esA.close()

    esB = ExitStack()
    sbB = lambda name, shape, dt: esB.enter_context(nc.sbuf_tensor("sc_" + name, shape, dt))
    acc = sbB("acc", [128, 8 * T], F32)
    gub_sb = sbB("gub_sb", [128, 256], F32)
    dnb_sb = sbB("dnb_sb", [16, D], F32)
    esel_sb = sbB("esel_sb", [16, NE * 128], F32)
    act = sbB("act", [128, 8 * T], BF16)
    wgu_sb = [sbB(f"wgu{i}", [128, 8 * 256], BF16) for i in range(3)]
    wd_sb = [sbB(f"wd{i}", [128, 8 * 128], BF16) for i in range(2)]
    sco_bc = sbB("sco_bc", [128, T], BF16)
    g1b = [sbB(f"g1b{i}", [128, 1024], F32) for i in range(2)]
    sigb = [sbB(f"sigb{i}", [128, 1024], F32) for i in range(2)]
    u2b = [sbB(f"u2b{i}", [128, 1024], F32) for i in range(2)]
    glub = [sbB("glub0", [128, 1024], F32)] * 2

    esB2 = ExitStack()
    ps_a = esB2.enter_context(nc.psum_tensor("ps_a", [128, 1024], F32))
    ps_b = esB2.enter_context(nc.psum_tensor("ps_b", [128, 1024], F32))
    ps_c = esB2.enter_context(nc.psum_tensor("ps_c", [128, 1024], F32))
    ps_d = esB2.enter_context(nc.psum_tensor("ps_d", [128, 1024], F32))
    PGT = [ps_a, ps_c]
    PUP = [ps_b, ps_d]
    PDN = [ps_b, ps_d]
    PSC = [ps_a, ps_c]

    s_r3 = sem("s_r3")
    s_wg = sem("s_wg")
    s_wd = sem("s_wd")
    p_bias = sem("p_bias")
    d_bias = sem("d_bias")
    p_sc3 = sem("p_sc3")
    d_sb3 = sem("d_sb3")
    p_gu = sem("p_gu")
    d_gu = sem("d_gu")
    d_g1 = sem("d_g1")
    a_sg3 = sem("a_sg3")
    d_act = sem("d_act")
    p_dn3 = sem("p_dn3")
    d_dn3 = sem("d_dn3")

    with nc.Block("P3") as b:
        @b.sync
        def _(sync):
            sync.dma_start(dnb_sb[:], dnb_h).then_inc(s_r3, 16)
            sync.dma_start(esel_sb[:], esel_h).then_inc(s_r3, 16)
            sync.dma_start(gub_sb[:], gub_h).then_inc(s_r3, 16)
            for c_ in range(8):
                sync.dma_start(
                    acc[:, c_ * T:(c_ + 1) * T].rearrange("p (gg t) -> p gg t", gg=4),
                    rscr.rearrange("p (gg c t) -> p gg c t", gg=4, c=8)[:, :, c_, :],
                ).then_inc(s_r3, 16)

        @b.gpsimd
        def _(g):
            for e in range(NE):
                for j in range(8):
                    idx = e * 8 + j
                    if idx >= 3:
                        g.wait_ge(p_gu, 4 * idx - 8)
                    g.dma_start(wgu_sb[idx % 3][:].rearrange("p (c f) -> p c f", c=8),
                                wgu_h[e, j].rearrange("(c p) f -> p c f", p=128)).then_inc(s_wg, 16)
                for hb in range(8):
                    idx = e * 8 + hb
                    if idx >= 2:
                        g.wait_ge(p_dn3, 4 * (idx - 1))
                    g.dma_start(wd_sb[idx % 2][:].rearrange("p (c f) -> p c f", c=8),
                                wd_h[e, hb].rearrange("(c p) f -> p c f", p=128)).then_inc(s_wd, 16)

        @b.tensor
        def _(t):
            t.wait_ge(s_r3, 16)
            for ds in range(32):
                hb, st = ds // 4, ds % 4
                if ds >= 2:
                    t.wait_ge(d_bias, ds - 1)
                t.matmul(PDN[ds % 2][:, 0:512], dnb_sb[:, hb * 128:(hb + 1) * 128],
                         scoresT[:, st * 512:(st + 1) * 512], start=True, stop=True).then_inc(p_bias, 1)
            def score_bc(e):
                for st in range(4):
                    ins = t.matmul(PSC[st // 2][:, (st % 2) * 512:(st % 2 + 1) * 512],
                                   esel_sb[:, e * 128:(e + 1) * 128],
                                   scoresT[:, st * 512:(st + 1) * 512], start=True, stop=True)
                ins.then_inc(p_sc3, 1)

            for e in range(NE):
                if e == 0:
                    t.wait_ge(s_r3, 32)
                    score_bc(0)
                    t.wait_ge(d_bias, 32)
                for u in range(16):
                    j, hf = u // 2, u % 2
                    U = 16 * e + u
                    t.wait_ge(s_wg, 16 * (e * 8 + j + 1))
                    if U >= 2:
                        t.wait_ge(d_gu, U - 1)
                    if u <= 1:
                        t.wait_ge(d_sb3, e + 1)
                        if e == 0:
                            t.wait_ge(d_bias, 32)
                        else:
                            t.wait_ge(d_dn3, 32 * e)
                    for fc in range(8):
                        for s2 in range(2):
                            t.matmul(PGT[U % 2][:, s2 * 512:(s2 + 1) * 512],
                                     wgu_sb[(e * 8 + j) % 3][:, fc * 256:fc * 256 + 128],
                                     xn_bf[:, fc * T + hf * 1024 + s2 * 512:fc * T + hf * 1024 + (s2 + 1) * 512],
                                     start=(fc == 0), stop=(fc == 7))
                    for fc in range(8):
                        for s2 in range(2):
                            ins = t.matmul(PUP[U % 2][:, s2 * 512:(s2 + 1) * 512],
                                           wgu_sb[(e * 8 + j) % 3][:, fc * 256 + 128:fc * 256 + 256],
                                           xn_bf[:, fc * T + hf * 1024 + s2 * 512:fc * T + hf * 1024 + (s2 + 1) * 512],
                                           start=(fc == 0), stop=(fc == 7))
                    ins.then_inc(p_gu, 2)
                t.wait_ge(d_gu, 16 * (e + 1))
                if e + 1 < NE:
                    score_bc(e + 1)
                t.wait_ge(d_act, 16 * (e + 1))
                for ds in range(32):
                    hb, st = ds // 4, ds % 4
                    t.wait_ge(s_wd, 16 * (e * 8 + hb + 1))
                    gds = 32 * e + ds
                    if gds >= 2:
                        t.wait_ge(d_dn3, gds - 1)
                    for fb in range(8):
                        ins = t.matmul(PDN[ds % 2][:, 0:512],
                                       wd_sb[(e * 8 + hb) % 2][:, fb * 128:(fb + 1) * 128],
                                       act[:, fb * T + st * 512:fb * T + (st + 1) * 512],
                                       start=(fb == 0), stop=(fb == 7))
                    ins.then_inc(p_dn3, 1)

        @b.vector
        def _(v):
            for ds in range(32):
                hb, st = ds // 4, ds % 4
                v.wait_ge(s_r3, 48 + 16 * (hb + 1))
                v.wait_ge(p_bias, ds + 1)
                v.tensor_add(acc[:, hb * T + st * 512:hb * T + (st + 1) * 512],
                             acc[:, hb * T + st * 512:hb * T + (st + 1) * 512],
                             PDN[ds % 2][:, 0:512]).then_inc(d_bias, 1)
            for e in range(NE):
                def frontend(u):
                    U = 16 * e + u
                    j, hf = u // 2, u % 2
                    v.wait_ge(p_gu, 2 * U + 2)
                    if U == 0:
                        v.wait_ge(s_r3, 48)
                    if U >= 1:
                        v.wait_ge(a_sg3, U - 1)
                    v.tensor_scalar(g1b[u % 2][:], PGT[U % 2][:],
                                    gub_sb[:, e * 16 + 2 * j:e * 16 + 2 * j + 1], 7.0,
                                    op0=ALU.add, op1=ALU.min)
                    v.tensor_scalar(u2b[u % 2][:], PUP[U % 2][:],
                                    gub_sb[:, e * 16 + 2 * j + 1:e * 16 + 2 * j + 2], -7.0,
                                    op0=ALU.add, op1=ALU.max).then_inc(d_g1, 1)
                    ins = v.tensor_scalar(u2b[u % 2][:], u2b[u % 2][:], 7.0, None, op0=ALU.min)
                    ins.then_inc(d_gu, 1)

                def backend(u):
                    U = 16 * e + u
                    j, hf = u // 2, u % 2
                    v.wait_ge(a_sg3, U + 1)
                    if u == 0:
                        v.wait_ge(p_dn3, 32 * e)
                        v.wait_ge(d_sb3, e + 1)
                    v.tensor_mul(glub[u % 2][:], g1b[u % 2][:], sigb[u % 2][:])
                    v.scalar_tensor_tensor(act[:, j * T + hf * 1024:j * T + (hf + 1) * 1024],
                                           u2b[u % 2][:], 1.0, glub[u % 2][:],
                                           op0=ALU.add, op1=ALU.mult)
                    v.tensor_mul(act[:, j * T + hf * 1024:j * T + (hf + 1) * 1024],
                                 act[:, j * T + hf * 1024:j * T + (hf + 1) * 1024],
                                 sco_bc[:, hf * 1024:(hf + 1) * 1024]).then_inc(d_act, 1)

                frontend(0)
                for u in range(1, 16):
                    frontend(u)
                    backend(u - 1)
                backend(15)
                for ds in range(32):
                    hb, st = ds // 4, ds % 4
                    v.wait_ge(p_dn3, 32 * e + ds + 1)
                    v.tensor_add(acc[:, hb * T + st * 512:hb * T + (st + 1) * 512],
                                 acc[:, hb * T + st * 512:hb * T + (st + 1) * 512],
                                 PDN[ds % 2][:, 0:512]).then_inc(d_dn3, 1)

        @b.scalar
        def _(s):
            for e in range(NE):
                s.wait_ge(p_sc3, e + 1)
                if e >= 1:
                    s.wait_ge(d_act, 16 * e)
                s.copy(sco_bc[:, 0:1024], PSC[0][:])
                s.copy(sco_bc[:, 1024:2048], PSC[1][:]).then_inc(d_sb3, 1)
                for u in range(16):
                    U = 16 * e + u
                    s.wait_ge(d_g1, U + 1)
                    if U >= 1:
                        s.wait_ge(d_act, U - 1)
                    s.activation(sigb[u % 2][:], g1b[u % 2][:], AF.Sigmoid,
                                 scale=1.702).then_inc(a_sg3, 1)

    esB2.close()
    ps_t = [esB.enter_context(nc.psum_tensor(f"ps_t{i}", [128, 1024], F32)) for i in range(2)]
    outb = g1b
    p_mg = sem("p_mg")
    d_mg = sem("d_mg")
    s_out = sem("s_out")

    s_xp = sem("s_xp")
    xb = sigb

    with nc.Block("P4") as b:
        @b.tensor
        def _(t):
            for cc in range(16):
                if cc >= 2:
                    t.wait_ge(d_mg, cc - 1)
                for fc in range(8):
                    ins = t.transpose(ps_t[cc % 2][:, fc * 128:(fc + 1) * 128],
                                      acc[:, fc * T + cc * 128:fc * T + (cc + 1) * 128], iden[:])
                ins.then_inc(p_mg, 1)

        @b.vector
        def _(v):
            for cc in range(16):
                v.wait_ge(p_mg, cc + 1)
                v.wait_ge(s_xp, 16 * (cc + 1))
                if cc >= 2:
                    v.wait_ge(s_out, 16 * (cc - 1))
                v.tensor_add(outb[cc % 2][:], xb[cc % 2][:], ps_t[cc % 2][:]).then_inc(d_mg, 1)

        @b.sync
        def _(sync):
            for cc in range(18):
                if cc < 16:
                    if cc >= 2:
                        sync.wait_ge(d_mg, cc - 1)
                    sync.dma_start(xb[cc % 2][:], xtok_h[cc * 128:(cc + 1) * 128, :]).then_inc(s_xp, 16)
                if cc >= 2:
                    sync.wait_ge(d_mg, cc - 1)
                    sync.dma_start(y_h[(cc - 2) * 128:(cc - 1) * 128, :],
                                   outb[cc % 2][:]).then_inc(s_out, 16)
            sync.wait_ge(s_out, 16 * 16)

    esB.close()
    es.close()
    return nc



_NC = None


def _get_nc():
    global _NC
    if _NC is None:
        _NC = build_nc()
    return _NC


def _host_consts():
    i = np.arange(128)
    c = {}
    c["iden"] = np.eye(128, dtype=np.float32)
    c["dclamp"] = np.maximum(i[None, :] - i[:, None], 0).astype(np.float32)
    c["mask01"] = (i[None, :] >= i[:, None]).astype(np.float32)
    c["tposb"] = np.broadcast_to((i + 1)[None, :], (128, 128)).astype(np.float32).copy()
    c["jrev"] = (127 - i)[:, None].astype(np.float32)
    m64 = np.zeros((128, 2), np.float32)
    m64[:64, 0] = 1.0
    m64[64:, 1] = 1.0
    c["m64"] = m64
    esel = np.zeros((16, NE * 128), np.float32)
    for e in range(NE):
        esel[e, e * 128:(e + 1) * 128] = 1.0
    c["esel"] = esel
    return c


def kernel(L0_attn_x, L0_attn_W_qkv, L0_attn_W_o, L0_attn_log_gamma,
           L0_ffn_router_w, L0_ffn_router_b, L0_ffn_gate_up_proj, L0_ffn_gate_up_bias,
           L0_ffn_down_proj, L0_ffn_down_bias):
    x = np.asarray(L0_attn_x, np.float32)
    Wqkv = np.asarray(L0_attn_W_qkv, np.float32)
    Wo = np.asarray(L0_attn_W_o, np.float32)
    lg = np.asarray(L0_attn_log_gamma, np.float32)
    rw = np.asarray(L0_ffn_router_w, np.float32)
    rb = np.asarray(L0_ffn_router_b, np.float32)
    gup = np.asarray(L0_ffn_gate_up_proj, np.float32)
    gub = np.asarray(L0_ffn_gate_up_bias, np.float32)
    wdp = np.asarray(L0_ffn_down_proj, np.float32)
    dnb = np.asarray(L0_ffn_down_bias, np.float32)

    consts = _host_consts()
    wq = np.ascontiguousarray(Wqkv[0:1024].T)
    wk = np.ascontiguousarray(Wqkv[1024:2048].T)
    wv = np.ascontiguousarray(Wqkv[2048:3072].T)
    wo = np.ascontiguousarray(Wo.T)

    gate_w = gup[:, :, 0::2]
    up_w = gup[:, :, 1::2]
    wgu = np.empty((NE, 8, D, 256), np.float32)
    for j in range(8):
        wgu[:, j, :, 0:128] = gate_w[:, :, j * 128:(j + 1) * 128]
        wgu[:, j, :, 128:256] = up_w[:, :, j * 128:(j + 1) * 128]
    wd = np.ascontiguousarray(wdp.reshape(NE, D, 8, 128).transpose(0, 2, 1, 3))

    gate_b = gub[:, 0::2]
    up_b = gub[:, 1::2]
    gub_cols = np.empty((128, 256), np.float32)
    for e in range(NE):
        for j in range(8):
            gub_cols[:, e * 16 + 2 * j] = gate_b[e, j * 128:(j + 1) * 128]
            gub_cols[:, e * 16 + 2 * j + 1] = up_b[e, j * 128:(j + 1) * 128]

    nc = _get_nc()
    in_maps = []
    for c in range(8):
        bb, hh = c // 2, c % 2
        sl = x[bb, 2048 * hh:2048 * (hh + 1)]
        xfc = np.ascontiguousarray(sl.T)
        if hh == 1:
            xpfc = np.ascontiguousarray(x[bb, 0:2048].T)
            flagc = np.ones((128, 1), np.float32)
        else:
            xpfc = xfc
            flagc = np.zeros((128, 1), np.float32)
        m = dict(xf=xfc, xpf=xpfc, xtok=np.ascontiguousarray(sl), wq=wq, wk=wk, wv=wv, wo=wo,
                 rw=rw, rb=rb.reshape(16, 1), wgu=wgu, wd=wd,
                 gub=gub_cols, dnb=dnb, esel=consts["esel"],
                 iden=consts["iden"], dclamp=consts["dclamp"], mask01=consts["mask01"],
                 tposb=consts["tposb"], jrev=consts["jrev"], m64=consts["m64"],
                 flag=flagc, lg=lg.reshape(1, 16))
        in_maps.append(m)

    res = run_bass_kernel_spmd(nc, in_maps, core_ids=list(range(8)))
    out = np.empty((4, 4096, 1024), np.float32)
    for c in range(8):
        bb, hh = c // 2, c % 2
        out[bb, 2048 * hh:2048 * (hh + 1)] = res.results[c]["y"]
    return out


# version 17
# speedup vs baseline: 1.0419x; 1.0040x over previous
import numpy as np
from contextlib import ExitStack

import concourse.bass as bass
import concourse.mybir as mybir
from concourse.bass_utils import run_bass_kernel_spmd

F32 = mybir.dt.float32
I32 = mybir.dt.int32
BF16 = mybir.dt.bfloat16
AF = mybir.ActivationFunctionType
ALU = mybir.AluOpType
AX = mybir.AxisListType.X

D = 1024
T = 2048
NG = 4
GT = 512
NH = 16
NE = 16
EPS = 1e-5
NEG = -1e30


def build_nc():
    nc = bass.Bass("TRN2", target_bir_lowering=False)

    xf = nc.dram_tensor("xf", [D, T], F32, kind="ExternalInput").ap()
    xpf = nc.dram_tensor("xpf", [D, T], F32, kind="ExternalInput").ap()
    wq_h = nc.dram_tensor("wq", [D, D], F32, kind="ExternalInput").ap()
    wk_h = nc.dram_tensor("wk", [D, D], F32, kind="ExternalInput").ap()
    wv_h = nc.dram_tensor("wv", [D, D], F32, kind="ExternalInput").ap()
    wo_h = nc.dram_tensor("wo", [D, D], F32, kind="ExternalInput").ap()
    rw_h = nc.dram_tensor("rw", [D, 16], F32, kind="ExternalInput").ap()
    rb_h = nc.dram_tensor("rb", [16, 1], F32, kind="ExternalInput").ap()
    wgu_h = nc.dram_tensor("wgu", [NE, 8, D, 256], F32, kind="ExternalInput").ap()
    wd_h = nc.dram_tensor("wd", [NE, 8, D, 128], F32, kind="ExternalInput").ap()
    gub_h = nc.dram_tensor("gub", [128, 256], F32, kind="ExternalInput").ap()
    dnb_h = nc.dram_tensor("dnb", [16, D], F32, kind="ExternalInput").ap()
    esel_h = nc.dram_tensor("esel", [16, NE * 128], F32, kind="ExternalInput").ap()
    iden_h = nc.dram_tensor("iden", [128, 128], F32, kind="ExternalInput").ap()
    dclamp_h = nc.dram_tensor("dclamp", [128, 128], F32, kind="ExternalInput").ap()
    mask01_h = nc.dram_tensor("mask01", [128, 128], F32, kind="ExternalInput").ap()
    tposb_h = nc.dram_tensor("tposb", [128, 128], F32, kind="ExternalInput").ap()
    jrev_h = nc.dram_tensor("jrev", [128, 1], F32, kind="ExternalInput").ap()
    m64_h = nc.dram_tensor("m64", [128, 2], F32, kind="ExternalInput").ap()
    flag_h = nc.dram_tensor("flag", [128, 1], F32, kind="ExternalInput").ap()
    lg_h = nc.dram_tensor("lg", [1, 16], F32, kind="ExternalInput").ap()

    y_h = nc.dram_tensor("y", [T, D], F32, kind="ExternalOutput").ap()
    xtok_h = nc.dram_tensor("xtok", [T, D], F32, kind="ExternalInput").ap()
    rscr = nc.dram_tensor("rscr", [128, 8 * T], F32, kind="Internal").ap()

    es = ExitStack()
    sb = lambda name, shape, dt: es.enter_context(nc.sbuf_tensor("sb_" + name, shape, dt))
    sem = lambda name: es.enter_context(nc.semaphore(name))

    xn_bf = sb("xn_bf", [128, 8 * T], BF16)
    scoresT = sb("scoresT", [16, T], F32)
    S_T = sb("S_T", [128, 512], F32)
    iden = sb("iden", [128, 128], F32)
    ones_r = sb("ones_r", [1, 128], F32)
    ones_c = sb("ones_c", [128, 1], F32)
    onesw = sb("onesw", [128, 64], F32)
    eps_c = sb("eps_c", [128, 1], F32)
    jnk1 = sb("jnk1", [128, 512], F32)
    jnkA = sb("jnkA", [128, 128], F32)
    flag = sb("flag", [128, 1], F32)
    rb_sb = sb("rb_sb", [16, 1], F32)
    rw_sb = sb("rw_sb", [128, 8 * 16], F32)

    esA = ExitStack()
    sbA = lambda name, shape, dt: esA.enter_context(nc.sbuf_tensor("sa_" + name, shape, dt))
    wq_sb = sbA("wq_sb", [128, 8 * D], BF16)
    wk_sb = sbA("wk_sb", [128, 8 * D], BF16)
    wv_sb = sbA("wv_sb", [128, 8 * D], BF16)
    wo_sb = sbA("wo_sb", [128, 8 * D], BF16)
    xg = sbA("xg", [128, 8 * GT], F32)
    scr = sbA("scr", [128, 8 * GT], F32)
    QY = sbA("QY", [128, 8 * GT], BF16)
    Vb = sbA("Vb", [128, NG * D], BF16)
    Kfm = sbA("Kfm", [128, 8 * GT], BF16)
    qg = sbA("qg", [128, 8 * 128], BF16)
    kg = sbA("kg", [128, D], BF16)
    Klnb = sbA("Klnb", [128, NG * D], BF16)
    iden_bf = sbA("iden_bf", [128, 128], BF16)
    adb = sbA("adb", [128, 2048], BF16)
    S_bf = sbA("S_bf", [128, 512], BF16)
    Dw = sbA("Dw", [128, NH * 128], BF16)
    Gq = sbA("Gq", [128, 8 * 128], BF16)
    Gk = sbA("Gk", [128, D], BF16)
    gC = sbA("gC", [128, 512], F32)
    rowb = sbA("rowb", [1, 3 * GT], F32)
    ltok = sbA("ltok", [128, 64], F32)
    lz = sbA("lz", [128, 64], F32)
    m8b = sbA("m8b", [128, 32], F32)
    nmx = sbA("nmx", [128, 4], F32)
    e4b = sbA("e4b", [128, 16], F32)
    p16b = sbA("p16b", [128, 64], F32)
    selb = sbA("selb", [128, 64], F32)
    scob = sbA("scob", [128, 64], F32)
    dnm = sbA("dnm", [128, 8], F32)
    kst = sbA("kst", [128, 256], F32)
    lraw = sbA("lraw", [16, GT], F32)
    esP = ExitStack()
    sbP = lambda name, shape, dt: esP.enter_context(nc.sbuf_tensor("sp_" + name, shape, dt))
    ptmp = sbP("ptmp", [128, 128], F32)
    grow = sbP("grow", [1, 16], F32)
    lnrow = sbP("lnrow", [1, 16], F32)
    lnbc = sbP("lnbc", [128, 16], F32)
    lnpair = sbP("lnpair", [128, 8], F32)
    gcol = sbP("gcol", [128, 1], F32)
    gCcol = sbP("gCcol", [128, 16], F32)
    gCrow = sbP("gCrow", [1, 16], F32)
    m64 = sbP("m64", [128, 2], F32)
    jrev = sbP("jrev", [128, 1], F32)
    dclamp = sbP("dclamp", [128, 128], BF16)
    mask01 = sbP("mask01", [128, 128], BF16)
    tposb = sbP("tposb", [128, 128], BF16)
    lgt = sbP("lgt", [1, 16], F32)

    ps_bc = esA.enter_context(nc.psum_tensor("ps_bc", [128, 1024], F32))
    ps_x0 = esA.enter_context(nc.psum_tensor("ps_x0", [128, 512], F32))
    ps_x1 = esA.enter_context(nc.psum_tensor("ps_x1", [128, 512], F32))
    ps_r0 = esA.enter_context(nc.psum_tensor("ps_r0", [128, 512], F32))
    ps_r1 = esA.enter_context(nc.psum_tensor("ps_r1", [128, 512], F32))
    ps_sd = esA.enter_context(nc.psum_tensor("ps_sd", [128, 512], F32))
    ps_y1 = esA.enter_context(nc.psum_tensor("ps_y1", [128, 512], F32))
    PSX = [ps_x0, ps_x1]
    PSR = [ps_r0, ps_r1]

    s_ld = sem("s_ld")
    s_ldw = sem("s_ldw")
    pg = sem("pg")
    p0s = sem("p0s")
    vself = sem("vself")
    vself_n = [0]

    def vg(v, ins):
        vself_n[0] += 1
        ins.then_inc(vself, 1)
        v.wait_ge(vself, vself_n[0])

    def emit_preamble(b):
        @b.sync
        def _(sync):
            n = 0
            for dst, src in [(iden, iden_h), (jrev, jrev_h), (m64, m64_h),
                             (flag, flag_h), (lgt, lg_h), (rb_sb, rb_h),
                             (rw_sb, rw_h)]:
                if dst is rw_sb:
                    sync.dma_start(rw_sb[:].rearrange("p (c o) -> p c o", c=8),
                                   rw_h.rearrange("(c p) o -> p c o", p=128)).then_inc(s_ld, 16)
                else:
                    sync.dma_start(dst[:], src).then_inc(s_ld, 16)
                n += 1

        @b.gpsimd
        def _(g):
            for dst, src in [(dclamp, dclamp_h), (mask01, mask01_h), (tposb, tposb_h)]:
                g.dma_start(dst[:].rearrange("p (a b) -> p a b", a=2),
                            src.rearrange("p (a b) -> p a b", a=2)).then_inc(s_ldw, 16)
            for dst, src in [(wk_sb, wk_h), (wv_sb, wv_h)]:
                g.dma_start(dst[:].rearrange("p (c o) -> p c o", c=8),
                            src.rearrange("(c p) o -> p c o", p=128)).then_inc(s_ldw, 16)

        @b.vector
        def _(v):
            v.wait_ge(s_ld, 7 * 16)
            v.wait_ge(s_ldw, 3 * 16)
            for t_, val in [(ones_r, 1.0), (ones_c, 1.0), (onesw, 1.0),
                            (eps_c, EPS), (jnk1, 0.0)]:
                v.memset(t_[:], val)
            v.tensor_copy(iden_bf[:], iden[:])
            v.memset(S_T[:], 0.0).then_inc(pg, 1)

            def vsp(semh, k=1):
                v.tensor_copy(jnk1[:], jnk1[:]).then_inc(semh, k)
            v.wait_ge(pg, 3)
            v.tensor_copy(lnbc[:], ps_bc[:, 0:16])
            vsp(p0s)
            v.wait_ge(p0s, 2)
            lnbc3 = lnbc[:].rearrange("p (a b) -> p a b", b=2)
            vg(v, v.tensor_scalar(lnpair[:], lnbc3[:, :, 0], m64[:, 0:1], None, op0=ALU.mult))
            vg(v, v.tensor_scalar(ptmp[:, 0:8], lnbc3[:, :, 1], m64[:, 1:2], None, op0=ALU.mult))
            vg(v, v.tensor_add(lnpair[:], lnpair[:], ptmp[:, 0:8]))
            vsp(p0s)
            v.wait_ge(p0s, 3)
            v.tensor_scalar(gCrow[:], lnrow[:], 128.0, None, op0=ALU.mult)
            vsp(pg)
            v.wait_ge(pg, 6)
            v.tensor_copy(gCcol[:], ps_bc[:, 16:32])
            vsp(p0s)
            v.wait_ge(p0s, 4)
            for h in range(NH):
                pb = 64 * (h % 2)
                v.tensor_scalar(gC[pb:pb + 64, (h // 2) * 64:(h // 2 + 1) * 64],
                                onesw[pb:pb + 64, :], gCcol[pb:pb + 64, h:h + 1], None,
                                op0=ALU.mult)
            v.nop().then_inc(pg, 1)
            TMP = xn_bf[:].bitcast(F32)
            for h in range(NH):
                v.tensor_scalar(TMP[:, h * 128:(h + 1) * 128], dclamp[:], lnbc[:, h:h + 1],
                                None, op0=ALU.mult)
            for p in range(8):
                v.tensor_scalar(TMP[:, 2048 + p * 128:2048 + (p + 1) * 128], tposb[:],
                                lnpair[:, p:p + 1], None, op0=ALU.mult)
            v.tensor_scalar(TMP[:, 3072:3088], lnbc[:], jrev[:], None, op0=ALU.mult)
            vsp(pg)
            v.wait_ge(pg, 9)
            for h in range(NH):
                v.tensor_mul(Dw[:, h * 128:(h + 1) * 128], TMP[:, h * 128:(h + 1) * 128],
                             mask01[:])
            v.tensor_copy(Gq[:], TMP[:, 2048:3072])
            for h in range(NH):
                v.tensor_scalar(Gk[:, h * 64:(h + 1) * 64], onesw[:], TMP[:, 3072 + h:3073 + h],
                                None, op0=ALU.mult)
            vg(v, v.tensor_mul(ptmp[:, 0:16], gCcol[:], gCcol[:]))
            vg(v, v.tensor_mul(ptmp[:, 16:32], ptmp[:, 0:16], gCcol[:]))
            Gk3 = Gk[:].rearrange("p (h d) -> p h d", d=64)
            for cc, fac in ((0, ptmp[:, 16:32]), (1, ptmp[:, 0:16]), (2, gCcol[:])):
                v.tensor_tensor(Kfm[:, cc * 1024:(cc + 1) * 1024].rearrange("p (h d) -> p h d", d=64),
                                Gk3, fac.unsqueeze(2).to_broadcast([128, NH, 64]), op=ALU.mult)

        @b.scalar
        def _(s):
            s.wait_ge(s_ld, 7 * 16)
            def asp(semh, k=1):
                s.copy(jnkA[:], iden[:]).then_inc(semh, k)

            s.activation(grow[:], lgt[:], AF.Sigmoid)
            asp(p0s)
            s.wait_ge(p0s, 1)
            s.activation(lnrow[:], grow[:], AF.Ln)
            asp(pg)
            s.wait_ge(pg, 4)
            s.activation(gCrow[:], gCrow[:], AF.Exp)
            asp(pg)
            TMPs = xn_bf[:].bitcast(F32)
            s.wait_ge(pg, 8)
            s.activation(TMPs[:, 0:2048], TMPs[:, 0:2048], AF.Exp)
            s.activation(TMPs[:, 2048:3088], TMPs[:, 2048:3088], AF.Exp)
            asp(pg)

        @b.tensor
        def _(t):
            t.wait_ge(pg, 2)
            t.matmul(ps_bc[:, 0:16], ones_r[:], lnrow[:], start=True, stop=True).then_inc(pg, 1)
            t.wait_ge(pg, 5)
            t.matmul(ps_bc[:, 16:32], ones_r[:], gCrow[:], start=True, stop=True).then_inc(pg, 1)

    ATTN_SEM_NAMES = ("sx sst a_sq p_st d_rw a_rs p_bc d_xn p_rl d_lg p_lt r1 r2 r3 r4 r5 "
                      "a_e4 d_dn a_p16 d_sc p_sT d_sE p_q d_q p_k d_ke a_kq d_ms d_st2 a_kr "
                      "d_kl p_kt d_kf p_v d_ve d_qk d_sb p_A d_ad p_y d_ye p_sd d_su p_wo d_r a_l1 a_l2 d_rdy")
    ATTN_SEMS = {n: sem("at_" + n) for n in ATTN_SEM_NAMES.split()}
    PREFIX_TOTALS = dict(sx=16 * NG, a_sq=NG, p_st=NG, d_rw=NG, a_rs=NG, p_bc=NG, d_xn=NG,
                         p_k=8 * NG, d_ke=8 * NG, a_kq=8 * NG, d_ms=8 * NG, d_st2=NG,
                         a_kr=NG, d_kl=4 * NG, p_v=8 * NG, d_ve=8 * NG, d_qk=4 * NG,
                         p_sd=NG, d_su=NG, a_l1=NG, a_l2=NG, d_rdy=NG)

    def attn_phase(b, xsrc, prefix, lvl=5):
        S = ATTN_SEMS
        BASE = {} if prefix else PREFIX_TOTALS
        L2 = (not prefix) and lvl >= 2
        L3 = (not prefix) and lvl >= 3
        L4 = (not prefix) and lvl >= 4
        L5 = (not prefix) and lvl >= 5
        xnT = QY if prefix else xn_bf
        PKB = [kg[:], qg[:], adb[:, 0:1024], adb[:, 1024:2048]]
        NMU, VAR, RSTD = 0, GT, 2 * GT

        def xn_col(fc, g):
            return fc * (GT if prefix else T) + (0 if prefix else g * GT)

        if True:
            @b.sync
            def _(sync):
                def w(s, v):
                    v += BASE.get(s, 0)
                    if v > 0:
                        sync.wait_ge(S[s], v)

                def xload(g):
                    if g > 0:
                        w("d_xn", g)
                        if L5:
                            w("p_st", g)
                    elif not prefix:
                        w("d_xn", 0)
                    sync.dma_start(
                        xg[:].rearrange("p (c t) -> p c t", c=8),
                        xsrc[:, g * GT:(g + 1) * GT].rearrange("(c p) t -> p c t", p=128),
                    ).then_inc(S["sx"], 16)

                def rstore(g):
                    rstage = Kfm[:].bitcast(F32)
                    for hh in range(2):
                        w("d_r", 8 * g + 4 * (hh + 1))
                        sync.dma_start(rscr[:, g * 4096 + hh * 2048:g * 4096 + (hh + 1) * 2048],
                                       rstage[:, 0:2048]).then_inc(S["sst"], 16)

                for g in range(NG):
                    xload(g)
                    if L5 and g >= 1:
                        rstore(g - 1)
                if L5:
                    rstore(NG - 1)
                    w("sst", 32 * NG)

            @b.tensor
            def _(t):
                def w(s, v):
                    v += BASE.get(s, 0)
                    if v > 0:
                        t.wait_ge(S[s], v)

                def st_xsum(g):
                    if prefix and g == 0:
                        t.wait_ge(p0s, 4)
                    w("sx", 16 * (g + 1)); w("d_ms", 8 * g)
                    if g == 0:
                        w("d_xn", 0)
                        if L2:
                            w("d_lg", 0)
                    for fc in range(8):
                        t.matmul(ps_bc[0:1, 0:512], ones_c[:], xg[:, fc * GT:(fc + 1) * GT],
                                 start=(fc == 0), stop=(fc == 7))

                def st_stats(g):
                    w("a_sq", g + 1); w("d_ms", 8 * g); w("d_xn", g)
                    for fc in range(8):
                        ins = t.matmul(ps_bc[0:1, 512:1024], ones_c[:], scr[:, fc * GT:(fc + 1) * GT],
                                       start=(fc == 0), stop=(fc == 7))
                    ins.then_inc(S["p_st"], 1)

                def st_bcast(g):
                    w("a_rs", g + 1)
                    t.matmul(ps_bc[:, 0:512], ones_r[:], rowb[0:1, 0:GT], start=True, stop=True)
                    t.matmul(ps_bc[:, 512:1024], ones_r[:], rowb[0:1, 2 * GT:3 * GT],
                             start=True, stop=True).then_inc(S["p_bc"], 1)

                def st_router(g):
                    w("d_xn", g + 1); w("d_lg", g)
                    if L4:
                        w("d_ad", 8 * g - 3)
                    if L3:
                        w("d_ke", 8 * g - 1)
                    for fc in range(8):
                        ins = t.matmul(ps_x0[0:16, :], rw_sb[:, fc * 16:(fc + 1) * 16],
                                       scr[:, fc * GT:(fc + 1) * GT],
                                       start=(fc == 0), stop=(fc == 7))
                    ins.then_inc(S["p_rl"], 1)

                def st_ltrans(g):
                    w("d_lg", g + 1); w("r1", g)
                    if L4 or prefix:
                        w("d_su", 4 * g)
                    for j in range(4):
                        ins = t.transpose(ps_sd[:, j * 16:(j + 1) * 16],
                                          lraw[:, j * 128:(j + 1) * 128], iden[0:16, 0:16])
                    ins.then_inc(S["p_lt"], 1)

                def st_q(g):
                    for ob in range(8):
                        if ob == 0:
                            if g == 0:
                                t.wait_ge(s_ldw, 6 * 16)
                            w("d_lg", g + 1)
                            if L5:
                                w("d_r", 8 * g - 1)
                            if L4:
                                w("d_ad", 8 * g - 1)
                            w("d_ke", 8 * g - 1)
                        elif ob == 1:
                            if L5:
                                w("d_r", 8 * g)
                            if L4:
                                w("d_ad", 8 * g)
                            w("d_ke", 8 * g)
                        else:
                            w("d_q", 8 * g + ob - 1)
                        for fc in range(8):
                            ins = t.matmul(PSX[ob % 2][:], wq_sb[:, fc * D + ob * 128:fc * D + (ob + 1) * 128],
                                           xn_bf[:, fc * T + g * GT:fc * T + (g + 1) * GT],
                                           start=(fc == 0), stop=(fc == 7))
                        ins.then_inc(S["p_q"], 1)

                def st_k(g):
                    for idx in range(8):
                        j, nb = idx // 2, idx % 2
                        if prefix:
                            if g == 0 and idx == 0:
                                t.wait_ge(s_ldw, 4 * 16)
                            w("d_ke", 8 * g + idx - 1)
                            if idx < 2:
                                w("d_xn", g + 1)
                        else:
                            if idx == 0:
                                w("d_q", 8 * g + 7)
                            elif idx == 1:
                                w("d_q", 8 * g + 8)
                            else:
                                w("d_ke", 8 * g + idx - 1)
                        for fc in range(8):
                            ins = t.matmul(PSX[idx % 2][:],
                                           xnT[:, xn_col(fc, g) + j * 128:xn_col(fc, g) + (j + 1) * 128],
                                           wk_sb[:, fc * D + nb * 512:fc * D + (nb + 1) * 512],
                                           start=(fc == 0), stop=(fc == 7))
                        ins.then_inc(S["p_k"], 1)

                def st_v(g):
                    for idx in range(8):
                        j, nb = idx // 2, idx % 2
                        if prefix and g == 0 and idx == 0:
                            t.wait_ge(s_ldw, 5 * 16)
                        if idx < 2 and not prefix:
                            if L4:
                                w("d_ad", 8 * g - 1 + idx)
                            elif L3:
                                w("d_kf", 8 * g - 1 + idx)
                        w("d_ve", 8 * g + idx - 1)
                        for fc in range(8):
                            ins = t.matmul(PSR[idx % 2][:],
                                           xnT[:, xn_col(fc, g) + j * 128:xn_col(fc, g) + (j + 1) * 128],
                                           wv_sb[:, fc * D + nb * 512:fc * D + (nb + 1) * 512],
                                           start=(fc == 0), stop=(fc == 7))
                        ins.then_inc(S["p_v"], 1)

                def st_sct(g):
                    w("d_sc", g + 1); w("d_ke", 8 * g + 8)
                    for j in range(4):
                        ins = t.transpose(ps_x1[0:16, j * 128:(j + 1) * 128],
                                          scob[:, j * 16:(j + 1) * 16], iden[:])
                    ins.then_inc(S["p_sT"], 1)

                def st_kt(g):
                    for i in range(8):
                        sub, half = i // 2, i % 2
                        w("d_kl", 4 * g + sub + 1)
                        if i == 0:
                            w("d_ve", 8 * g + 7)
                        elif i == 1:
                            w("d_ve", 8 * g + 8)
                        else:
                            w("d_kf", 8 * g + i - 1)
                        dstb = PSR[i % 2][:].bitcast(BF16)
                        for tt in range(4):
                            p = half * 4 + tt
                            ins = t.transpose(dstb[:, tt * 128:(tt + 1) * 128],
                                              Klnb[:, sub * D + p * 128:sub * D + (p + 1) * 128],
                                              iden_bf[:])
                        ins.then_inc(S["p_kt"], 1)

                def st_ret(g, cc):
                    w("d_sb", 4 * g + cc + 1)
                    w("d_qk", 4 * g + cc + 1)
                    w("d_kf", 8 * g + 8)
                    w("d_q", 8 * g + 8)
                    w("d_rdy", g + 1)
                    if cc == 0:
                        w("d_ke", 8 * g + 8)
                        w("d_sE", g + 1)
                    else:
                        w("d_ad", 8 * g + 2 * cc)
                    for p in range(8):
                        for hh in range(2):
                            dst = (PSX if hh == 0 else PSR)[p // 4]
                            ins = t.matmul(dst[:, (p % 4) * 128:(p % 4) * 128 + 128],
                                           Kfm[64 * hh:64 * hh + 64,
                                               p * GT + cc * 128:p * GT + (cc + 1) * 128],
                                           QY[64 * hh:64 * hh + 64,
                                              p * GT + cc * 128:p * GT + (cc + 1) * 128],
                                           start=True, stop=True)
                    ins.then_inc(S["p_A"], 1)
                    for r in range(2):
                        w("d_ad", 8 * g + 2 * cc + r + 1)
                        w("d_ye", 8 * g + 2 * cc + r)
                        for p in range(4 * r, 4 * r + 4):
                            for hh in range(2):
                                h = 2 * p + hh
                                t.matmul(ps_y1[64 * hh:64 * hh + 64, (p % 4) * 128:(p % 4) * 128 + 128],
                                         Vb[:, cc * D + h * 64:cc * D + (h + 1) * 64],
                                         adb[:, h * 128:(h + 1) * 128],
                                         start=True, stop=False)
                                ins = t.matmul(ps_y1[64 * hh:64 * hh + 64, (p % 4) * 128:(p % 4) * 128 + 128],
                                               S_bf[64 * hh:64 * hh + 64, p * 64:(p + 1) * 64],
                                               qg[64 * hh:64 * hh + 64, p * 128:(p + 1) * 128],
                                               start=False, stop=True)
                        ins.then_inc(S["p_y"], 1)
                    w("d_su", 4 * g + cc)
                    for h in range(NH):
                        ins = t.matmul(ps_sd[64 * (h % 2):64 * (h % 2) + 64,
                                             (h // 2) * 64:(h // 2 + 1) * 64],
                                       kg[:, h * 64:(h + 1) * 64],
                                       Vb[:, cc * D + h * 64:cc * D + (h + 1) * 64],
                                       start=True, stop=True)
                    ins.then_inc(S["p_sd"], 1)

                def st_sdall(g):
                    w("d_qk", 4 * g + 4)
                    w("d_ve", 8 * g + 8)
                    w("d_rdy", g + 1)
                    w("d_su", g)
                    for h in range(NH):
                        for cc in range(4):
                            kb = PKB[cc]
                            ins = t.matmul(ps_sd[64 * (h % 2):64 * (h % 2) + 64,
                                                 (h // 2) * 64:(h // 2 + 1) * 64],
                                           kb[:, h * 64:(h + 1) * 64],
                                           Vb[:, cc * D + h * 64:cc * D + (h + 1) * 64],
                                           start=(cc == 0), stop=(cc == 3))
                    ins.then_inc(S["p_sd"], 1)

                def st_warm(g):
                    for q in range(8):
                        w("a_kq", 8 * g + q + 1)
                        t.matmul(ps_y1[0:16, 0:16], ones_r[0:1, 0:16], ones_r[0:1, 0:16],
                                 start=True, stop=True)
                    for q in range(8):
                        w("d_ms", 8 * g + q + 1)
                        t.matmul(ps_y1[0:16, 0:16], ones_r[0:1, 0:16], ones_r[0:1, 0:16],
                                 start=True, stop=True)
                    for j in range(4):
                        w("d_kl", 4 * g + j + 1)
                        t.matmul(ps_y1[0:16, 0:16], ones_r[0:1, 0:16], ones_r[0:1, 0:16],
                                 start=True, stop=True)

                def st_wo(g):
                    w("d_ye", 8 * (g + 1))
                    for oc in range(8):
                        if oc == 0:
                            if g == 0:
                                t.wait_ge(s_ldw, 7 * 16)
                            w("d_ad", 8 * g + 7)
                        elif oc == 1:
                            w("d_ad", 8 * g + 8)
                        else:
                            w("d_r", 8 * g + oc - 1)
                        for yc in range(8):
                            ins = t.matmul(PSX[oc % 2][:],
                                           wo_sb[:, yc * D + oc * 128:yc * D + (oc + 1) * 128],
                                           QY[:, yc * GT:(yc + 1) * GT],
                                           start=(yc == 0), stop=(yc == 7))
                        ins.then_inc(S["p_wo"], 1)

                for g in range(NG):
                    bk = g >= 1
                    if prefix:
                        if g == 0:
                            st_xsum(0)
                        st_stats(g)
                        if bk:
                            st_sdall(g - 1)
                        st_bcast(g)
                        st_k(g)
                        st_v(g)
                        st_warm(g)
                        if g + 1 < NG:
                            st_xsum(g + 1)
                    else:
                        if g == 0:
                            st_xsum(0)
                        if bk and L4:
                            st_ret(g - 1, 0)
                        st_stats(g)
                        if bk and L4:
                            st_ret(g - 1, 1)
                        st_bcast(g)
                        if bk and L4:
                            st_ret(g - 1, 2)
                        if L2:
                            st_router(g)
                        if bk and L4:
                            st_ret(g - 1, 3)
                        if L2:
                            st_ltrans(g)
                        if bk and L5:
                            st_wo(g - 1)
                        if L3:
                            st_q(g)
                            st_k(g)
                            st_v(g)
                            st_warm(g)
                            if g + 1 < NG:
                                st_xsum(g + 1)
                        if L2:
                            st_sct(g)
                        if L3:
                            st_kt(g)
                if prefix:
                    st_sdall(NG - 1)
                else:
                    if L4:
                        for cc in range(4):
                            st_ret(NG - 1, cc)
                    if L5:
                        st_wo(NG - 1)

            @b.vector
            def _(v):
                def w(s, val):
                    val += BASE.get(s, 0)
                    if val > 0:
                        v.wait_ge(S[s], val)

                def sp(sname, k=1):
                    v.tensor_copy(jnk1[:], jnk1[:]).then_inc(S[sname], k)

                def newton_rsqrt(dst, varap, scr_f32):
                    vg(v, v.tensor_scalar(varap, varap, EPS, None, op0=ALU.add))
                    iv = varap.bitcast(I32)
                    si = scr_f32.bitcast(I32)
                    vg(v, v.tensor_scalar(si, iv, 1, None, op0=ALU.arith_shift_right))
                    vg(v, v.tensor_scalar(si, si, -1, None, op0=ALU.mult))
                    vg(v, v.tensor_scalar(si, si, 0x5F3759DF, None, op0=ALU.add))
                    for _ in range(2):
                        vg(v, v.tensor_mul(dst, scr_f32, scr_f32))
                        vg(v, v.tensor_mul(dst, dst, varap))
                        vg(v, v.tensor_scalar(dst, dst, -0.5, 1.5, op0=ALU.mult, op1=ALU.add))
                        vg(v, v.tensor_mul(scr_f32, scr_f32, dst))
                    ins = v.tensor_copy(dst, scr_f32)
                    vg(v, ins)
                    return ins
                def dv_rw(g):
                    mus = jnk1[0:1, 0:GT]
                    w("p_st", g + 1); w("p_bc", g)
                    vg(v, v.tensor_scalar(mus, ps_bc[0:1, 0:512], 1.0 / D, None, op0=ALU.mult))
                    vg(v, v.tensor_scalar(rowb[0:1, NMU:NMU + GT], mus, -1.0, None, op0=ALU.mult))
                    vg(v, v.tensor_mul(mus, mus, mus))
                    vg(v, v.tensor_scalar(rowb[0:1, VAR:VAR + GT], ps_bc[0:1, 512:1024], 1.0 / D, None, op0=ALU.mult))
                    vg(v, v.tensor_sub(rowb[0:1, VAR:VAR + GT], rowb[0:1, VAR:VAR + GT], mus))
                    if prefix:
                        sp("d_rw")
                    else:
                        newton_rsqrt(rowb[0:1, RSTD:RSTD + GT], rowb[0:1, VAR:VAR + GT], mus)
                        sp("a_rs")

                def dv_xn(g):
                    w("p_bc", g + 1)
                    if L2:
                        w("p_rl", g)
                    if prefix:
                        w("p_v", 8 * g)
                        w("p_k", 8 * g)
                    for fc in range(8):
                        v.tensor_add(scr[:, fc * GT:(fc + 1) * GT], xg[:, fc * GT:(fc + 1) * GT],
                                     ps_bc[:, 0:512])
                        v.tensor_mul(xnT[:, xn_col(fc, g):xn_col(fc, g) + GT],
                                     scr[:, fc * GT:(fc + 1) * GT], ps_bc[:, 512:1024])
                    sp("d_xn")

                def dv_lga(g):
                    w("p_rl", g + 1); w("p_lt", g)
                    vg(v, v.tensor_copy(lraw[:], ps_x0[0:16, :]))
                    vg(v, v.tensor_mul(lraw[:], lraw[:], ps_bc[0:16, 512:1024]))
                    vg(v, v.tensor_scalar(lraw[:], lraw[:], rb_sb[:], None, op0=ALU.add))
                    sp("d_lg")

                def dv_lgb(g):
                    w("p_lt", g + 1); w("a_e4", g); w("a_p16", g)
                    for j in range(4):
                        ins = v.tensor_copy(ltok[:, j * 16:(j + 1) * 16], ps_sd[:, j * 16:(j + 1) * 16])
                    ins.then_inc(S["r1"], 1)
                    v.wait_ge(S["r1"], g + 1)
                    for j in range(4):
                        ins = v.max(out=m8b[:, j * 8:(j + 1) * 8], in_=ltok[:, j * 16:(j + 1) * 16])
                    ins.then_inc(S["r2"], 1)
                    v.wait_ge(S["r2"], g + 1)
                    for j in range(4):
                        ins = v.memset(m8b[:, j * 8 + 4:(j + 1) * 8], NEG)
                    ins.then_inc(S["r3"], 1)
                    v.wait_ge(S["r3"], g + 1)
                    for j in range(4):
                        ins = v.match_replace(out=lz[:, j * 16:(j + 1) * 16],
                                              in_to_replace=m8b[:, j * 8:(j + 1) * 8],
                                              in_values=ltok[:, j * 16:(j + 1) * 16],
                                              imm_value=NEG)
                    ins.then_inc(S["r4"], 1)
                    v.wait_ge(S["r4"], g + 1)
                    for j in range(4):
                        ins = v.tensor_scalar(nmx[:, j:j + 1], m8b[:, j * 8:j * 8 + 1], -1.0, None,
                                              op0=ALU.mult)
                    ins.then_inc(S["r5"], 1)
                    for j in range(4):
                        ins = v.tensor_tensor(selb[:, j * 16:(j + 1) * 16], lz[:, j * 16:(j + 1) * 16],
                                              ltok[:, j * 16:(j + 1) * 16], op=ALU.not_equal)
                    vg(v, ins)

                def dv_lgc(g):
                    w("a_e4", g + 1)
                    for j in range(4):
                        ins = v.reduce_sum(dnm[:, j:j + 1], e4b[:, j * 4:(j + 1) * 4], axis=AX)
                    ins.then_inc(S["d_dn"], 1)
                    v.wait_ge(S["d_dn"], 2 * g + 1)
                    for j in range(4):
                        ins = v.reciprocal(dnm[:, 4 + j:5 + j], dnm[:, j:j + 1])
                    ins.then_inc(S["d_dn"], 1)
                    w("a_p16", g + 1); w("p_sT", g)
                    v.wait_ge(S["d_dn"], 2 * g + 2)
                    for j in range(4):
                        v.scalar_tensor_tensor(scob[:, j * 16:(j + 1) * 16],
                                               p16b[:, j * 16:(j + 1) * 16],
                                               dnm[:, 4 + j:5 + j],
                                               selb[:, j * 16:(j + 1) * 16],
                                               op0=ALU.mult, op1=ALU.mult)
                    sp("d_sc")

                def dv_kln(g):
                    for j in range(4):
                        w("d_ke", 8 * g + 2 * j + 2)
                        v.reduce_sum(kst[:, j * 16:(j + 1) * 16],
                                     scr[:, j * D:(j + 1) * D].rearrange("p (h d) -> p h d", d=64),
                                     axis=AX)
                        for hf in range(2):
                            q = 2 * j + hf
                            w("a_kq", 8 * g + q + 1)
                            c0 = 64 + j * 16 + hf * 8
                            ins = v.reduce_sum(kst[:, c0:c0 + 8],
                                               ps_bc[:, (q % 2) * 512:(q % 2) * 512 + 512]
                                               .rearrange("p (h d) -> p h d", d=64), axis=AX)
                            ins.then_inc(S["d_ms"], 1)
                    v.wait_ge(S["d_ms"], 8 * g + 8 + BASE.get("d_ms", 0))
                    vg(v, v.tensor_scalar(kst[:, 128:192], kst[:, 0:64], 1.0 / 64, None, op0=ALU.mult))
                    vg(v, v.tensor_mul(kst[:, 0:64], kst[:, 128:192], kst[:, 128:192]))
                    vg(v, v.tensor_scalar(kst[:, 64:128], kst[:, 64:128], 1.0 / 64, None, op0=ALU.mult))
                    vg(v, v.tensor_sub(kst[:, 64:128], kst[:, 64:128], kst[:, 0:64]))
                    sp("a_kr")
                    v.wait_ge(S["d_st2"], g + 1 + BASE.get("d_st2", 0))
                    if not prefix:
                        w("p_kt", 8 * g)
                    for j in range(4):
                        K3 = scr[:, j * D:(j + 1) * D].rearrange("p (h d) -> p h d", d=64)
                        KB3 = Klnb[:, j * D:(j + 1) * D].rearrange("p (h d) -> p h d", d=64)
                        mu3 = kst[:, 128 + j * 16:128 + (j + 1) * 16].unsqueeze(2).to_broadcast([128, 16, 64])
                        rs3 = kst[:, 192 + j * 16:192 + (j + 1) * 16].unsqueeze(2).to_broadcast([128, 16, 64])
                        v.tensor_tensor(K3, K3, mu3, op=ALU.subtract)
                        v.tensor_tensor(KB3, K3, rs3, op=ALU.mult).then_inc(S["d_kl"], 1)

                def dv_ret(g, cc):
                    w("p_sd", 4 * g + cc)
                    w("p_y", 8 * g + 2 * cc)
                    w("d_q", 8 * g + 8)
                    w("d_rdy", g + 1)
                    v.tensor_mul(kg[:], Klnb[:, cc * D:(cc + 1) * D], Gk[:])
                    v.tensor_mul(qg[:].rearrange("p (pr t) -> p pr t", t=128),
                                 QY[:].rearrange("p (pr t) -> p pr t", t=GT)[:, :, cc * 128:(cc + 1) * 128],
                                 Gq[:].rearrange("p (pr t) -> p pr t", t=128))
                    sp("d_qk")
                    v.tensor_copy(S_bf[:], S_T[:])
                    sp("d_sb")
                    w("p_A", 4 * g + cc + 1)
                    adb3 = adb[:].rearrange("p (pr two t) -> p pr two t", two=2, t=128)
                    Dw3 = Dw[:].rearrange("p (pr two t) -> p pr two t", two=2, t=128)
                    for half in range(2):
                        v.tensor_tensor(adb3[:, 4 * half:4 * half + 4, 0, :],
                                        PSX[half][:].rearrange("p (s t) -> p s t", t=128),
                                        Dw3[:, 4 * half:4 * half + 4, 0, :], op=ALU.mult)
                        v.tensor_tensor(adb3[:, 4 * half:4 * half + 4, 1, :],
                                        PSR[half][:].rearrange("p (s t) -> p s t", t=128),
                                        Dw3[:, 4 * half:4 * half + 4, 1, :], op=ALU.mult)
                        sp("d_ad")
                    w("p_sd", 4 * g + cc + 1)
                    v.tensor_mul(S_T[:], S_T[:], gC[:])
                    v.tensor_add(S_T[:], S_T[:], ps_sd[:]).then_inc(S["d_su"], 1)

                def dv_sd(g, cc):
                    kb = PKB[cc]
                    w("p_sd", g)
                    w("d_rdy", g + 1)
                    gkp = Gk[:] if cc == 3 else Kfm[:, cc * 1024:(cc + 1) * 1024]
                    v.tensor_mul(kb[:], Klnb[:, cc * D:(cc + 1) * D], gkp)
                    sp("d_qk")

                def dv_sadd(g):
                    w("p_sd", g + 1)
                    for _ in range(4):
                        v.tensor_mul(S_T[:], S_T[:], gC[:])
                    v.tensor_add(S_T[:], S_T[:], ps_sd[:]).then_inc(S["d_su"], 1)

                for g in range(NG):
                    bk = g >= 1
                    if prefix:
                        if bk:
                            dv_sd(g - 1, 0)
                        dv_rw(g)
                        if bk:
                            dv_sd(g - 1, 1)
                            dv_sd(g - 1, 2)
                            dv_sd(g - 1, 3)
                        dv_xn(g)
                        if bk:
                            dv_sadd(g - 1)
                        dv_kln(g)
                    else:
                        if bk and L4:
                            dv_ret(g - 1, 0)
                            dv_ret(g - 1, 1)
                        dv_rw(g)
                        if bk and L4:
                            dv_ret(g - 1, 2)
                        dv_xn(g)
                        if L2:
                            dv_lga(g)
                        if bk and L4:
                            dv_ret(g - 1, 3)
                        if L3:
                            dv_kln(g)
                        if L2:
                            dv_lgb(g)
                            dv_lgc(g)
                if prefix:
                    for cc in range(4):
                        dv_sd(NG - 1, cc)
                    dv_sadd(NG - 1)
                    v.tensor_scalar(S_T[:], S_T[:], flag[:], None, op0=ALU.mult)
                elif L4:
                    for cc in range(4):
                        dv_ret(NG - 1, cc)

            if prefix:
                @b.gpsimd
                def _(gp):
                    for dst, srch in [(wq_sb, wq_h), (wo_sb, wo_h)]:
                        gp.dma_start(dst[:].rearrange("p (c o) -> p c o", c=8),
                                     srch.rearrange("(c p) o -> p c o", p=128)).then_inc(s_ldw, 16)
                    gp.wait_ge(s_ldw, 7 * 16)

            @b.scalar
            def _(s):
                def w(sn, val):
                    val += BASE.get(sn, 0)
                    if val > 0:
                        s.wait_ge(S[sn], val)

                def asp(sname, k=1):
                    s.copy(jnkA[:], iden[:]).then_inc(S[sname], k)

                def ac_rw(g):
                    w("d_rw", g + 1)
                    s.activation(rowb[0:1, VAR:VAR + GT], rowb[0:1, VAR:VAR + GT], AF.Ln,
                                 bias=eps_c[0:1, :])
                    s.copy(jnkA[:], iden[:])
                    s.activation(rowb[0:1, RSTD:RSTD + GT], rowb[0:1, VAR:VAR + GT], AF.Exp,
                                 scale=-0.5)
                    asp("a_rs")

                def ac_krstd(g):
                    w("a_kr", g + 1)
                    s.activation(kst[:, 64:128], kst[:, 64:128], AF.Ln, bias=eps_c[:])
                    s.copy(jnkA[:], iden[:])
                    s.activation(kst[:, 192:256], kst[:, 64:128], AF.Exp, scale=-0.5)
                    asp("d_st2")

                def ac_sq(g):
                    w("sx", 16 * (g + 1))
                    if not (prefix or L3):
                        if L2:
                            w("p_rl", g)
                        w("p_st", g)
                        s.activation(scr[:], xg[:], AF.Square).then_inc(S["a_sq"], 1)
                        return
                    w("p_st", g)
                    for j in range(4):
                        w("d_kl", 4 * (g - 1) + j + 1)
                        ins = s.activation(scr[:, j * D:(j + 1) * D], xg[:, j * D:(j + 1) * D],
                                           AF.Square)
                    ins.then_inc(S["a_sq"], 1)

                def ac_yev(g, cc, r):
                    w("p_y", 8 * g + 2 * cc + r + 1)
                    ins = s.copy(
                        QY[:].rearrange("p (pr t) -> p pr t", t=GT)[:, 4 * r:4 * r + 4, cc * 128:(cc + 1) * 128],
                        ps_y1[:].rearrange("p (s t) -> p s t", t=128))
                    if cc == 3 and r == 1:
                        asp("d_ye")
                    else:
                        ins.then_inc(S["d_ye"], 1)

                def ac_rev(g):
                    rstage = Kfm[:].bitcast(F32)
                    for oc in range(8):
                        w("p_wo", 8 * g + oc + 1)
                        if oc == 4:
                            w("sst", 32 * g + 16)
                        ins = s.copy(rstage[:, (oc % 4) * 512:(oc % 4 + 1) * 512], PSX[oc % 2][:])
                        if (oc + 1) % 4 != 0:
                            ins.then_inc(S["d_r"], 1)
                        else:
                            asp("d_r")

                def ac_e4p16(g):
                    w("r5", g + 1); w("d_dn", 2 * g)
                    for j in range(4):
                        ins = s.activation(e4b[:, j * 4:(j + 1) * 4], m8b[:, j * 8:j * 8 + 4],
                                           AF.Exp, bias=nmx[:, j:j + 1])
                    ins.then_inc(S["a_e4"], 1)
                    w("d_sc", g)
                    for j in range(4):
                        ins = s.activation(p16b[:, j * 16:(j + 1) * 16], ltok[:, j * 16:(j + 1) * 16],
                                           AF.Exp, bias=nmx[:, j:j + 1])
                    ins.then_inc(S["a_p16"], 1)

                def ac_qev(g):
                    w("p_wo", 8 * g)
                    for ob in range(8):
                        w("p_q", 8 * g + ob + 1)
                        ins = s.copy(QY[:, ob * GT:(ob + 1) * GT], PSX[ob % 2][:])
                        if ob < 7:
                            ins.then_inc(S["d_q"], 1)
                        else:
                            asp("d_q")

                def ac_kev(g):
                    for idx in range(8):
                        j, nb = idx // 2, idx % 2
                        w("p_k", 8 * g + idx + 1)
                        ins = s.copy(scr[:, j * D + nb * 512:j * D + (nb + 1) * 512],
                                     PSX[idx % 2][:])
                        if idx < 7:
                            ins.then_inc(S["d_ke"], 1)
                        else:
                            asp("d_ke")

                def ac_ksq(g):
                    for q in range(8):
                        j, hf = q // 2, q % 2
                        w("d_ms", 8 * g + q - 1)
                        s.activation(ps_bc[:, (q % 2) * 512:(q % 2) * 512 + 512],
                                     scr[:, j * D + hf * 512:j * D + hf * 512 + 512],
                                     AF.Square).then_inc(S["a_kq"], 1)

                def ac_vev(g):
                    if prefix:
                        w("p_sd", g)
                    elif L4:
                        w("p_sd", 4 * g)
                    if L4:
                        w("p_y", 8 * g)
                    for idx in range(8):
                        j, nb = idx // 2, idx % 2
                        w("p_v", 8 * g + idx + 1)
                        ins = s.copy(Vb[:, j * D + nb * 512:j * D + (nb + 1) * 512],
                                     PSR[idx % 2][:])
                        if idx < 7:
                            ins.then_inc(S["d_ve"], 1)
                        else:
                            asp("d_ve")

                def ac_scev(g):
                    w("p_sT", g + 1)
                    s.copy(scoresT[:, g * GT:(g + 1) * GT], ps_x1[0:16, 0:512]).then_inc(S["d_sE"], 1)

                def ac_ktev(g):
                    if L4:
                        w("p_A", 4 * g)
                    if L5:
                        w("sst", 32 * g)
                    for i in range(8):
                        sub, half = i // 2, i % 2
                        w("p_kt", 8 * g + i + 1)
                        s.copy(
                            Kfm[:].rearrange("p (pr t) -> p pr t", pr=8)
                                  [:, half * 4:(half + 1) * 4, sub * 128:(sub + 1) * 128],
                            PSR[i % 2][:].bitcast(BF16)[:, 0:512].rearrange("p (tt t) -> p tt t", tt=4),
                        ).then_inc(S["d_kf"], 1)

                for g in range(NG):
                    ac_sq(g)
                    if prefix:
                        ac_rw(g)
                    if g >= 1:
                        if L4:
                            for cc in range(4):
                                for r in range(2):
                                    ac_yev(g - 1, cc, r)
                        if L5:
                            ac_rev(g - 1)
                    if L3:
                        ac_qev(g)
                    if prefix or L3:
                        ac_kev(g)
                        ac_vev(g)
                        ac_ksq(g)
                        ac_krstd(g)
                    if L2:
                        ac_e4p16(g)
                        ac_scev(g)
                    if L3:
                        ac_ktev(g)
                    if prefix or L3:
                        asp("d_rdy")
                if L4:
                    for cc in range(4):
                        for r in range(2):
                            ac_yev(NG - 1, cc, r)
                if L5:
                    ac_rev(NG - 1)

        return S

    with nc.Block("PA") as blk:
        emit_preamble(blk)
        attn_phase(blk, xpf, prefix=True)
        attn_phase(blk, xf, prefix=False)
    esP.close()

    esA.close()

    esB = ExitStack()
    sbB = lambda name, shape, dt: esB.enter_context(nc.sbuf_tensor("sc_" + name, shape, dt))
    acc = sbB("acc", [128, 8 * T], F32)
    gub_sb = sbB("gub_sb", [128, 256], F32)
    dnb_sb = sbB("dnb_sb", [16, D], F32)
    esel_sb = sbB("esel_sb", [16, NE * 128], F32)
    act = sbB("act", [128, 8 * T], BF16)
    wgu_sb = [sbB(f"wgu{i}", [128, 8 * 256], BF16) for i in range(3)]
    wd_sb = [sbB(f"wd{i}", [128, 8 * 128], BF16) for i in range(2)]
    sco_bc = sbB("sco_bc", [128, T], BF16)
    g1b = [sbB(f"g1b{i}", [128, 1024], F32) for i in range(2)]
    sigb = [sbB(f"sigb{i}", [128, 1024], F32) for i in range(2)]
    u2b = [sbB(f"u2b{i}", [128, 1024], F32) for i in range(2)]
    glub = [sbB("glub0", [128, 1024], F32)] * 2

    esB2 = ExitStack()
    ps_a = esB2.enter_context(nc.psum_tensor("ps_a", [128, 1024], F32))
    ps_b = esB2.enter_context(nc.psum_tensor("ps_b", [128, 1024], F32))
    ps_c = esB2.enter_context(nc.psum_tensor("ps_c", [128, 1024], F32))
    ps_d = esB2.enter_context(nc.psum_tensor("ps_d", [128, 1024], F32))
    PGT = [ps_a, ps_c]
    PUP = [ps_b, ps_d]
    PDN = [ps_b, ps_d]
    PSC = [ps_a, ps_c]

    s_r3 = sem("s_r3")
    s_wg = sem("s_wg")
    s_wd = sem("s_wd")
    p_bias = sem("p_bias")
    d_bias = sem("d_bias")
    p_sc3 = sem("p_sc3")
    d_sb3 = sem("d_sb3")
    p_gu = sem("p_gu")
    d_gu = sem("d_gu")
    d_g1 = sem("d_g1")
    a_sg3 = sem("a_sg3")
    d_act = sem("d_act")
    p_dn3 = sem("p_dn3")
    d_dn3 = sem("d_dn3")

    with nc.Block("P3") as b:
        @b.sync
        def _(sync):
            sync.dma_start(dnb_sb[:], dnb_h).then_inc(s_r3, 16)
            sync.dma_start(esel_sb[:], esel_h).then_inc(s_r3, 16)
            sync.dma_start(gub_sb[:], gub_h).then_inc(s_r3, 16)
            for c_ in range(8):
                sync.dma_start(
                    acc[:, c_ * T:(c_ + 1) * T].rearrange("p (gg t) -> p gg t", gg=4),
                    rscr.rearrange("p (gg c t) -> p gg c t", gg=4, c=8)[:, :, c_, :],
                ).then_inc(s_r3, 16)

        @b.gpsimd
        def _(g):
            def wgu_load(e, j):
                idx = e * 8 + j
                if idx >= 3:
                    g.wait_ge(p_gu, 4 * idx - 8)
                g.dma_start(wgu_sb[idx % 3][:].rearrange("p (c f) -> p c f", c=8),
                            wgu_h[e, j].rearrange("(c p) f -> p c f", p=128)).then_inc(s_wg, 16)
            wgu_load(0, 0)
            for e in range(NE):
                for j in range(1, 8):
                    wgu_load(e, j)
                if e + 1 < NE:
                    wgu_load(e + 1, 0)
                for hb in range(8):
                    idx = e * 8 + hb
                    if idx >= 2:
                        g.wait_ge(p_dn3, 4 * (idx - 1))
                    g.dma_start(wd_sb[idx % 2][:].rearrange("p (c f) -> p c f", c=8),
                                wd_h[e, hb].rearrange("(c p) f -> p c f", p=128)).then_inc(s_wd, 16)

        @b.tensor
        def _(t):
            t.wait_ge(s_r3, 16)
            for ds in range(32):
                hb, st = ds // 4, ds % 4
                if ds >= 2:
                    t.wait_ge(d_bias, ds - 1)
                t.matmul(PDN[ds % 2][:, 0:512], dnb_sb[:, hb * 128:(hb + 1) * 128],
                         scoresT[:, st * 512:(st + 1) * 512], start=True, stop=True).then_inc(p_bias, 1)
            def score_bc(e):
                for st in range(4):
                    ins = t.matmul(PSC[st // 2][:, (st % 2) * 512:(st % 2 + 1) * 512],
                                   esel_sb[:, e * 128:(e + 1) * 128],
                                   scoresT[:, st * 512:(st + 1) * 512], start=True, stop=True)
                ins.then_inc(p_sc3, 1)

            for e in range(NE):
                if e == 0:
                    t.wait_ge(s_r3, 32)
                    score_bc(0)
                    t.wait_ge(d_bias, 32)
                for u in range(16):
                    j, hf = u // 2, u % 2
                    U = 16 * e + u
                    t.wait_ge(s_wg, 16 * (e * 8 + j + 1))
                    if U >= 2:
                        t.wait_ge(d_gu, U - 1)
                    if u <= 1:
                        t.wait_ge(d_sb3, e + 1)
                    for fc in range(8):
                        for s2 in range(2):
                            t.matmul(PGT[U % 2][:, s2 * 512:(s2 + 1) * 512],
                                     wgu_sb[(e * 8 + j) % 3][:, fc * 256:fc * 256 + 128],
                                     xn_bf[:, fc * T + hf * 1024 + s2 * 512:fc * T + hf * 1024 + (s2 + 1) * 512],
                                     start=(fc == 0), stop=(fc == 7))
                    if u <= 1:
                        if e == 0:
                            t.wait_ge(d_bias, 32)
                        else:
                            t.wait_ge(d_dn3, 32 * e)
                    for fc in range(8):
                        for s2 in range(2):
                            ins = t.matmul(PUP[U % 2][:, s2 * 512:(s2 + 1) * 512],
                                           wgu_sb[(e * 8 + j) % 3][:, fc * 256 + 128:fc * 256 + 256],
                                           xn_bf[:, fc * T + hf * 1024 + s2 * 512:fc * T + hf * 1024 + (s2 + 1) * 512],
                                           start=(fc == 0), stop=(fc == 7))
                    ins.then_inc(p_gu, 2)
                t.wait_ge(d_gu, 16 * (e + 1))
                if e + 1 < NE:
                    score_bc(e + 1)
                t.wait_ge(d_act, 16 * (e + 1))
                for ds in range(32):
                    hb, st = ds // 4, ds % 4
                    t.wait_ge(s_wd, 16 * (e * 8 + hb + 1))
                    gds = 32 * e + ds
                    if gds >= 2:
                        t.wait_ge(d_dn3, gds - 1)
                    for fb in range(8):
                        ins = t.matmul(PDN[ds % 2][:, 0:512],
                                       wd_sb[(e * 8 + hb) % 2][:, fb * 128:(fb + 1) * 128],
                                       act[:, fb * T + st * 512:fb * T + (st + 1) * 512],
                                       start=(fb == 0), stop=(fb == 7))
                    ins.then_inc(p_dn3, 1)

        @b.vector
        def _(v):
            for ds in range(32):
                hb, st = ds // 4, ds % 4
                v.wait_ge(s_r3, 48 + 16 * (hb + 1))
                v.wait_ge(p_bias, ds + 1)
                v.tensor_add(acc[:, hb * T + st * 512:hb * T + (st + 1) * 512],
                             acc[:, hb * T + st * 512:hb * T + (st + 1) * 512],
                             PDN[ds % 2][:, 0:512]).then_inc(d_bias, 1)
            for e in range(NE):
                def frontend(u):
                    U = 16 * e + u
                    j, hf = u // 2, u % 2
                    v.wait_ge(p_gu, 2 * U + 2)
                    if U == 0:
                        v.wait_ge(s_r3, 48)
                    if U >= 1:
                        v.wait_ge(a_sg3, U - 1)
                    v.tensor_scalar(g1b[u % 2][:], PGT[U % 2][:],
                                    gub_sb[:, e * 16 + 2 * j:e * 16 + 2 * j + 1], 7.0,
                                    op0=ALU.add, op1=ALU.min)
                    v.tensor_scalar(u2b[u % 2][:], PUP[U % 2][:],
                                    gub_sb[:, e * 16 + 2 * j + 1:e * 16 + 2 * j + 2], -7.0,
                                    op0=ALU.add, op1=ALU.max).then_inc(d_g1, 1)
                    ins = v.tensor_scalar(u2b[u % 2][:], u2b[u % 2][:], 7.0, None, op0=ALU.min)
                    ins.then_inc(d_gu, 1)

                def backend(u):
                    U = 16 * e + u
                    j, hf = u // 2, u % 2
                    v.wait_ge(a_sg3, U + 1)
                    if u == 0:
                        v.wait_ge(p_dn3, 32 * e)
                        v.wait_ge(d_sb3, e + 1)
                    v.tensor_mul(glub[u % 2][:], g1b[u % 2][:], sigb[u % 2][:])
                    v.scalar_tensor_tensor(act[:, j * T + hf * 1024:j * T + (hf + 1) * 1024],
                                           u2b[u % 2][:], 1.0, glub[u % 2][:],
                                           op0=ALU.add, op1=ALU.mult)
                    v.tensor_mul(act[:, j * T + hf * 1024:j * T + (hf + 1) * 1024],
                                 act[:, j * T + hf * 1024:j * T + (hf + 1) * 1024],
                                 sco_bc[:, hf * 1024:(hf + 1) * 1024]).then_inc(d_act, 1)

                frontend(0)
                for u in range(1, 16):
                    frontend(u)
                    backend(u - 1)
                backend(15)
                for ds in range(32):
                    hb, st = ds // 4, ds % 4
                    v.wait_ge(p_dn3, 32 * e + ds + 1)
                    v.tensor_add(acc[:, hb * T + st * 512:hb * T + (st + 1) * 512],
                                 acc[:, hb * T + st * 512:hb * T + (st + 1) * 512],
                                 PDN[ds % 2][:, 0:512]).then_inc(d_dn3, 1)

        @b.scalar
        def _(s):
            for e in range(NE):
                s.wait_ge(p_sc3, e + 1)
                if e >= 1:
                    s.wait_ge(d_act, 16 * e)
                s.copy(sco_bc[:, 0:1024], PSC[0][:])
                s.copy(sco_bc[:, 1024:2048], PSC[1][:]).then_inc(d_sb3, 1)
                for u in range(16):
                    U = 16 * e + u
                    s.wait_ge(d_g1, U + 1)
                    if U >= 1:
                        s.wait_ge(d_act, U - 1)
                    s.activation(sigb[u % 2][:], g1b[u % 2][:], AF.Sigmoid,
                                 scale=1.702).then_inc(a_sg3, 1)

    esB2.close()
    ps_t = [esB.enter_context(nc.psum_tensor(f"ps_t{i}", [128, 1024], F32)) for i in range(2)]
    outb = g1b
    p_mg = sem("p_mg")
    d_mg = sem("d_mg")
    s_out = sem("s_out")

    s_xp = sem("s_xp")
    xb = sigb

    with nc.Block("P4") as b:
        @b.tensor
        def _(t):
            for cc in range(16):
                if cc >= 2:
                    t.wait_ge(d_mg, cc - 1)
                for fc in range(8):
                    ins = t.transpose(ps_t[cc % 2][:, fc * 128:(fc + 1) * 128],
                                      acc[:, fc * T + cc * 128:fc * T + (cc + 1) * 128], iden[:])
                ins.then_inc(p_mg, 1)

        @b.vector
        def _(v):
            for cc in range(16):
                v.wait_ge(p_mg, cc + 1)
                v.wait_ge(s_xp, 16 * (cc + 1))
                if cc >= 2:
                    v.wait_ge(s_out, 16 * (cc - 1))
                v.tensor_add(outb[cc % 2][:], xb[cc % 2][:], ps_t[cc % 2][:]).then_inc(d_mg, 1)

        @b.sync
        def _(sync):
            for cc in range(18):
                if cc < 16:
                    if cc >= 2:
                        sync.wait_ge(d_mg, cc - 1)
                    sync.dma_start(xb[cc % 2][:], xtok_h[cc * 128:(cc + 1) * 128, :]).then_inc(s_xp, 16)
                if cc >= 2:
                    sync.wait_ge(d_mg, cc - 1)
                    sync.dma_start(y_h[(cc - 2) * 128:(cc - 1) * 128, :],
                                   outb[cc % 2][:]).then_inc(s_out, 16)
            sync.wait_ge(s_out, 16 * 16)

    esB.close()
    es.close()
    return nc



_NC = None


def _get_nc():
    global _NC
    if _NC is None:
        _NC = build_nc()
    return _NC


def _host_consts():
    i = np.arange(128)
    c = {}
    c["iden"] = np.eye(128, dtype=np.float32)
    c["dclamp"] = np.maximum(i[None, :] - i[:, None], 0).astype(np.float32)
    c["mask01"] = (i[None, :] >= i[:, None]).astype(np.float32)
    c["tposb"] = np.broadcast_to((i + 1)[None, :], (128, 128)).astype(np.float32).copy()
    c["jrev"] = (127 - i)[:, None].astype(np.float32)
    m64 = np.zeros((128, 2), np.float32)
    m64[:64, 0] = 1.0
    m64[64:, 1] = 1.0
    c["m64"] = m64
    esel = np.zeros((16, NE * 128), np.float32)
    for e in range(NE):
        esel[e, e * 128:(e + 1) * 128] = 1.0
    c["esel"] = esel
    return c


def kernel(L0_attn_x, L0_attn_W_qkv, L0_attn_W_o, L0_attn_log_gamma,
           L0_ffn_router_w, L0_ffn_router_b, L0_ffn_gate_up_proj, L0_ffn_gate_up_bias,
           L0_ffn_down_proj, L0_ffn_down_bias):
    x = np.asarray(L0_attn_x, np.float32)
    Wqkv = np.asarray(L0_attn_W_qkv, np.float32)
    Wo = np.asarray(L0_attn_W_o, np.float32)
    lg = np.asarray(L0_attn_log_gamma, np.float32)
    rw = np.asarray(L0_ffn_router_w, np.float32)
    rb = np.asarray(L0_ffn_router_b, np.float32)
    gup = np.asarray(L0_ffn_gate_up_proj, np.float32)
    gub = np.asarray(L0_ffn_gate_up_bias, np.float32)
    wdp = np.asarray(L0_ffn_down_proj, np.float32)
    dnb = np.asarray(L0_ffn_down_bias, np.float32)

    consts = _host_consts()
    wq = np.ascontiguousarray(Wqkv[0:1024].T)
    wk = np.ascontiguousarray(Wqkv[1024:2048].T)
    wv = np.ascontiguousarray(Wqkv[2048:3072].T)
    wo = np.ascontiguousarray(Wo.T)

    gate_w = gup[:, :, 0::2]
    up_w = gup[:, :, 1::2]
    wgu = np.empty((NE, 8, D, 256), np.float32)
    for j in range(8):
        wgu[:, j, :, 0:128] = gate_w[:, :, j * 128:(j + 1) * 128]
        wgu[:, j, :, 128:256] = up_w[:, :, j * 128:(j + 1) * 128]
    wd = np.ascontiguousarray(wdp.reshape(NE, D, 8, 128).transpose(0, 2, 1, 3))

    gate_b = gub[:, 0::2]
    up_b = gub[:, 1::2]
    gub_cols = np.empty((128, 256), np.float32)
    for e in range(NE):
        for j in range(8):
            gub_cols[:, e * 16 + 2 * j] = gate_b[e, j * 128:(j + 1) * 128]
            gub_cols[:, e * 16 + 2 * j + 1] = up_b[e, j * 128:(j + 1) * 128]

    nc = _get_nc()
    in_maps = []
    for c in range(8):
        bb, hh = c // 2, c % 2
        sl = x[bb, 2048 * hh:2048 * (hh + 1)]
        xfc = np.ascontiguousarray(sl.T)
        if hh == 1:
            xpfc = np.ascontiguousarray(x[bb, 0:2048].T)
            flagc = np.ones((128, 1), np.float32)
        else:
            xpfc = xfc
            flagc = np.zeros((128, 1), np.float32)
        m = dict(xf=xfc, xpf=xpfc, xtok=np.ascontiguousarray(sl), wq=wq, wk=wk, wv=wv, wo=wo,
                 rw=rw, rb=rb.reshape(16, 1), wgu=wgu, wd=wd,
                 gub=gub_cols, dnb=dnb, esel=consts["esel"],
                 iden=consts["iden"], dclamp=consts["dclamp"], mask01=consts["mask01"],
                 tposb=consts["tposb"], jrev=consts["jrev"], m64=consts["m64"],
                 flag=flagc, lg=lg.reshape(1, 16))
        in_maps.append(m)

    res = run_bass_kernel_spmd(nc, in_maps, core_ids=list(range(8)))
    out = np.empty((4, 4096, 1024), np.float32)
    for c in range(8):
        bb, hh = c // 2, c % 2
        out[bb, 2048 * hh:2048 * (hh + 1)] = res.results[c]["y"]
    return out


# version 19
# speedup vs baseline: 1.0474x; 1.0053x over previous
import numpy as np
from contextlib import ExitStack

import concourse.bass as bass
import concourse.mybir as mybir
from concourse.bass_utils import run_bass_kernel_spmd

F32 = mybir.dt.float32
I32 = mybir.dt.int32
BF16 = mybir.dt.bfloat16
AF = mybir.ActivationFunctionType
ALU = mybir.AluOpType
AX = mybir.AxisListType.X

D = 1024
T = 2048
NG = 4
GT = 512
NH = 16
NE = 16
EPS = 1e-5
NEG = -1e30


def build_nc():
    nc = bass.Bass("TRN2", target_bir_lowering=False)

    xf = nc.dram_tensor("xf", [D, T], F32, kind="ExternalInput").ap()
    xpf = nc.dram_tensor("xpf", [D, T], F32, kind="ExternalInput").ap()
    wq_h = nc.dram_tensor("wq", [D, D], F32, kind="ExternalInput").ap()
    wk_h = nc.dram_tensor("wk", [D, D], F32, kind="ExternalInput").ap()
    wv_h = nc.dram_tensor("wv", [D, D], F32, kind="ExternalInput").ap()
    wo_h = nc.dram_tensor("wo", [D, D], F32, kind="ExternalInput").ap()
    rw_h = nc.dram_tensor("rw", [D, 16], F32, kind="ExternalInput").ap()
    rb_h = nc.dram_tensor("rb", [16, 1], F32, kind="ExternalInput").ap()
    wgu_h = nc.dram_tensor("wgu", [NE, 8, D, 256], F32, kind="ExternalInput").ap()
    wd_h = nc.dram_tensor("wd", [NE, 8, D, 128], F32, kind="ExternalInput").ap()
    gub_h = nc.dram_tensor("gub", [128, 256], F32, kind="ExternalInput").ap()
    dnb_h = nc.dram_tensor("dnb", [16, D], F32, kind="ExternalInput").ap()
    esel_h = nc.dram_tensor("esel", [16, NE * 128], F32, kind="ExternalInput").ap()
    iden_h = nc.dram_tensor("iden", [128, 128], F32, kind="ExternalInput").ap()
    dclamp_h = nc.dram_tensor("dclamp", [128, 128], F32, kind="ExternalInput").ap()
    mask01_h = nc.dram_tensor("mask01", [128, 128], F32, kind="ExternalInput").ap()
    tposb_h = nc.dram_tensor("tposb", [128, 128], F32, kind="ExternalInput").ap()
    jrev_h = nc.dram_tensor("jrev", [128, 1], F32, kind="ExternalInput").ap()
    m64_h = nc.dram_tensor("m64", [128, 2], F32, kind="ExternalInput").ap()
    flag_h = nc.dram_tensor("flag", [128, 1], F32, kind="ExternalInput").ap()
    lg_h = nc.dram_tensor("lg", [1, 16], F32, kind="ExternalInput").ap()

    y_h = nc.dram_tensor("y", [T, D], F32, kind="ExternalOutput").ap()
    xtok_h = nc.dram_tensor("xtok", [T, D], F32, kind="ExternalInput").ap()
    rscr = nc.dram_tensor("rscr", [128, 8 * T], F32, kind="Internal").ap()

    es = ExitStack()
    sb = lambda name, shape, dt: es.enter_context(nc.sbuf_tensor("sb_" + name, shape, dt))
    sem = lambda name: es.enter_context(nc.semaphore(name))

    xn_bf = sb("xn_bf", [128, 8 * T], BF16)
    scoresT = sb("scoresT", [16, T], F32)
    S_T = sb("S_T", [128, 512], F32)
    iden = sb("iden", [128, 128], F32)
    ones_r = sb("ones_r", [1, 128], F32)
    ones_c = sb("ones_c", [128, 1], F32)
    onesw = sb("onesw", [128, 64], F32)
    eps_c = sb("eps_c", [128, 1], F32)
    jnk1 = sb("jnk1", [128, 512], F32)
    jnkA = sb("jnkA", [128, 128], F32)
    flag = sb("flag", [128, 1], F32)
    rb_sb = sb("rb_sb", [16, 1], F32)
    rw_sb = sb("rw_sb", [128, 8 * 16], F32)

    esA = ExitStack()
    sbA = lambda name, shape, dt: esA.enter_context(nc.sbuf_tensor("sa_" + name, shape, dt))
    wq_sb = sbA("wq_sb", [128, 8 * D], BF16)
    wk_sb = sbA("wk_sb", [128, 8 * D], BF16)
    wv_sb = sbA("wv_sb", [128, 8 * D], BF16)
    wo_sb = sbA("wo_sb", [128, 8 * D], BF16)
    xg = sbA("xg", [128, 8 * GT], F32)
    scr = sbA("scr", [128, 8 * GT], F32)
    QY = sbA("QY", [128, 8 * GT], BF16)
    Vb = sbA("Vb", [128, NG * D], BF16)
    Kfm = sbA("Kfm", [128, 8 * GT], BF16)
    qg = sbA("qg", [128, 8 * 128], BF16)
    kg = sbA("kg", [128, D], BF16)
    Klnb = sbA("Klnb", [128, NG * D], BF16)
    iden_bf = sbA("iden_bf", [128, 128], BF16)
    adb = sbA("adb", [128, 2048], BF16)
    S_bf = sbA("S_bf", [128, 512], BF16)
    Dw = sbA("Dw", [128, NH * 128], BF16)
    Gq = sbA("Gq", [128, 8 * 128], BF16)
    Gk = sbA("Gk", [128, D], BF16)
    gC = sbA("gC", [128, 512], F32)
    rowb = sbA("rowb", [1, 3 * GT], F32)
    ltok = sbA("ltok", [128, 64], F32)
    lz = sbA("lz", [128, 64], F32)
    m8b = sbA("m8b", [128, 32], F32)
    nmx = sbA("nmx", [128, 4], F32)
    e4b = sbA("e4b", [128, 16], F32)
    p16b = sbA("p16b", [128, 64], F32)
    selb = sbA("selb", [128, 64], F32)
    scob = sbA("scob", [128, 64], F32)
    dnm = sbA("dnm", [128, 8], F32)
    kst = sbA("kst", [128, 256], F32)
    lraw = sbA("lraw", [16, GT], F32)
    esP = ExitStack()
    sbP = lambda name, shape, dt: esP.enter_context(nc.sbuf_tensor("sp_" + name, shape, dt))
    ptmp = sbP("ptmp", [128, 128], F32)
    grow = sbP("grow", [1, 16], F32)
    lnrow = sbP("lnrow", [1, 16], F32)
    lnbc = sbP("lnbc", [128, 16], F32)
    lnpair = sbP("lnpair", [128, 8], F32)
    gcol = sbP("gcol", [128, 1], F32)
    gCcol = sbP("gCcol", [128, 16], F32)
    gCrow = sbP("gCrow", [1, 16], F32)
    m64 = sbP("m64", [128, 2], F32)
    jrev = sbP("jrev", [128, 1], F32)
    dclamp = sbP("dclamp", [128, 128], BF16)
    mask01 = sbP("mask01", [128, 128], BF16)
    tposb = sbP("tposb", [128, 128], BF16)
    lgt = sbP("lgt", [1, 16], F32)

    ps_bc = esA.enter_context(nc.psum_tensor("ps_bc", [128, 1024], F32))
    ps_x0 = esA.enter_context(nc.psum_tensor("ps_x0", [128, 512], F32))
    ps_x1 = esA.enter_context(nc.psum_tensor("ps_x1", [128, 512], F32))
    ps_r0 = esA.enter_context(nc.psum_tensor("ps_r0", [128, 512], F32))
    ps_r1 = esA.enter_context(nc.psum_tensor("ps_r1", [128, 512], F32))
    ps_sd = esA.enter_context(nc.psum_tensor("ps_sd", [128, 512], F32))
    ps_y1 = esA.enter_context(nc.psum_tensor("ps_y1", [128, 512], F32))
    PSX = [ps_x0, ps_x1]
    PSR = [ps_r0, ps_r1]

    s_ld = sem("s_ld")
    s_ldw = sem("s_ldw")
    pg = sem("pg")
    p0s = sem("p0s")
    vself = sem("vself")
    vself_n = [0]

    def vg(v, ins):
        vself_n[0] += 1
        ins.then_inc(vself, 1)
        v.wait_ge(vself, vself_n[0])

    def emit_preamble(b):
        @b.sync
        def _(sync):
            n = 0
            for dst, src in [(iden, iden_h), (jrev, jrev_h), (m64, m64_h),
                             (flag, flag_h), (lgt, lg_h), (rb_sb, rb_h),
                             (rw_sb, rw_h)]:
                if dst is rw_sb:
                    sync.dma_start(rw_sb[:].rearrange("p (c o) -> p c o", c=8),
                                   rw_h.rearrange("(c p) o -> p c o", p=128)).then_inc(s_ld, 16)
                else:
                    sync.dma_start(dst[:], src).then_inc(s_ld, 16)
                n += 1

        @b.gpsimd
        def _(g):
            for dst, src in [(dclamp, dclamp_h), (mask01, mask01_h), (tposb, tposb_h)]:
                g.dma_start(dst[:].rearrange("p (a b) -> p a b", a=2),
                            src.rearrange("p (a b) -> p a b", a=2)).then_inc(s_ldw, 16)
            for dst, src in [(wk_sb, wk_h), (wv_sb, wv_h)]:
                g.dma_start(dst[:].rearrange("p (c o) -> p c o", c=8),
                            src.rearrange("(c p) o -> p c o", p=128)).then_inc(s_ldw, 16)

        @b.vector
        def _(v):
            v.wait_ge(s_ld, 7 * 16)
            v.wait_ge(s_ldw, 3 * 16)
            for t_, val in [(ones_r, 1.0), (ones_c, 1.0), (onesw, 1.0),
                            (eps_c, EPS), (jnk1, 0.0)]:
                v.memset(t_[:], val)
            v.tensor_copy(iden_bf[:], iden[:])
            v.memset(S_T[:], 0.0).then_inc(pg, 1)

            def vsp(semh, k=1):
                v.tensor_copy(jnk1[:], jnk1[:]).then_inc(semh, k)
            v.wait_ge(pg, 3)
            v.tensor_copy(lnbc[:], ps_bc[:, 0:16])
            vsp(p0s)
            v.wait_ge(p0s, 2)
            lnbc3 = lnbc[:].rearrange("p (a b) -> p a b", b=2)
            vg(v, v.tensor_scalar(lnpair[:], lnbc3[:, :, 0], m64[:, 0:1], None, op0=ALU.mult))
            vg(v, v.tensor_scalar(ptmp[:, 0:8], lnbc3[:, :, 1], m64[:, 1:2], None, op0=ALU.mult))
            vg(v, v.tensor_add(lnpair[:], lnpair[:], ptmp[:, 0:8]))
            vsp(p0s)
            v.wait_ge(p0s, 3)
            v.tensor_scalar(gCrow[:], lnrow[:], 128.0, None, op0=ALU.mult)
            vsp(pg)
            v.wait_ge(pg, 6)
            v.tensor_copy(gCcol[:], ps_bc[:, 16:32])
            vsp(p0s)
            v.wait_ge(p0s, 4)
            for h in range(NH):
                pb = 64 * (h % 2)
                v.tensor_scalar(gC[pb:pb + 64, (h // 2) * 64:(h // 2 + 1) * 64],
                                onesw[pb:pb + 64, :], gCcol[pb:pb + 64, h:h + 1], None,
                                op0=ALU.mult)
            v.nop().then_inc(pg, 1)
            TMP = xn_bf[:].bitcast(F32)
            for h in range(NH):
                v.tensor_scalar(TMP[:, h * 128:(h + 1) * 128], dclamp[:], lnbc[:, h:h + 1],
                                None, op0=ALU.mult)
            for p in range(8):
                v.tensor_scalar(TMP[:, 2048 + p * 128:2048 + (p + 1) * 128], tposb[:],
                                lnpair[:, p:p + 1], None, op0=ALU.mult)
            v.tensor_scalar(TMP[:, 3072:3088], lnbc[:], jrev[:], None, op0=ALU.mult)
            vsp(pg)
            v.wait_ge(pg, 9)
            for h in range(NH):
                v.tensor_mul(Dw[:, h * 128:(h + 1) * 128], TMP[:, h * 128:(h + 1) * 128],
                             mask01[:])
            v.tensor_copy(Gq[:], TMP[:, 2048:3072])
            for h in range(NH):
                v.tensor_scalar(Gk[:, h * 64:(h + 1) * 64], onesw[:], TMP[:, 3072 + h:3073 + h],
                                None, op0=ALU.mult)
            vg(v, v.tensor_mul(ptmp[:, 0:16], gCcol[:], gCcol[:]))
            vg(v, v.tensor_mul(ptmp[:, 16:32], ptmp[:, 0:16], gCcol[:]))
            Gk3 = Gk[:].rearrange("p (h d) -> p h d", d=64)
            for cc, fac in ((0, ptmp[:, 16:32]), (1, ptmp[:, 0:16]), (2, gCcol[:])):
                v.tensor_tensor(Kfm[:, cc * 1024:(cc + 1) * 1024].rearrange("p (h d) -> p h d", d=64),
                                Gk3, fac.unsqueeze(2).to_broadcast([128, NH, 64]), op=ALU.mult)

        @b.scalar
        def _(s):
            s.wait_ge(s_ld, 7 * 16)
            def asp(semh, k=1):
                s.copy(jnkA[:], iden[:]).then_inc(semh, k)

            s.activation(grow[:], lgt[:], AF.Sigmoid)
            asp(p0s)
            s.wait_ge(p0s, 1)
            s.activation(lnrow[:], grow[:], AF.Ln)
            asp(pg)
            s.wait_ge(pg, 4)
            s.activation(gCrow[:], gCrow[:], AF.Exp)
            asp(pg)
            TMPs = xn_bf[:].bitcast(F32)
            s.wait_ge(pg, 8)
            s.activation(TMPs[:, 0:2048], TMPs[:, 0:2048], AF.Exp)
            s.activation(TMPs[:, 2048:3088], TMPs[:, 2048:3088], AF.Exp)
            asp(pg)

        @b.tensor
        def _(t):
            t.wait_ge(pg, 2)
            t.matmul(ps_bc[:, 0:16], ones_r[:], lnrow[:], start=True, stop=True).then_inc(pg, 1)
            t.wait_ge(pg, 5)
            t.matmul(ps_bc[:, 16:32], ones_r[:], gCrow[:], start=True, stop=True).then_inc(pg, 1)

    ATTN_SEM_NAMES = ("sx sst a_sq p_st d_rw a_rs p_bc d_xn p_rl d_lg p_lt r1 r2 r3 r4 r5 "
                      "a_e4 d_dn a_p16 d_sc p_sT d_sE p_q d_q p_k d_ke a_kq d_ms d_st2 a_kr "
                      "d_kl p_kt d_kf p_v d_ve d_qk d_sb p_A d_ad p_y d_ye p_sd d_su p_wo d_r a_l1 a_l2 d_rdy")
    ATTN_SEMS = {n: sem("at_" + n) for n in ATTN_SEM_NAMES.split()}
    PREFIX_TOTALS = dict(sx=16 * NG, a_sq=NG, p_st=NG, d_rw=NG, a_rs=NG, p_bc=NG, d_xn=NG,
                         p_k=8 * NG, d_ke=8 * NG, a_kq=8 * NG, d_ms=8 * NG, d_st2=NG,
                         a_kr=NG, d_kl=4 * NG, p_v=8 * NG, d_ve=8 * NG, d_qk=4 * NG,
                         p_sd=NG, d_su=NG, a_l1=NG, a_l2=NG, d_rdy=NG)

    def attn_phase(b, xsrc, prefix, lvl=5):
        S = ATTN_SEMS
        BASE = {} if prefix else PREFIX_TOTALS
        L2 = (not prefix) and lvl >= 2
        L3 = (not prefix) and lvl >= 3
        L4 = (not prefix) and lvl >= 4
        L5 = (not prefix) and lvl >= 5
        xnT = QY if prefix else xn_bf
        PKB = [kg[:], qg[:], adb[:, 0:1024], adb[:, 1024:2048]]
        NMU, VAR, RSTD = 0, GT, 2 * GT

        def xn_col(fc, g):
            return fc * (GT if prefix else T) + (0 if prefix else g * GT)

        if True:
            @b.sync
            def _(sync):
                def w(s, v):
                    v += BASE.get(s, 0)
                    if v > 0:
                        sync.wait_ge(S[s], v)

                def xload(g):
                    if g > 0:
                        w("d_xn", g)
                        if L5:
                            w("p_st", g)
                    elif not prefix:
                        w("d_xn", 0)
                    sync.dma_start(
                        xg[:].rearrange("p (c t) -> p c t", c=8),
                        xsrc[:, g * GT:(g + 1) * GT].rearrange("(c p) t -> p c t", p=128),
                    ).then_inc(S["sx"], 16)

                def rstore(g):
                    rstage = Kfm[:].bitcast(F32)
                    for hh in range(2):
                        w("d_r", 8 * g + 4 * (hh + 1))
                        sync.dma_start(rscr[:, g * 4096 + hh * 2048:g * 4096 + (hh + 1) * 2048],
                                       rstage[:, 0:2048]).then_inc(S["sst"], 16)

                for g in range(NG):
                    xload(g)
                    if L5 and g >= 1:
                        rstore(g - 1)
                if L5:
                    rstore(NG - 1)
                    w("sst", 32 * NG)

            @b.tensor
            def _(t):
                def w(s, v):
                    v += BASE.get(s, 0)
                    if v > 0:
                        t.wait_ge(S[s], v)

                def st_xsum(g):
                    if prefix and g == 0:
                        t.wait_ge(p0s, 4)
                    w("sx", 16 * (g + 1)); w("d_ms", 8 * g)
                    if g == 0:
                        w("d_xn", 0)
                        if L2:
                            w("d_lg", 0)
                    for fc in range(8):
                        t.matmul(ps_bc[0:1, 0:512], ones_c[:], xg[:, fc * GT:(fc + 1) * GT],
                                 start=(fc == 0), stop=(fc == 7))

                def st_stats(g):
                    w("a_sq", g + 1); w("d_ms", 8 * g); w("d_xn", g)
                    for fc in range(8):
                        ins = t.matmul(ps_bc[0:1, 512:1024], ones_c[:], scr[:, fc * GT:(fc + 1) * GT],
                                       start=(fc == 0), stop=(fc == 7))
                    ins.then_inc(S["p_st"], 1)

                def st_bcast(g):
                    w("a_rs", g + 1)
                    t.matmul(ps_bc[:, 0:512], ones_r[:], rowb[0:1, 0:GT], start=True, stop=True)
                    t.matmul(ps_bc[:, 512:1024], ones_r[:], rowb[0:1, 2 * GT:3 * GT],
                             start=True, stop=True).then_inc(S["p_bc"], 1)

                def st_router(g):
                    w("d_xn", g + 1); w("d_lg", g)
                    if L4:
                        w("d_ad", 8 * g - 3)
                    if L3:
                        w("d_ke", 8 * g - 1)
                    for fc in range(8):
                        ins = t.matmul(ps_x0[0:16, :], rw_sb[:, fc * 16:(fc + 1) * 16],
                                       scr[:, fc * GT:(fc + 1) * GT],
                                       start=(fc == 0), stop=(fc == 7))
                    ins.then_inc(S["p_rl"], 1)

                def st_ltrans(g):
                    w("d_lg", g + 1); w("r1", g)
                    if L4 or prefix:
                        w("d_su", 4 * g)
                    for j in range(4):
                        ins = t.transpose(ps_sd[:, j * 16:(j + 1) * 16],
                                          lraw[:, j * 128:(j + 1) * 128], iden[0:16, 0:16])
                    ins.then_inc(S["p_lt"], 1)

                def st_q(g):
                    for ob in range(8):
                        if ob == 0:
                            if g == 0:
                                t.wait_ge(s_ldw, 6 * 16)
                            w("d_lg", g + 1)
                            if L5:
                                w("d_r", 8 * g - 1)
                            if L4:
                                w("d_ad", 8 * g - 1)
                            w("d_ke", 8 * g - 1)
                        elif ob == 1:
                            if L5:
                                w("d_r", 8 * g)
                            if L4:
                                w("d_ad", 8 * g)
                            w("d_ke", 8 * g)
                        else:
                            w("d_q", 8 * g + ob - 1)
                        for fc in range(8):
                            ins = t.matmul(PSX[ob % 2][:], wq_sb[:, fc * D + ob * 128:fc * D + (ob + 1) * 128],
                                           xn_bf[:, fc * T + g * GT:fc * T + (g + 1) * GT],
                                           start=(fc == 0), stop=(fc == 7))
                        ins.then_inc(S["p_q"], 1)

                def st_k(g):
                    for idx in range(8):
                        j, nb = idx // 2, idx % 2
                        if prefix:
                            if g == 0 and idx == 0:
                                t.wait_ge(s_ldw, 4 * 16)
                            w("d_ke", 8 * g + idx - 1)
                            if idx < 2:
                                w("d_xn", g + 1)
                        else:
                            if idx == 0:
                                w("d_q", 8 * g + 7)
                            elif idx == 1:
                                w("d_q", 8 * g + 8)
                            else:
                                w("d_ke", 8 * g + idx - 1)
                        for fc in range(8):
                            ins = t.matmul(PSX[idx % 2][:],
                                           xnT[:, xn_col(fc, g) + j * 128:xn_col(fc, g) + (j + 1) * 128],
                                           wk_sb[:, fc * D + nb * 512:fc * D + (nb + 1) * 512],
                                           start=(fc == 0), stop=(fc == 7))
                        ins.then_inc(S["p_k"], 1)

                def st_v(g):
                    for idx in range(8):
                        j, nb = idx // 2, idx % 2
                        if prefix and g == 0 and idx == 0:
                            t.wait_ge(s_ldw, 5 * 16)
                        if idx < 2 and not prefix:
                            if L4:
                                w("d_ad", 8 * g - 1 + idx)
                            elif L3:
                                w("d_kf", 8 * g - 1 + idx)
                        w("d_ve", 8 * g + idx - 1)
                        for fc in range(8):
                            ins = t.matmul(PSR[idx % 2][:],
                                           xnT[:, xn_col(fc, g) + j * 128:xn_col(fc, g) + (j + 1) * 128],
                                           wv_sb[:, fc * D + nb * 512:fc * D + (nb + 1) * 512],
                                           start=(fc == 0), stop=(fc == 7))
                        ins.then_inc(S["p_v"], 1)

                def st_sct(g):
                    w("d_sc", g + 1); w("d_ke", 8 * g + 8)
                    for j in range(4):
                        ins = t.transpose(ps_x1[0:16, j * 128:(j + 1) * 128],
                                          scob[:, j * 16:(j + 1) * 16], iden[:])
                    ins.then_inc(S["p_sT"], 1)

                def st_kt(g):
                    for i in range(8):
                        sub, half = i // 2, i % 2
                        w("d_kl", 4 * g + sub + 1)
                        if i == 0:
                            w("d_ve", 8 * g + 7)
                        elif i == 1:
                            w("d_ve", 8 * g + 8)
                        else:
                            w("d_kf", 8 * g + i - 1)
                        dstb = PSR[i % 2][:].bitcast(BF16)
                        for tt in range(4):
                            p = half * 4 + tt
                            ins = t.transpose(dstb[:, tt * 128:(tt + 1) * 128],
                                              Klnb[:, sub * D + p * 128:sub * D + (p + 1) * 128],
                                              iden_bf[:])
                        ins.then_inc(S["p_kt"], 1)

                def st_ret(g, cc):
                    w("d_sb", 4 * g + cc + 1)
                    w("d_qk", 4 * g + cc + 1)
                    w("d_kf", 8 * g + 8)
                    w("d_q", 8 * g + 8)
                    w("d_rdy", g + 1)
                    if cc == 0:
                        w("d_ke", 8 * g + 8)
                        w("d_sE", g + 1)
                    else:
                        w("d_ad", 8 * g + 2 * cc)
                    for p in range(8):
                        for hh in range(2):
                            dst = (PSX if hh == 0 else PSR)[p // 4]
                            ins = t.matmul(dst[:, (p % 4) * 128:(p % 4) * 128 + 128],
                                           Kfm[64 * hh:64 * hh + 64,
                                               p * GT + cc * 128:p * GT + (cc + 1) * 128],
                                           QY[64 * hh:64 * hh + 64,
                                              p * GT + cc * 128:p * GT + (cc + 1) * 128],
                                           start=True, stop=True)
                    ins.then_inc(S["p_A"], 1)
                    for r in range(2):
                        w("d_ad", 8 * g + 2 * cc + r + 1)
                        w("d_ye", 8 * g + 2 * cc + r)
                        for p in range(4 * r, 4 * r + 4):
                            for hh in range(2):
                                h = 2 * p + hh
                                t.matmul(ps_y1[64 * hh:64 * hh + 64, (p % 4) * 128:(p % 4) * 128 + 128],
                                         Vb[:, cc * D + h * 64:cc * D + (h + 1) * 64],
                                         adb[:, h * 128:(h + 1) * 128],
                                         start=True, stop=False)
                                ins = t.matmul(ps_y1[64 * hh:64 * hh + 64, (p % 4) * 128:(p % 4) * 128 + 128],
                                               S_bf[64 * hh:64 * hh + 64, p * 64:(p + 1) * 64],
                                               qg[64 * hh:64 * hh + 64, p * 128:(p + 1) * 128],
                                               start=False, stop=True)
                        ins.then_inc(S["p_y"], 1)
                    w("d_su", 4 * g + cc)
                    for h in range(NH):
                        ins = t.matmul(ps_sd[64 * (h % 2):64 * (h % 2) + 64,
                                             (h // 2) * 64:(h // 2 + 1) * 64],
                                       kg[:, h * 64:(h + 1) * 64],
                                       Vb[:, cc * D + h * 64:cc * D + (h + 1) * 64],
                                       start=True, stop=True)
                    ins.then_inc(S["p_sd"], 1)

                def st_sdall(g):
                    w("d_qk", 4 * g + 4)
                    w("d_ve", 8 * g + 8)
                    w("d_rdy", g + 1)
                    w("d_su", g)
                    for h in range(NH):
                        for cc in range(4):
                            kb = PKB[cc]
                            ins = t.matmul(ps_sd[64 * (h % 2):64 * (h % 2) + 64,
                                                 (h // 2) * 64:(h // 2 + 1) * 64],
                                           kb[:, h * 64:(h + 1) * 64],
                                           Vb[:, cc * D + h * 64:cc * D + (h + 1) * 64],
                                           start=(cc == 0), stop=(cc == 3))
                    ins.then_inc(S["p_sd"], 1)

                def st_warm(g):
                    for q in range(8):
                        w("a_kq", 8 * g + q + 1)
                        t.matmul(ps_y1[0:16, 0:16], ones_r[0:1, 0:16], ones_r[0:1, 0:16],
                                 start=True, stop=True)
                    for q in range(8):
                        w("d_ms", 8 * g + q + 1)
                        t.matmul(ps_y1[0:16, 0:16], ones_r[0:1, 0:16], ones_r[0:1, 0:16],
                                 start=True, stop=True)
                    for j in range(4):
                        w("d_kl", 4 * g + j + 1)
                        t.matmul(ps_y1[0:16, 0:16], ones_r[0:1, 0:16], ones_r[0:1, 0:16],
                                 start=True, stop=True)

                def st_wo(g):
                    w("d_ye", 8 * (g + 1))
                    for oc in range(8):
                        if oc == 0:
                            if g == 0:
                                t.wait_ge(s_ldw, 7 * 16)
                            w("d_ad", 8 * g + 7)
                        elif oc == 1:
                            w("d_ad", 8 * g + 8)
                        else:
                            w("d_r", 8 * g + oc - 1)
                        for yc in range(8):
                            ins = t.matmul(PSX[oc % 2][:],
                                           wo_sb[:, yc * D + oc * 128:yc * D + (oc + 1) * 128],
                                           QY[:, yc * GT:(yc + 1) * GT],
                                           start=(yc == 0), stop=(yc == 7))
                        ins.then_inc(S["p_wo"], 1)

                for g in range(NG):
                    bk = g >= 1
                    if prefix:
                        if g == 0:
                            st_xsum(0)
                        st_stats(g)
                        if bk:
                            st_sdall(g - 1)
                        st_bcast(g)
                        st_k(g)
                        st_v(g)
                        st_warm(g)
                        if g + 1 < NG:
                            st_xsum(g + 1)
                    else:
                        if g == 0:
                            st_xsum(0)
                        if bk and L4:
                            st_ret(g - 1, 0)
                        st_stats(g)
                        if bk and L4:
                            st_ret(g - 1, 1)
                        st_bcast(g)
                        if bk and L4:
                            st_ret(g - 1, 2)
                        if L2:
                            st_router(g)
                        if bk and L4:
                            st_ret(g - 1, 3)
                        if L2:
                            st_ltrans(g)
                        if bk and L5:
                            st_wo(g - 1)
                        if L3:
                            st_q(g)
                            st_k(g)
                            st_v(g)
                            st_warm(g)
                            if g + 1 < NG:
                                st_xsum(g + 1)
                        if L2:
                            st_sct(g)
                        if L3:
                            st_kt(g)
                if prefix:
                    st_sdall(NG - 1)
                else:
                    if L4:
                        for cc in range(4):
                            st_ret(NG - 1, cc)
                    if L5:
                        st_wo(NG - 1)

            @b.vector
            def _(v):
                def w(s, val):
                    val += BASE.get(s, 0)
                    if val > 0:
                        v.wait_ge(S[s], val)

                def sp(sname, k=1):
                    v.tensor_copy(jnk1[:], jnk1[:]).then_inc(S[sname], k)

                def newton_rsqrt(dst, varap, scr_f32):
                    vg(v, v.tensor_scalar(varap, varap, EPS, None, op0=ALU.add))
                    iv = varap.bitcast(I32)
                    si = scr_f32.bitcast(I32)
                    vg(v, v.tensor_scalar(si, iv, 1, None, op0=ALU.arith_shift_right))
                    vg(v, v.tensor_scalar(si, si, -1, None, op0=ALU.mult))
                    vg(v, v.tensor_scalar(si, si, 0x5F3759DF, None, op0=ALU.add))
                    for _ in range(2):
                        vg(v, v.tensor_mul(dst, scr_f32, scr_f32))
                        vg(v, v.tensor_mul(dst, dst, varap))
                        vg(v, v.tensor_scalar(dst, dst, -0.5, 1.5, op0=ALU.mult, op1=ALU.add))
                        vg(v, v.tensor_mul(scr_f32, scr_f32, dst))
                    ins = v.tensor_copy(dst, scr_f32)
                    vg(v, ins)
                    return ins
                def dv_rw(g):
                    mus = jnk1[0:1, 0:GT]
                    w("p_st", g + 1); w("p_bc", g)
                    vg(v, v.tensor_scalar(mus, ps_bc[0:1, 0:512], 1.0 / D, None, op0=ALU.mult))
                    vg(v, v.tensor_scalar(rowb[0:1, NMU:NMU + GT], mus, -1.0, None, op0=ALU.mult))
                    vg(v, v.tensor_mul(mus, mus, mus))
                    vg(v, v.tensor_scalar(rowb[0:1, VAR:VAR + GT], ps_bc[0:1, 512:1024], 1.0 / D, None, op0=ALU.mult))
                    vg(v, v.tensor_sub(rowb[0:1, VAR:VAR + GT], rowb[0:1, VAR:VAR + GT], mus))
                    if prefix:
                        sp("d_rw")
                    else:
                        newton_rsqrt(rowb[0:1, RSTD:RSTD + GT], rowb[0:1, VAR:VAR + GT], mus)
                        sp("a_rs")

                def dv_xn(g):
                    w("p_bc", g + 1)
                    if L2:
                        w("p_rl", g)
                    if prefix:
                        w("p_v", 8 * g)
                        w("p_k", 8 * g)
                    for fc in range(8):
                        v.tensor_add(scr[:, fc * GT:(fc + 1) * GT], xg[:, fc * GT:(fc + 1) * GT],
                                     ps_bc[:, 0:512])
                        v.tensor_mul(xnT[:, xn_col(fc, g):xn_col(fc, g) + GT],
                                     scr[:, fc * GT:(fc + 1) * GT], ps_bc[:, 512:1024])
                    sp("d_xn")

                def dv_lga(g):
                    w("p_rl", g + 1); w("p_lt", g)
                    vg(v, v.tensor_copy(lraw[:], ps_x0[0:16, :]))
                    vg(v, v.tensor_mul(lraw[:], lraw[:], ps_bc[0:16, 512:1024]))
                    vg(v, v.tensor_scalar(lraw[:], lraw[:], rb_sb[:], None, op0=ALU.add))
                    sp("d_lg")

                def dv_lgb(g):
                    w("p_lt", g + 1); w("a_e4", g); w("a_p16", g)
                    for j in range(4):
                        ins = v.tensor_copy(ltok[:, j * 16:(j + 1) * 16], ps_sd[:, j * 16:(j + 1) * 16])
                    ins.then_inc(S["r1"], 1)
                    v.wait_ge(S["r1"], g + 1)
                    for j in range(4):
                        ins = v.max(out=m8b[:, j * 8:(j + 1) * 8], in_=ltok[:, j * 16:(j + 1) * 16])
                    ins.then_inc(S["r2"], 1)
                    v.wait_ge(S["r2"], g + 1)
                    for j in range(4):
                        ins = v.memset(m8b[:, j * 8 + 4:(j + 1) * 8], NEG)
                    ins.then_inc(S["r3"], 1)
                    v.wait_ge(S["r3"], g + 1)
                    for j in range(4):
                        ins = v.match_replace(out=lz[:, j * 16:(j + 1) * 16],
                                              in_to_replace=m8b[:, j * 8:(j + 1) * 8],
                                              in_values=ltok[:, j * 16:(j + 1) * 16],
                                              imm_value=NEG)
                    ins.then_inc(S["r4"], 1)
                    v.wait_ge(S["r4"], g + 1)
                    for j in range(4):
                        ins = v.tensor_scalar(nmx[:, j:j + 1], m8b[:, j * 8:j * 8 + 1], -1.0, None,
                                              op0=ALU.mult)
                    ins.then_inc(S["r5"], 1)
                    for j in range(4):
                        ins = v.tensor_tensor(selb[:, j * 16:(j + 1) * 16], lz[:, j * 16:(j + 1) * 16],
                                              ltok[:, j * 16:(j + 1) * 16], op=ALU.not_equal)
                    vg(v, ins)

                def dv_lgc(g):
                    w("a_e4", g + 1)
                    for j in range(4):
                        ins = v.reduce_sum(dnm[:, j:j + 1], e4b[:, j * 4:(j + 1) * 4], axis=AX)
                    ins.then_inc(S["d_dn"], 1)
                    v.wait_ge(S["d_dn"], 2 * g + 1)
                    for j in range(4):
                        ins = v.reciprocal(dnm[:, 4 + j:5 + j], dnm[:, j:j + 1])
                    ins.then_inc(S["d_dn"], 1)
                    w("a_p16", g + 1); w("p_sT", g)
                    v.wait_ge(S["d_dn"], 2 * g + 2)
                    for j in range(4):
                        v.scalar_tensor_tensor(scob[:, j * 16:(j + 1) * 16],
                                               p16b[:, j * 16:(j + 1) * 16],
                                               dnm[:, 4 + j:5 + j],
                                               selb[:, j * 16:(j + 1) * 16],
                                               op0=ALU.mult, op1=ALU.mult)
                    sp("d_sc")

                def dv_kln(g):
                    for j in range(4):
                        w("d_ke", 8 * g + 2 * j + 2)
                        v.reduce_sum(kst[:, j * 16:(j + 1) * 16],
                                     scr[:, j * D:(j + 1) * D].rearrange("p (h d) -> p h d", d=64),
                                     axis=AX)
                        for hf in range(2):
                            q = 2 * j + hf
                            w("a_kq", 8 * g + q + 1)
                            c0 = 64 + j * 16 + hf * 8
                            ins = v.reduce_sum(kst[:, c0:c0 + 8],
                                               ps_bc[:, (q % 2) * 512:(q % 2) * 512 + 512]
                                               .rearrange("p (h d) -> p h d", d=64), axis=AX)
                            ins.then_inc(S["d_ms"], 1)
                    v.wait_ge(S["d_ms"], 8 * g + 8 + BASE.get("d_ms", 0))
                    vg(v, v.tensor_scalar(kst[:, 128:192], kst[:, 0:64], 1.0 / 64, None, op0=ALU.mult))
                    vg(v, v.tensor_mul(kst[:, 0:64], kst[:, 128:192], kst[:, 128:192]))
                    vg(v, v.tensor_scalar(kst[:, 64:128], kst[:, 64:128], 1.0 / 64, None, op0=ALU.mult))
                    vg(v, v.tensor_sub(kst[:, 64:128], kst[:, 64:128], kst[:, 0:64]))
                    sp("a_kr")
                    v.wait_ge(S["d_st2"], g + 1 + BASE.get("d_st2", 0))
                    if not prefix:
                        w("p_kt", 8 * g)
                    for j in range(4):
                        K3 = scr[:, j * D:(j + 1) * D].rearrange("p (h d) -> p h d", d=64)
                        KB3 = Klnb[:, j * D:(j + 1) * D].rearrange("p (h d) -> p h d", d=64)
                        mu3 = kst[:, 128 + j * 16:128 + (j + 1) * 16].unsqueeze(2).to_broadcast([128, 16, 64])
                        rs3 = kst[:, 192 + j * 16:192 + (j + 1) * 16].unsqueeze(2).to_broadcast([128, 16, 64])
                        v.tensor_tensor(K3, K3, mu3, op=ALU.subtract)
                        v.tensor_tensor(KB3, K3, rs3, op=ALU.mult).then_inc(S["d_kl"], 1)

                def dv_ret(g, cc):
                    w("p_sd", 4 * g + cc)
                    w("p_y", 8 * g + 2 * cc)
                    w("d_q", 8 * g + 8)
                    w("d_rdy", g + 1)
                    v.tensor_mul(kg[:], Klnb[:, cc * D:(cc + 1) * D], Gk[:])
                    v.tensor_mul(qg[:].rearrange("p (pr t) -> p pr t", t=128),
                                 QY[:].rearrange("p (pr t) -> p pr t", t=GT)[:, :, cc * 128:(cc + 1) * 128],
                                 Gq[:].rearrange("p (pr t) -> p pr t", t=128))
                    sp("d_qk")
                    v.tensor_copy(S_bf[:], S_T[:])
                    sp("d_sb")
                    w("p_A", 4 * g + cc + 1)
                    adb3 = adb[:].rearrange("p (pr two t) -> p pr two t", two=2, t=128)
                    Dw3 = Dw[:].rearrange("p (pr two t) -> p pr two t", two=2, t=128)
                    for half in range(2):
                        v.tensor_tensor(adb3[:, 4 * half:4 * half + 4, 0, :],
                                        PSX[half][:].rearrange("p (s t) -> p s t", t=128),
                                        Dw3[:, 4 * half:4 * half + 4, 0, :], op=ALU.mult)
                        v.tensor_tensor(adb3[:, 4 * half:4 * half + 4, 1, :],
                                        PSR[half][:].rearrange("p (s t) -> p s t", t=128),
                                        Dw3[:, 4 * half:4 * half + 4, 1, :], op=ALU.mult)
                        sp("d_ad")
                    w("p_sd", 4 * g + cc + 1)
                    v.tensor_mul(S_T[:], S_T[:], gC[:])
                    v.tensor_add(S_T[:], S_T[:], ps_sd[:]).then_inc(S["d_su"], 1)

                def dv_sd(g, cc):
                    kb = PKB[cc]
                    w("p_sd", g)
                    w("d_rdy", g + 1)
                    gkp = Gk[:] if cc == 3 else Kfm[:, cc * 1024:(cc + 1) * 1024]
                    v.tensor_mul(kb[:], Klnb[:, cc * D:(cc + 1) * D], gkp)
                    sp("d_qk")

                def dv_sadd(g):
                    w("p_sd", g + 1)
                    for _ in range(4):
                        v.tensor_mul(S_T[:], S_T[:], gC[:])
                    v.tensor_add(S_T[:], S_T[:], ps_sd[:]).then_inc(S["d_su"], 1)

                for g in range(NG):
                    bk = g >= 1
                    if prefix:
                        if bk:
                            dv_sd(g - 1, 0)
                        dv_rw(g)
                        if bk:
                            dv_sd(g - 1, 1)
                            dv_sd(g - 1, 2)
                            dv_sd(g - 1, 3)
                        dv_xn(g)
                        if bk:
                            dv_sadd(g - 1)
                        dv_kln(g)
                    else:
                        if bk and L4:
                            dv_ret(g - 1, 0)
                            dv_ret(g - 1, 1)
                        dv_rw(g)
                        if bk and L4:
                            dv_ret(g - 1, 2)
                        dv_xn(g)
                        if L2:
                            dv_lga(g)
                        if bk and L4:
                            dv_ret(g - 1, 3)
                        if L3:
                            dv_kln(g)
                        if L2:
                            dv_lgb(g)
                            dv_lgc(g)
                if prefix:
                    for cc in range(4):
                        dv_sd(NG - 1, cc)
                    dv_sadd(NG - 1)
                    v.tensor_scalar(S_T[:], S_T[:], flag[:], None, op0=ALU.mult)
                elif L4:
                    for cc in range(4):
                        dv_ret(NG - 1, cc)

            if prefix:
                @b.gpsimd
                def _(gp):
                    for dst, srch in [(wq_sb, wq_h), (wo_sb, wo_h)]:
                        gp.dma_start(dst[:].rearrange("p (c o) -> p c o", c=8),
                                     srch.rearrange("(c p) o -> p c o", p=128)).then_inc(s_ldw, 16)
                    gp.wait_ge(s_ldw, 7 * 16)

            @b.scalar
            def _(s):
                def w(sn, val):
                    val += BASE.get(sn, 0)
                    if val > 0:
                        s.wait_ge(S[sn], val)

                def asp(sname, k=1):
                    s.copy(jnkA[:], iden[:]).then_inc(S[sname], k)

                def ac_rw(g):
                    w("d_rw", g + 1)
                    s.activation(rowb[0:1, VAR:VAR + GT], rowb[0:1, VAR:VAR + GT], AF.Ln,
                                 bias=eps_c[0:1, :])
                    s.copy(jnkA[:], iden[:])
                    s.activation(rowb[0:1, RSTD:RSTD + GT], rowb[0:1, VAR:VAR + GT], AF.Exp,
                                 scale=-0.5)
                    asp("a_rs")

                def ac_krstd(g):
                    w("a_kr", g + 1)
                    s.activation(kst[:, 64:128], kst[:, 64:128], AF.Ln, bias=eps_c[:])
                    s.copy(jnkA[:], iden[:])
                    s.activation(kst[:, 192:256], kst[:, 64:128], AF.Exp, scale=-0.5)
                    asp("d_st2")

                def ac_sq(g):
                    w("sx", 16 * (g + 1))
                    if not (prefix or L3):
                        if L2:
                            w("p_rl", g)
                        w("p_st", g)
                        s.activation(scr[:], xg[:], AF.Square).then_inc(S["a_sq"], 1)
                        return
                    w("p_st", g)
                    for j in range(4):
                        w("d_kl", 4 * (g - 1) + j + 1)
                        ins = s.activation(scr[:, j * D:(j + 1) * D], xg[:, j * D:(j + 1) * D],
                                           AF.Square)
                    ins.then_inc(S["a_sq"], 1)

                def ac_yev(g, cc, r):
                    w("p_y", 8 * g + 2 * cc + r + 1)
                    ins = s.copy(
                        QY[:].rearrange("p (pr t) -> p pr t", t=GT)[:, 4 * r:4 * r + 4, cc * 128:(cc + 1) * 128],
                        ps_y1[:].rearrange("p (s t) -> p s t", t=128))
                    if cc == 3 and r == 1:
                        asp("d_ye")
                    else:
                        ins.then_inc(S["d_ye"], 1)

                def ac_rev(g):
                    rstage = Kfm[:].bitcast(F32)
                    for oc in range(8):
                        w("p_wo", 8 * g + oc + 1)
                        if oc == 4:
                            w("sst", 32 * g + 16)
                        ins = s.copy(rstage[:, (oc % 4) * 512:(oc % 4 + 1) * 512], PSX[oc % 2][:])
                        if (oc + 1) % 4 != 0:
                            ins.then_inc(S["d_r"], 1)
                        else:
                            asp("d_r")

                def ac_e4p16(g):
                    w("r5", g + 1); w("d_dn", 2 * g)
                    for j in range(4):
                        ins = s.activation(e4b[:, j * 4:(j + 1) * 4], m8b[:, j * 8:j * 8 + 4],
                                           AF.Exp, bias=nmx[:, j:j + 1])
                    ins.then_inc(S["a_e4"], 1)
                    w("d_sc", g)
                    for j in range(4):
                        ins = s.activation(p16b[:, j * 16:(j + 1) * 16], ltok[:, j * 16:(j + 1) * 16],
                                           AF.Exp, bias=nmx[:, j:j + 1])
                    ins.then_inc(S["a_p16"], 1)

                def ac_qev(g):
                    w("p_wo", 8 * g)
                    for ob in range(8):
                        w("p_q", 8 * g + ob + 1)
                        ins = s.copy(QY[:, ob * GT:(ob + 1) * GT], PSX[ob % 2][:])
                        if ob < 7:
                            ins.then_inc(S["d_q"], 1)
                        else:
                            asp("d_q")

                def ac_kev(g):
                    for idx in range(8):
                        j, nb = idx // 2, idx % 2
                        w("p_k", 8 * g + idx + 1)
                        ins = s.copy(scr[:, j * D + nb * 512:j * D + (nb + 1) * 512],
                                     PSX[idx % 2][:])
                        if idx < 7:
                            ins.then_inc(S["d_ke"], 1)
                        else:
                            asp("d_ke")

                def ac_ksq(g):
                    for q in range(8):
                        j, hf = q // 2, q % 2
                        w("d_ms", 8 * g + q - 1)
                        s.activation(ps_bc[:, (q % 2) * 512:(q % 2) * 512 + 512],
                                     scr[:, j * D + hf * 512:j * D + hf * 512 + 512],
                                     AF.Square).then_inc(S["a_kq"], 1)

                def ac_vev(g):
                    if prefix:
                        w("p_sd", g)
                    elif L4:
                        w("p_sd", 4 * g)
                    if L4:
                        w("p_y", 8 * g)
                    for idx in range(8):
                        j, nb = idx // 2, idx % 2
                        w("p_v", 8 * g + idx + 1)
                        ins = s.copy(Vb[:, j * D + nb * 512:j * D + (nb + 1) * 512],
                                     PSR[idx % 2][:])
                        if idx < 7:
                            ins.then_inc(S["d_ve"], 1)
                        else:
                            asp("d_ve")

                def ac_scev(g):
                    w("p_sT", g + 1)
                    s.copy(scoresT[:, g * GT:(g + 1) * GT], ps_x1[0:16, 0:512]).then_inc(S["d_sE"], 1)

                def ac_ktev(g):
                    if L4:
                        w("p_A", 4 * g)
                    if L5:
                        w("sst", 32 * g)
                    for i in range(8):
                        sub, half = i // 2, i % 2
                        w("p_kt", 8 * g + i + 1)
                        s.copy(
                            Kfm[:].rearrange("p (pr t) -> p pr t", pr=8)
                                  [:, half * 4:(half + 1) * 4, sub * 128:(sub + 1) * 128],
                            PSR[i % 2][:].bitcast(BF16)[:, 0:512].rearrange("p (tt t) -> p tt t", tt=4),
                        ).then_inc(S["d_kf"], 1)

                for g in range(NG):
                    ac_sq(g)
                    if prefix:
                        ac_rw(g)
                    if g >= 1:
                        if L4:
                            for cc in range(4):
                                for r in range(2):
                                    ac_yev(g - 1, cc, r)
                        if L5:
                            ac_rev(g - 1)
                    if L3:
                        ac_qev(g)
                    if prefix or L3:
                        ac_kev(g)
                        ac_vev(g)
                        ac_ksq(g)
                        ac_krstd(g)
                    if L2:
                        ac_e4p16(g)
                        ac_scev(g)
                    if L3:
                        ac_ktev(g)
                    if prefix or L3:
                        asp("d_rdy")
                if L4:
                    for cc in range(4):
                        for r in range(2):
                            ac_yev(NG - 1, cc, r)
                if L5:
                    ac_rev(NG - 1)

        return S

    with nc.Block("PA") as blk:
        emit_preamble(blk)
        attn_phase(blk, xpf, prefix=True)
        attn_phase(blk, xf, prefix=False)
    esP.close()

    esA.close()

    esB = ExitStack()
    sbB = lambda name, shape, dt: esB.enter_context(nc.sbuf_tensor("sc_" + name, shape, dt))
    acc = sbB("acc", [128, 8 * T], F32)
    gub_sb = sbB("gub_sb", [128, 256], F32)
    dnb_sb = sbB("dnb_sb", [16, D], F32)
    esel_sb = sbB("esel_sb", [16, NE * 128], F32)
    act = sbB("act", [128, 8 * T], BF16)
    wgu_sb = [sbB(f"wgu{i}", [128, 8 * 256], BF16) for i in range(3)]
    wd_sb = [sbB(f"wd{i}", [128, 8 * 128], BF16) for i in range(2)]
    sco_bc = sbB("sco_bc", [128, T], BF16)
    g1b = [sbB(f"g1b{i}", [128, 1024], F32) for i in range(2)]
    sigb = [sbB(f"sigb{i}", [128, 1024], F32) for i in range(2)]
    u2b = [sbB(f"u2b{i}", [128, 1024], F32) for i in range(2)]
    glub = [sbB("glub0", [128, 1024], F32)] * 2

    esB2 = ExitStack()
    ps_a = esB2.enter_context(nc.psum_tensor("ps_a", [128, 1024], F32))
    ps_b = esB2.enter_context(nc.psum_tensor("ps_b", [128, 1024], F32))
    ps_c = esB2.enter_context(nc.psum_tensor("ps_c", [128, 1024], F32))
    ps_d = esB2.enter_context(nc.psum_tensor("ps_d", [128, 1024], F32))
    PGT = [ps_a, ps_c]
    PUP = [ps_b, ps_d]
    PDN = [ps_b, ps_d]
    PSC = [ps_a, ps_c]

    s_r3 = sem("s_r3")
    s_wg = sem("s_wg")
    s_wd = sem("s_wd")
    p_bias = sem("p_bias")
    d_bias = sem("d_bias")
    p_sc3 = sem("p_sc3")
    d_sb3 = sem("d_sb3")
    p_gu = sem("p_gu")
    d_gu = sem("d_gu")
    d_g1 = sem("d_g1")
    a_sg3 = sem("a_sg3")
    d_act = sem("d_act")
    p_dn3 = sem("p_dn3")
    d_dn3 = sem("d_dn3")

    with nc.Block("P3") as b:
        @b.sync
        def _(sync):
            sync.dma_start(dnb_sb[:], dnb_h).then_inc(s_r3, 16)
            sync.dma_start(esel_sb[:], esel_h).then_inc(s_r3, 16)
            sync.dma_start(gub_sb[:], gub_h).then_inc(s_r3, 16)
            for c_ in range(8):
                sync.dma_start(
                    acc[:, c_ * T:(c_ + 1) * T].rearrange("p (gg t) -> p gg t", gg=4),
                    rscr.rearrange("p (gg c t) -> p gg c t", gg=4, c=8)[:, :, c_, :],
                ).then_inc(s_r3, 16)

        @b.gpsimd
        def _(g):
            def wgu_load(e, j):
                idx = e * 8 + j
                if idx >= 3:
                    g.wait_ge(p_gu, 4 * idx - 8)
                g.dma_start(wgu_sb[idx % 3][:].rearrange("p (c f) -> p c f", c=8),
                            wgu_h[e, j].rearrange("(c p) f -> p c f", p=128)).then_inc(s_wg, 16)
            wgu_load(0, 0)
            for e in range(NE):
                for j in range(1, 8):
                    wgu_load(e, j)
                if e + 1 < NE:
                    wgu_load(e + 1, 0)
                for hb in range(8):
                    idx = e * 8 + hb
                    if idx >= 2:
                        g.wait_ge(p_dn3, 4 * (idx - 1))
                    g.dma_start(wd_sb[idx % 2][:].rearrange("p (c f) -> p c f", c=8),
                                wd_h[e, hb].rearrange("(c p) f -> p c f", p=128)).then_inc(s_wd, 16)

        @b.tensor
        def _(t):
            t.wait_ge(s_r3, 16)
            for ds in range(32):
                hb, st = ds // 4, ds % 4
                if ds >= 2:
                    t.wait_ge(d_bias, ds - 1)
                t.matmul(PDN[ds % 2][:, 0:512], dnb_sb[:, hb * 128:(hb + 1) * 128],
                         scoresT[:, st * 512:(st + 1) * 512], start=True, stop=True).then_inc(p_bias, 1)
            def score_bc(e):
                for st in range(4):
                    ins = t.matmul(PSC[st // 2][:, (st % 2) * 512:(st % 2 + 1) * 512],
                                   esel_sb[:, e * 128:(e + 1) * 128],
                                   scoresT[:, st * 512:(st + 1) * 512], start=True, stop=True)
                ins.then_inc(p_sc3, 1)

            for e in range(NE):
                if e == 0:
                    t.wait_ge(s_r3, 32)
                    score_bc(0)
                    t.wait_ge(d_bias, 32)
                for u in range(16):
                    j, hf = u // 2, u % 2
                    U = 16 * e + u
                    t.wait_ge(s_wg, 16 * (e * 8 + j + 1))
                    if U >= 2:
                        t.wait_ge(d_gu, U - 1)
                    if u <= 1:
                        t.wait_ge(d_sb3, e + 1)
                    for fc in range(8):
                        for s2 in range(2):
                            t.matmul(PGT[U % 2][:, s2 * 512:(s2 + 1) * 512],
                                     wgu_sb[(e * 8 + j) % 3][:, fc * 256:fc * 256 + 128],
                                     xn_bf[:, fc * T + hf * 1024 + s2 * 512:fc * T + hf * 1024 + (s2 + 1) * 512],
                                     start=(fc == 0), stop=(fc == 7))
                    if u <= 1:
                        if e == 0:
                            t.wait_ge(d_bias, 32)
                        else:
                            t.wait_ge(d_dn3, 32 * e)
                    for fc in range(8):
                        for s2 in range(2):
                            ins = t.matmul(PUP[U % 2][:, s2 * 512:(s2 + 1) * 512],
                                           wgu_sb[(e * 8 + j) % 3][:, fc * 256 + 128:fc * 256 + 256],
                                           xn_bf[:, fc * T + hf * 1024 + s2 * 512:fc * T + hf * 1024 + (s2 + 1) * 512],
                                           start=(fc == 0), stop=(fc == 7))
                    ins.then_inc(p_gu, 2)
                t.wait_ge(d_gu, 16 * (e + 1))
                if e + 1 < NE:
                    score_bc(e + 1)
                t.wait_ge(d_act, 16 * (e + 1))
                for ds in range(32):
                    hb, st = ds // 4, ds % 4
                    t.wait_ge(s_wd, 16 * (e * 8 + hb + 1))
                    gds = 32 * e + ds
                    if gds >= 2:
                        t.wait_ge(d_dn3, gds - 1)
                    for fb in range(8):
                        ins = t.matmul(PDN[ds % 2][:, 0:512],
                                       wd_sb[(e * 8 + hb) % 2][:, fb * 128:(fb + 1) * 128],
                                       act[:, fb * T + st * 512:fb * T + (st + 1) * 512],
                                       start=(fb == 0), stop=(fb == 7))
                    ins.then_inc(p_dn3, 1)

        @b.vector
        def _(v):
            for ds in range(32):
                hb, st = ds // 4, ds % 4
                v.wait_ge(s_r3, 48 + 16 * (hb + 1))
                v.wait_ge(p_bias, ds + 1)
                v.tensor_add(acc[:, hb * T + st * 512:hb * T + (st + 1) * 512],
                             acc[:, hb * T + st * 512:hb * T + (st + 1) * 512],
                             PDN[ds % 2][:, 0:512]).then_inc(d_bias, 1)
            for e in range(NE):
                def frontend(u):
                    U = 16 * e + u
                    j, hf = u // 2, u % 2
                    v.wait_ge(p_gu, 2 * U + 2)
                    if U == 0:
                        v.wait_ge(s_r3, 48)
                    if U >= 1:
                        v.wait_ge(a_sg3, U - 1)
                    v.tensor_scalar(g1b[u % 2][:], PGT[U % 2][:],
                                    gub_sb[:, e * 16 + 2 * j:e * 16 + 2 * j + 1], 7.0,
                                    op0=ALU.add, op1=ALU.min)
                    v.tensor_scalar(u2b[u % 2][:], PUP[U % 2][:],
                                    gub_sb[:, e * 16 + 2 * j + 1:e * 16 + 2 * j + 2], -7.0,
                                    op0=ALU.add, op1=ALU.max).then_inc(d_g1, 1)
                    ins = v.tensor_scalar(u2b[u % 2][:], u2b[u % 2][:], 7.0, None, op0=ALU.min)
                    ins.then_inc(d_gu, 1)

                def backend(u):
                    U = 16 * e + u
                    j, hf = u // 2, u % 2
                    v.wait_ge(a_sg3, U + 1)
                    if u == 0:
                        v.wait_ge(p_dn3, 32 * e)
                        v.wait_ge(d_sb3, e + 1)
                    v.tensor_mul(glub[u % 2][:], g1b[u % 2][:], sigb[u % 2][:])
                    v.scalar_tensor_tensor(act[:, j * T + hf * 1024:j * T + (hf + 1) * 1024],
                                           u2b[u % 2][:], 1.0, glub[u % 2][:],
                                           op0=ALU.add, op1=ALU.mult)
                    v.tensor_mul(act[:, j * T + hf * 1024:j * T + (hf + 1) * 1024],
                                 act[:, j * T + hf * 1024:j * T + (hf + 1) * 1024],
                                 sco_bc[:, hf * 1024:(hf + 1) * 1024]).then_inc(d_act, 1)

                frontend(0)
                for u in range(1, 16):
                    frontend(u)
                    backend(u - 1)
                backend(15)
                for ds in range(32):
                    hb, st = ds // 4, ds % 4
                    v.wait_ge(p_dn3, 32 * e + ds + 1)
                    v.tensor_add(acc[:, hb * T + st * 512:hb * T + (st + 1) * 512],
                                 acc[:, hb * T + st * 512:hb * T + (st + 1) * 512],
                                 PDN[ds % 2][:, 0:512]).then_inc(d_dn3, 1)

        @b.scalar
        def _(s):
            for e in range(NE):
                s.wait_ge(p_sc3, e + 1)
                if e >= 1:
                    s.wait_ge(d_act, 16 * e)
                s.copy(sco_bc[:, 0:1024], PSC[0][:])
                s.copy(sco_bc[:, 1024:2048], PSC[1][:]).then_inc(d_sb3, 1)
                for u in range(16):
                    U = 16 * e + u
                    s.wait_ge(d_g1, U + 1)
                    if U >= 1:
                        s.wait_ge(d_act, U - 1)
                    s.activation(sigb[u % 2][:], g1b[u % 2][:], AF.Sigmoid,
                                 scale=1.702).then_inc(a_sg3, 1)

    esB2.close()
    ps_t = [esB.enter_context(nc.psum_tensor(f"ps_t{i}", [128, 1024], F32)) for i in range(4)]
    outb = [g1b[0], g1b[1], u2b[0], u2b[1]]
    p_mg = sem("p_mg")
    d_mg = sem("d_mg")
    s_out = sem("s_out")

    s_xp = sem("s_xp")
    xb = [sigb[0], sigb[1], glub[0]]

    with nc.Block("P4") as b:
        @b.tensor
        def _(t):
            for cc in range(16):
                if cc >= 4:
                    t.wait_ge(d_mg, cc - 3)
                for fc in range(8):
                    ins = t.transpose(ps_t[cc % 4][:, fc * 128:(fc + 1) * 128],
                                      acc[:, fc * T + cc * 128:fc * T + (cc + 1) * 128], iden[:])
                ins.then_inc(p_mg, 1)

        @b.vector
        def _(v):
            for cc in range(16):
                v.wait_ge(p_mg, cc + 1)
                v.wait_ge(s_xp, 16 * (cc + 1))
                if cc >= 4:
                    v.wait_ge(s_out, 16 * (cc - 3))
                v.tensor_add(outb[cc % 4][:], xb[cc % 3][:], ps_t[cc % 4][:]).then_inc(d_mg, 1)

        @b.sync
        def _(sync):
            for cc in range(18):
                if cc < 16:
                    if cc >= 3:
                        sync.wait_ge(d_mg, cc - 2)
                    sync.dma_start(xb[cc % 3][:], xtok_h[cc * 128:(cc + 1) * 128, :]).then_inc(s_xp, 16)
                if cc >= 2:
                    sync.wait_ge(d_mg, cc - 1)
                    sync.dma_start(y_h[(cc - 2) * 128:(cc - 1) * 128, :],
                                   outb[(cc - 2) % 4][:]).then_inc(s_out, 16)
            sync.wait_ge(s_out, 16 * 16)

    esB.close()
    es.close()
    return nc



_NC = None


def _get_nc():
    global _NC
    if _NC is None:
        _NC = build_nc()
    return _NC


def _host_consts():
    i = np.arange(128)
    c = {}
    c["iden"] = np.eye(128, dtype=np.float32)
    c["dclamp"] = np.maximum(i[None, :] - i[:, None], 0).astype(np.float32)
    c["mask01"] = (i[None, :] >= i[:, None]).astype(np.float32)
    c["tposb"] = np.broadcast_to((i + 1)[None, :], (128, 128)).astype(np.float32).copy()
    c["jrev"] = (127 - i)[:, None].astype(np.float32)
    m64 = np.zeros((128, 2), np.float32)
    m64[:64, 0] = 1.0
    m64[64:, 1] = 1.0
    c["m64"] = m64
    esel = np.zeros((16, NE * 128), np.float32)
    for e in range(NE):
        esel[e, e * 128:(e + 1) * 128] = 1.0
    c["esel"] = esel
    return c


def kernel(L0_attn_x, L0_attn_W_qkv, L0_attn_W_o, L0_attn_log_gamma,
           L0_ffn_router_w, L0_ffn_router_b, L0_ffn_gate_up_proj, L0_ffn_gate_up_bias,
           L0_ffn_down_proj, L0_ffn_down_bias):
    x = np.asarray(L0_attn_x, np.float32)
    Wqkv = np.asarray(L0_attn_W_qkv, np.float32)
    Wo = np.asarray(L0_attn_W_o, np.float32)
    lg = np.asarray(L0_attn_log_gamma, np.float32)
    rw = np.asarray(L0_ffn_router_w, np.float32)
    rb = np.asarray(L0_ffn_router_b, np.float32)
    gup = np.asarray(L0_ffn_gate_up_proj, np.float32)
    gub = np.asarray(L0_ffn_gate_up_bias, np.float32)
    wdp = np.asarray(L0_ffn_down_proj, np.float32)
    dnb = np.asarray(L0_ffn_down_bias, np.float32)

    consts = _host_consts()
    wq = np.ascontiguousarray(Wqkv[0:1024].T)
    wk = np.ascontiguousarray(Wqkv[1024:2048].T)
    wv = np.ascontiguousarray(Wqkv[2048:3072].T)
    wo = np.ascontiguousarray(Wo.T)

    gate_w = gup[:, :, 0::2]
    up_w = gup[:, :, 1::2]
    wgu = np.empty((NE, 8, D, 256), np.float32)
    for j in range(8):
        wgu[:, j, :, 0:128] = gate_w[:, :, j * 128:(j + 1) * 128]
        wgu[:, j, :, 128:256] = up_w[:, :, j * 128:(j + 1) * 128]
    wd = np.ascontiguousarray(wdp.reshape(NE, D, 8, 128).transpose(0, 2, 1, 3))

    gate_b = gub[:, 0::2]
    up_b = gub[:, 1::2]
    gub_cols = np.empty((128, 256), np.float32)
    for e in range(NE):
        for j in range(8):
            gub_cols[:, e * 16 + 2 * j] = gate_b[e, j * 128:(j + 1) * 128]
            gub_cols[:, e * 16 + 2 * j + 1] = up_b[e, j * 128:(j + 1) * 128]

    nc = _get_nc()
    in_maps = []
    for c in range(8):
        bb, hh = c // 2, c % 2
        sl = x[bb, 2048 * hh:2048 * (hh + 1)]
        xfc = np.ascontiguousarray(sl.T)
        if hh == 1:
            xpfc = np.ascontiguousarray(x[bb, 0:2048].T)
            flagc = np.ones((128, 1), np.float32)
        else:
            xpfc = xfc
            flagc = np.zeros((128, 1), np.float32)
        m = dict(xf=xfc, xpf=xpfc, xtok=np.ascontiguousarray(sl), wq=wq, wk=wk, wv=wv, wo=wo,
                 rw=rw, rb=rb.reshape(16, 1), wgu=wgu, wd=wd,
                 gub=gub_cols, dnb=dnb, esel=consts["esel"],
                 iden=consts["iden"], dclamp=consts["dclamp"], mask01=consts["mask01"],
                 tposb=consts["tposb"], jrev=consts["jrev"], m64=consts["m64"],
                 flag=flagc, lg=lg.reshape(1, 16))
        in_maps.append(m)

    res = run_bass_kernel_spmd(nc, in_maps, core_ids=list(range(8)))
    out = np.empty((4, 4096, 1024), np.float32)
    for c in range(8):
        bb, hh = c // 2, c % 2
        out[bb, 2048 * hh:2048 * (hh + 1)] = res.results[c]["y"]
    return out
